# version 16
# baseline (speedup 1.0000x reference)
import numpy as np
import ml_dtypes
import concourse.bass as bass
import concourse.mybir as mybir
from concourse.bass_utils import run_bass_kernel_spmd
from contextlib import ExitStack

F32 = mybir.dt.float32
BF16 = mybir.dt.bfloat16
I32 = mybir.dt.int32
AX = mybir.AxisListType.X
OP = mybir.AluOpType
AF = mybir.ActivationFunctionType

NCORE = 8
D = 2048
SEQ = 2048
BATCH = 4
NH = 16
NKV = 4
WIN = 512
TOK = 1024
EXT = 1536
NTT = 8
NET = 12
NFC = 16
I_DIM = 8192
NIC = 64
EPS_LN = 1e-5
QMAX = 127.0
QINV = float(np.float32(1.0) / np.float32(127.0))
QEPS = 1e-8
C8 = float(1.0 / np.sqrt(8.0))
ENGS = ("tensor", "vector", "scalar", "gpsimd", "sync")


def mask_slots(qt):
    return [0] + [m for m in (1, 2, 3) if m < 4 - qt] + [4]


MASK_INDEX = {}
_mi = 0
for _qt in range(8):
    for _s in mask_slots(_qt):
        MASK_INDEX[(_qt, _s)] = _mi
        _mi += 1
N_MASKS = _mi


class Planner:
    NCH = 16

    def __init__(self, nc, ctx):
        self.nc = nc
        self.q = {e: [] for e in ENGS}
        self.sem = {}
        self.cnt = {}
        for e in ENGS:
            self.sem[e] = ctx.enter_context(nc.semaphore("sem_" + e))
            self.cnt[e] = 0
            for ch in range(self.NCH):
                key = f"dma_{e}{ch}"
                self.sem[key] = ctx.enter_context(
                    nc.semaphore(f"dsem_{e}{ch}"))
                self.cnt[key] = 0
        self.dpos = {e: 0 for e in ENGS}
        self.water = {e: {} for e in ENGS}

    def _wait(self, eng, dep):
        semname, val = dep
        if val <= 0:
            return
        w = self.water[eng]
        if w.get(semname, 0) >= val:
            return
        w[semname] = val
        sem = self.sem[semname]
        self.q[eng].append(lambda E, s=sem, v=val: E.wait_ge(s, v))

    def emit(self, eng, fn, deps=(), inc=1):
        for d in deps:
            self._wait(eng, d)
        sem = self.sem[eng]
        self.q[eng].append(lambda E, f=fn, s=sem, i=inc: f(E).then_inc(s, i))
        self.cnt[eng] += inc
        return (eng, self.cnt[eng])

    def dma(self, eng, out, in_, deps=()):
        n = self.dpos[eng]
        self.dpos[eng] += 1
        key = f"dma_{eng}{n % self.NCH}"
        for d in deps:
            self._wait(eng, d)
        self._wait(eng, (key, self.cnt[key]))
        sem = self.sem[key]
        self.q[eng].append(lambda E, o=out, i=in_, s=sem:
                           E.dma_start(out=o, in_=i).then_inc(s, 16))
        self.cnt[key] += 16
        return (key, self.cnt[key])

    def collective(self, kind, op, ins, outs, deps=()):
        eng = "gpsimd"
        for d in deps:
            self._wait(eng, d)
        sem = self.sem[eng]
        self.q[eng].append(
            lambda E, k=kind, o=op, I=ins, O=outs, s=sem:
            E.collective_compute(k, o, replica_groups=[list(range(NCORE))],
                                 ins=I, outs=O).then_inc(s, 1))
        self.cnt[eng] += 1
        return (eng, self.cnt[eng])

    def wait_only(self, eng, deps):
        for d in deps:
            self._wait(eng, d)

    def bar(self, eng):
        return (eng, self.cnt[eng])

    def dma_bar(self, eng):
        return [(f"dma_{eng}{ch}", self.cnt[f"dma_{eng}{ch}"])
                for ch in range(self.NCH)]

    def run(self, block):
        def mk(name):
            def body(E):
                for fn in self.q[name]:
                    fn(E)
            return body
        for name in ENGS:
            getattr(block, name)(mk(name))


class _StopBuild(Exception):
    pass


def build_nc(stop=None):
    import os
    stop = stop or os.environ.get("KSTOP") or None
    nc = bass.Bass("TRN2", target_bir_lowering=False)

    h_x = nc.dram_tensor("x_ext", [EXT, D], F32, kind="ExternalInput").ap()
    h_wq = nc.dram_tensor("Wq", [D, D], F32, kind="ExternalInput").ap()
    h_wk = nc.dram_tensor("Wk", [D, 512], F32, kind="ExternalInput").ap()
    h_wv = nc.dram_tensor("Wv", [D, 512], F32, kind="ExternalInput").ap()
    h_wo = nc.dram_tensor("Wo", [D, D], F32, kind="ExternalInput").ap()
    h_lam = nc.dram_tensor("lam", [1, NH], F32, kind="ExternalInput").ap()
    h_w1s = nc.dram_tensor("w1s", [1024, D], F32, kind="ExternalInput").ap()
    h_b1 = nc.dram_tensor("b1", [1, I_DIM], F32, kind="ExternalInput").ap()
    h_w2s = nc.dram_tensor("w2s", [256, I_DIM], F32, kind="ExternalInput").ap()
    h_b2 = nc.dram_tensor("b2", [1, D], F32, kind="ExternalInput").ap()
    h_cos = nc.dram_tensor("cos_s", [128, EXT], F32, kind="ExternalInput").ap()
    h_sin = nc.dram_tensor("sin_s", [128, EXT], F32, kind="ExternalInput").ap()
    h_msk = nc.dram_tensor("masks", [N_MASKS, 128, 128], BF16,
                           kind="ExternalInput").ap()
    h_ibf = nc.dram_tensor("ident_bf", [128, 128], BF16,
                           kind="ExternalInput").ap()
    h_if32 = nc.dram_tensor("ident_f32", [128, 128], F32,
                            kind="ExternalInput").ap()
    h_p = nc.dram_tensor("rotP", [128, 128], BF16, kind="ExternalInput").ap()
    h_out = nc.dram_tensor("out", [TOK, D], F32, kind="ExternalOutput").ap()

    d_qw1sh = nc.dram_tensor("qw1sh", [NFC, 128, 1024], BF16)
    d_qw1 = nc.dram_tensor("qw1full", [NCORE, NFC, 128, 1024], BF16)
    d_qw2sh = nc.dram_tensor("qw2sh", [NIC // 2, 128, 512], BF16)
    d_qw2 = nc.dram_tensor("qw2full", [NCORE, NIC // 2, 128, 512], BF16)
    d_sw1sh = nc.dram_tensor("sw1sh", [1, 1024], F32)
    d_sw1 = nc.dram_tensor("sw1full", [NCORE, 1024], F32)
    d_sw2sh = nc.dram_tensor("sw2sh", [1, 256], F32)
    d_sw2 = nc.dram_tensor("sw2full", [NCORE, 256], F32)
    d_ar1i = nc.dram_tensor("ar1i", [1, 1], F32)
    d_ar1o = nc.dram_tensor("ar1o", [1, 1], F32)
    d_ar2i = nc.dram_tensor("ar2i", [1, 1], F32)
    d_ar2o = nc.dram_tensor("ar2o", [1, 1], F32)

    d_h = nc.dram_tensor("h_spill", [NIC, 128, TOK], F32)
    d_xn = nc.dram_tensor("xn_spill", [NTT, 128, D], F32)
    d_qx = nc.dram_tensor("qx_spill", [NFC, 128, TOK], BF16)
    d_wqbf = nc.dram_tensor("wqbf", [D, D], BF16)
    d_afm = nc.dram_tensor("afm_spill", [NH, 128, TOK], BF16)

    with ExitStack() as ctx:
      try:
        P = Planner(nc, ctx)
        import os as _os
        stack = []
        birth = {}
        tok_pool = []
        live_psum = {}

        def sb(name, shape, dt=F32):
            cm = nc.sbuf_tensor(name, shape, dt)
            h = cm.__enter__()
            stack.append([name, cm, False])
            birth[name] = list(tok_pool)
            if _os.environ.get("KBUILD_DEBUG"):
                print(f"[sb] {name} remaining={nc.sbuf_bytes_remaining}")
            return h

        def bd(name):
            return birth.get(name, [])

        import os as _os
        def _dump_stack(tag):
            if _os.environ.get("KBUILD_DEBUG"):
                alive = [(e[0], e[2]) for e in stack]
                print(f"[{tag}] stack:", alive)

        def free(*names, token=None):
            for n in names:
                if n in live_psum:
                    live_psum.pop(n).__exit__(None, None, None)
                    continue
                for ent in stack:
                    if ent[0] == n:
                        ent[2] = True
                        break
                else:
                    raise KeyError(n)
            if token is not None:
                tok_pool.append(token)
            while stack and stack[-1][2]:
                nm, cm, _ = stack.pop()
                cm.__exit__(None, None, None)
                if _os.environ.get("KBUILD_DEBUG"):
                    print(f"[pop] {nm} remaining={nc.sbuf_bytes_remaining}")

        def psum(name, shape, dt=F32):
            cm = nc.psum_tensor(name, shape, dt)
            h = cm.__enter__()
            live_psum[name] = cm
            return h

        t_ibf = sb("ibf", [128, 128], BF16)
        t_if32 = sb("if32", [128, 128], F32)
        t_p = sb("rotp", [128, 128], BF16)
        t_lam1 = sb("lam1", [128, NH], F32)
        t_lamn = sb("lamn", [128, NH], F32)
        t_wax = sb("w_ax", [128, 8], F32)
        t_wc = sb("w_c", [128, 8], F32)
        t_sw1 = sb("sw1loc", [128, 8], F32)
        t_w2ax = sb("w2ax", [128, 8], F32)
        t_w2axf = sb("w2axf", [128, 2], F32)
        t_w2c = sb("w2c", [128, 2], F32)
        t_sw2 = sb("sw2loc", [128, 2], F32)
        t_eps = sb("epsb", [128, 1], F32)
        t_ve = sb("lnve", [128, 1], F32)
        t_nw = sb("lnnw", [128, 1], F32)
        t_bs = sb("bnstat", [128, 24], F32)
        t_bo = sb("bnout", [128, 2], F32)
        t_rstd = sb("rstd", [128, 1], F32)
        t_ax = sb("axn", [128, NTT], F32)
        t_ax1 = sb("ax1", [128, 1], F32)
        t_axr = sb("axr", [1, 1], F32)
        t_amx = sb("amx", [128, 1], F32)
        t_cx = sb("cx", [128, 1], F32)
        t_sx1 = sb("sx1", [128, 1], F32)
        t_ch = sb("chq", [128, 1], F32)
        t_amh = sb("amh", [128, 1], F32)
        t_sx2 = sb("sx2", [128, 1], F32)
        t_st = sb("stats", [128, 2 * NH], F32)
        t_rr = sb("rstats", [128, 2 * NH], F32)
        t_mn = sb("mneg", [128, NH], F32)
        t_bneg = sb("bneg", [128, NH], F32)
        t_uu = sb("unorm", [128, NH], F32)
        t_ww = sb("wnorm", [128, NH], F32)
        t_ds = sb("dsum", [128, NH], F32)
        t_hax = sb("hax", [128, 128], F32)
        t_hx1 = sb("hx1", [128, 1], F32)
        t_hxr = sb("hxr", [1, 1], F32)
        t_swld = sb("swld", [64, 128], F32)
        t_s1c = sb("s1c", [128, 64], F32)
        t_b1c = sb("b1c", [128, 64], F32)
        t_s2c = sb("s2c", [128, 16], F32)
        t_b2c = sb("b2c", [128, 16], F32)
        stg = [sb(f"stg{i}", [128, 512], F32) for i in range(6)]
        bstg = [sb(f"bstg{i}", [128, D], F32) for i in range(2)]
        t_i32s = sb("i32stg", [128, 512], I32)
        t_fstg = [sb("fstgA", [128, 512], F32), sb("fstgB", [128, 512], F32)]
        t_kraw = sb("kraw", [128, 512], BF16)
        t_krot = sb("krot", [128, 512], BF16)
        t_u1 = sb("ropeu", [128, 512], F32)
        t_u2 = sb("ropev", [128, 512], F32)
        t_qstg = [sb("qstgA", [128, 512], BF16), sb("qstgB", [128, 512], BF16)]
        t_msk = sb("msk", [128, N_MASKS * 128], BF16)
        t_cos = sb("cosb", [128, EXT], F32)
        t_sin = sb("sinb", [128, EXT], F32)
        t_vtm = sb("vtm", [128, NET * 512], BF16)
        t_kfm = sb("kfm", [128, NKV * EXT], BF16)
        t_qrp = sb("qrp", [128, NH * TOK], BF16)
        t_xnfm = sb("xnfm", [128, NFC * EXT], BF16)

        dep_ibf = P.dma("sync", t_ibf[:], h_ibf[:, :])
        dep_if32 = P.dma("sync", t_if32[:], h_if32[:, :])
        dep_p = P.dma("sync", t_p[:], h_p[:, :])
        dep_msk = None
        for i in range(N_MASKS):
            dep_msk = P.dma("sync", t_msk[:, i * 128:(i + 1) * 128],
                            h_msk[i, :, :])
        dep_cos = P.dma("sync", t_cos[:], h_cos[:, :])
        dep_sin = P.dma("sync", t_sin[:], h_sin[:, :])
        dep_lam = P.dma("sync", t_lam1[:],
                        bass.AP(h_lam.tensor, 0, [[0, 128], [1, NH]]))
        consts_bar = P.dma_bar("sync")
        dep_msk = consts_bar[-1] if False else dep_msk
        dep_lamn = P.emit("vector", lambda E: E.tensor_single_scalar(
            out=t_lamn[:], in_=t_lam1[:], scalar=-1.0, op=OP.mult),
            deps=[dep_lam])
        dep_eps = P.emit("vector", lambda E: E.memset(t_eps[:], EPS_LN))

        def _stop_here(tokens):
            st = P.dma("gpsimd", h_out[0:128, 0:512], stg[0][:],
                       deps=list(tokens))
            P.wait_only("gpsimd", [st])
            raise _StopBuild()

        try:
            pass
        except Exception:
            pass
        t_wkbf = sb("wkbf", [128, NFC * 512], BF16)
        t_wvbf = sb("wvbf", [128, NFC * 512], BF16)
        wfree = [None, None]
        nxt = 0
        for fc in range(NFC):
            for (hsrc, dst) in ((h_wk, t_wkbf), (h_wv, t_wvbf)):
                bi = nxt % 2
                nxt += 1
                buf = stg[bi]
                dep = [wfree[bi]] if wfree[bi] else []
                ld = P.dma("scalar", buf[:], hsrc[fc * 128:(fc + 1) * 128, :],
                           deps=dep)
                ct = P.emit("scalar", lambda E, b=buf, d=dst, fc=fc: E.copy(
                    out=d[:, fc * 512:(fc + 1) * 512], in_=b[:]), deps=[ld])
                wfree[bi] = ct
        wkv_done = P.bar("scalar")
        stg_tok = [wfree[0], wfree[1], None, None, None, None]

        t_wqc = [sb("wqcA", [128, 512], BF16), sb("wqcB", [128, 512], BF16),
                 sb("wqcC", [128, 512], BF16), sb("wqcD", [128, 512], BF16)]

        p_wt = [psum("p_wt", [128, 512], BF16),
                psum("p_wt2", [128, 512], BF16)]
        qstg_tok = [[], []]
        bstg_tok = [[], []]
        t_i32b = sb("i32stgB", [128, 512], I32)
        qs = {"q": 0, "items": {}, "vv": {}, "r0": {}, "q8": {},
              "tp": [None, None], "ev": [None, None]}

        def _qbufs(q):
            qb = q % 2
            return (stg[2] if qb == 0 else t_fstg[0],
                    stg[3] if qb == 0 else t_fstg[1],
                    t_i32s if qb == 0 else t_i32b,
                    t_kraw if qb == 0 else t_krot,
                    p_wt[qb])

        def _vv(q):
            (buf, c_col, dst_fn, tile0, qtr) = qs["items"][q]
            vbuf = _qbufs(q)[0]
            qs["vv"][q] = P.emit("vector", lambda E, b=buf, qtr=qtr,
                                 c=c_col, vb=vbuf: E.tensor_scalar(
                out=vb[:], in0=b[:, qtr * 512:(qtr + 1) * 512],
                scalar1=c, scalar2=0.5, op0=OP.mult, op1=OP.add),
                deps=([qs["r0"][q - 2]] if q - 2 in qs["r0"] else []) +
                     ([stg_tok[2]] if stg_tok[2] and q < 2 else []))
            return qs["vv"][q]

        def _r0(q):
            vbuf, mbuf, ibuf, kbuf, pwt = _qbufs(q)
            qs["r0"][q] = P.emit("scalar", lambda E, ib=ibuf, vb=vbuf:
                                 E.activation(
                out=ib[:], in_=vb[:], func=AF.Copy),
                deps=[qs["vv"][q]] +
                     ([qs["q8"][q - 2]] if q - 2 in qs["q8"] else []))

        def _sb2(q):
            (buf, c_col, dst_fn, tile0, qtr) = qs["items"].pop(q)
            vbuf, mbuf, ibuf, kbuf, pwt = _qbufs(q)
            qb = q % 2
            mm_ = P.emit("vector", lambda E, mb=mbuf, vb=vbuf, ib=ibuf:
                         E.tensor_tensor(
                out=mb[:], in0=vb[:], in1=ib[:], op=OP.is_lt),
                deps=[qs["r0"][q]] +
                     ([stg_tok[3]] if stg_tok[3] and q < 2 else []))
            q8 = P.emit("vector", lambda E, kb=kbuf, ib=ibuf, mb=mbuf:
                        E.scalar_tensor_tensor(
                out=kb[:], in0=ib[:], scalar=0.0, in1=mb[:],
                op0=OP.add, op1=OP.subtract),
                deps=[mm_] + ([qs["tp"][qb]] if qs["tp"][qb] else []))
            qs["q8"][q] = q8
            tts = None
            for k in range(4):
                tts = P.emit("tensor", lambda E, k=k, kb=kbuf, pw=pwt:
                             E.transpose(
                    out=pw[:, k * 128:(k + 1) * 128],
                    in_=kb[:, k * 128:(k + 1) * 128],
                    identity=t_ibf[:]),
                    deps=[q8, dep_ibf] +
                         ([qs["ev"][qb]] if qs["ev"][qb] and k == 0
                          else []))
            qs["tp"][qb] = tts
            ev = P.emit("scalar", lambda E, pw=pwt, gb=qb: E.copy(
                out=t_qstg[gb][:], in_=pw[:, 0:512]),
                deps=[tts] + qstg_tok[qb])
            qs["ev"][qb] = ev
            qstg_tok[qb] = [P.dma("scalar", dfn,
                                  t_qstg[qb][:, c0:c1], deps=[ev])
                            for (dfn, c0, c1) in dst_fn(tile0)]

        def _qstep(item):
            q = qs["q"]
            if item is not None:
                qs["items"][q] = item
            if q - 2 in qs["items"]:
                _sb2(q - 2)
            last = None
            if item is not None:
                last = _vv(q)
            if q - 1 in qs["vv"]:
                _r0(q - 1)
            qs["q"] += 1
            return last

        def quant_slice(buf, c_col, dst_fn, qoff):
            last_vv = None
            for qtr in range(4):
                last_vv = _qstep((buf, c_col, dst_fn, qoff + qtr * 4, qtr))
            return last_vv

        def quant_flush():
            while qs["items"]:
                _qstep(None)

        wq_fpool = [t_u1, t_u2, stg[4], stg[5]]
        _wqst = {"ch": 0, "fq": [None] * 4, "cq": [None] * 4}

        def emit_wq_cast(n):
            for _ in range(n):
                ch = _wqst["ch"]
                if ch >= 64:
                    return
                _wqst["ch"] = ch + 1
                r, c4 = divmod(ch, 4)
                fi = ch % 4
                bo = ch % 4
                fbuf = wq_fpool[fi]
                ld = P.dma("gpsimd", fbuf[:],
                           h_wq[r * 128:(r + 1) * 128,
                                c4 * 512:(c4 + 1) * 512],
                           deps=[_wqst["fq"][fi]] if _wqst["fq"][fi] else [])
                ct = P.emit("vector", lambda E, fb=fbuf, bo=bo:
                            E.tensor_copy(
                    out=t_wqc[bo][:], in_=fb[:]),
                    deps=[ld] + ([_wqst["cq"][bo]] if _wqst["cq"][bo] else
                                 bd("wqcA") + bd("wqcB") + bd("wqcC") +
                                 bd("wqcD")))
                _wqst["fq"][fi] = ct
                st = P.dma("gpsimd",
                           d_wqbf[r * 128:(r + 1) * 128,
                                  c4 * 512:(c4 + 1) * 512],
                           t_wqc[bo][:], deps=[ct])
                _wqst["cq"][bo] = st

        for pt in range(8):
            bi = pt % 2
            buf = bstg[bi]
            dep = list(bstg_tok[bi])
            ld = P.dma("sync", buf[:], h_w1s[pt * 128:(pt + 1) * 128, :],
                       deps=dep)
            amax = P.emit("vector", lambda E, b=buf, pt=pt: E.tensor_reduce(
                out=t_wax[:, pt:pt + 1], in_=b[:], axis=AX, op=OP.max,
                apply_absolute_value=True), deps=[ld])
            rec = P.emit("vector", lambda E, pt=pt: E.reciprocal(
                out=t_wc[:, pt:pt + 1], in_=t_wax[:, pt:pt + 1]), deps=[amax])
            cc = P.emit("vector", lambda E, pt=pt: E.tensor_scalar(
                out=t_wc[:, pt:pt + 1], in0=t_wc[:, pt:pt + 1], scalar1=QMAX,
                scalar2=1e8, op0=OP.mult, op1=OP.min), deps=[rec])
            P.emit("vector", lambda E, pt=pt: E.tensor_scalar(
                out=t_sw1[:, pt:pt + 1], in0=t_wax[:, pt:pt + 1], scalar1=QINV,
                scalar2=QEPS, op0=OP.mult, op1=OP.max), deps=[amax])
            P.wait_only("vector", [cc])
            bstg_tok[bi] = [quant_slice(
                buf, t_wc[:, pt:pt + 1],
                lambda fc0, pt=pt: [(bass.AP(
                    d_qw1sh.ap().tensor, fc0 * 128 * 1024 + pt * 128,
                    [[1024, 128], [128 * 1024, 4], [1, 128]]), 0, 512)],
                0)]
            emit_wq_cast(5)
        quant_flush()
        sh1 = P.dma_bar("scalar")
        p_sws = psum("p_sws", [128, 128], F32)
        tr1 = P.emit("tensor", lambda E: E.transpose(
            out=p_sws[0:8, :], in_=t_sw1[:, 0:8], identity=t_if32[:]),
            deps=[P.bar("vector"), dep_if32])
        ev1 = P.emit("vector", lambda E: E.tensor_copy(
            out=t_swld[0:8, :], in_=p_sws[0:8, :]), deps=[tr1])
        sw1st = P.dma("sync",
                      bass.AP(d_sw1sh.ap().tensor, 0, [[128, 8], [1, 128]]),
                      t_swld[0:8, :], deps=[ev1])

        for pt in range(2):
            for cc2 in range(4):
                bi = cc2 % 2
                buf = bstg[bi]
                dep = list(bstg_tok[bi])
                ld = P.dma("sync", buf[:],
                           h_w2s[pt * 128:(pt + 1) * 128,
                                 cc2 * 2048:(cc2 + 1) * 2048], deps=dep)
                am = P.emit("vector", lambda E, b=buf, pt=pt, cc2=cc2:
                            E.tensor_reduce(
                    out=t_w2ax[:, pt * 4 + cc2:pt * 4 + cc2 + 1],
                    in_=b[:], axis=AX, op=OP.max,
                    apply_absolute_value=True), deps=[ld])
                bstg_tok[bi] = [am]
                emit_wq_cast(2)
            amf = P.emit("vector", lambda E, pt=pt: E.tensor_reduce(
                out=t_w2axf[:, pt:pt + 1], in_=t_w2ax[:, pt * 4:pt * 4 + 4],
                axis=AX, op=OP.max), deps=bstg_tok[0] + bstg_tok[1])
            rec = P.emit("vector", lambda E, pt=pt: E.reciprocal(
                out=t_w2c[:, pt:pt + 1], in_=t_w2axf[:, pt:pt + 1]),
                deps=[amf])
            P.emit("vector", lambda E, pt=pt: E.tensor_scalar(
                out=t_w2c[:, pt:pt + 1], in0=t_w2c[:, pt:pt + 1], scalar1=QMAX,
                scalar2=1e8, op0=OP.mult, op1=OP.min), deps=[rec])
            P.emit("vector", lambda E, pt=pt: E.tensor_scalar(
                out=t_sw2[:, pt:pt + 1], in0=t_w2axf[:, pt:pt + 1],
                scalar1=QINV, scalar2=QEPS, op0=OP.mult, op1=OP.max),
                deps=[amf])
        c2done = P.bar("vector")
        tr2 = P.emit("tensor", lambda E: E.transpose(
            out=p_sws[0:2, :], in_=t_sw2[:, 0:2], identity=t_if32[:]),
            deps=[c2done, dep_if32, ev1])
        ev2 = P.emit("vector", lambda E: E.tensor_copy(
            out=t_swld[32:34, :], in_=p_sws[0:2, :]), deps=[tr2, ev1])
        sw2st = P.dma("sync",
                      bass.AP(d_sw2sh.ap().tensor, 0, [[128, 2], [1, 128]]),
                      t_swld[32:34, :], deps=[ev2])
        for cc2 in range(4):
            for pt in range(2):
                bi = pt % 2
                buf = bstg[bi]
                dep = list(bstg_tok[bi])
                ld = P.dma("sync", buf[:],
                           h_w2s[pt * 128:(pt + 1) * 128,
                                 cc2 * 2048:(cc2 + 1) * 2048],
                           deps=dep + [c2done])
                P.wait_only("vector", [ld])
                bstg_tok[bi] = [quant_slice(
                    buf, t_w2c[:, pt:pt + 1],
                    lambda icl0, pt=pt: [
                        (bass.AP(d_qw2sh.ap().tensor,
                                 (icl0 // 2 + hi) * 128 * 512 + pt * 128,
                                 [[512, 128], [256, 2], [1, 128]]),
                         hi * 256, hi * 256 + 256) for hi in range(2)],
                    cc2 * 16)]
                emit_wq_cast(1)
        sh2_done = None
        quant_flush()
        free("i32stgB", token=qs["q8"][63])
        sh2_bar = P.dma_bar("scalar")
        if stop == "W":
            _stop_here(sh2_bar)
        ag_sw1 = P.collective("AllGather", OP.bypass, [d_sw1sh.ap().opt()],
                              [d_sw1.ap().opt()], deps=[sw1st])
        ag_sw2 = P.collective("AllGather", OP.bypass, [d_sw2sh.ap().opt()],
                              [d_sw2.ap().opt()], deps=[sw2st])

        emit_wq_cast(64)
        wq_sts_bar = P.dma_bar("gpsimd")
        free("wqcD", token=_wqst["cq"][3])
        free("wqcC", token=_wqst["cq"][2])
        free("wqcB", token=_wqst["cq"][1])
        free("wqcA", token=_wqst["cq"][0])

        p_tr = psum("p_tr", [128, 512], F32)

        prev_ln = None
        prev_sq = None
        prev_trx = None
        axd = []
        xnsp = {}
        for tt in range(NET):
            bi = tt % 2
            buf = bstg[bi]
            dep = list(bstg_tok[bi])
            ld = P.dma("sync", buf[:], h_x[tt * 128:(tt + 1) * 128, :],
                       deps=dep)
            bns = None
            for chk in range(4):
                bns = P.emit("vector", lambda E, b=buf, chk=chk: E.bn_stats(
                    out=t_bs[:, chk * 6:(chk + 1) * 6],
                    in_=b[:, chk * 512:(chk + 1) * 512]),
                    deps=[ld] + ([prev_ln] if prev_ln and chk == 0 else []))
            ag = P.emit("vector", lambda E: E.bn_aggr(
                out=t_bo[:], in_=t_bs[:]),
                deps=[bns] + ([prev_sq] if prev_sq else []))
            sq = P.emit("scalar", lambda E: E.activation(
                out=t_rstd[:], in_=t_bo[:, 1:2], func=AF.Sqrt,
                bias=t_eps[:]), deps=[ag, dep_eps])
            prev_sq = sq
            rc0 = P.emit("vector", lambda E: E.reciprocal(
                out=t_rstd[:], in_=t_rstd[:]), deps=[sq])
            ve = P.emit("vector", lambda E: E.tensor_scalar(
                out=t_ve[:], in0=t_bo[:, 1:2], scalar1=t_eps[:], scalar2=0.0,
                op0=OP.add, op1=OP.add), deps=[ag, dep_eps])
            n1 = P.emit("vector", lambda E: E.tensor_tensor(
                out=t_nw[:], in0=t_rstd[:], in1=t_rstd[:], op=OP.mult),
                deps=[rc0])
            n2 = P.emit("vector", lambda E: E.tensor_tensor(
                out=t_nw[:], in0=t_nw[:], in1=t_ve[:], op=OP.mult),
                deps=[n1, ve])
            n3 = P.emit("vector", lambda E: E.tensor_scalar(
                out=t_nw[:], in0=t_nw[:], scalar1=-0.5, scalar2=1.5,
                op0=OP.mult, op1=OP.add), deps=[n2])
            rc = P.emit("vector", lambda E: E.tensor_tensor(
                out=t_rstd[:], in0=t_rstd[:], in1=t_nw[:], op=OP.mult),
                deps=[n3])
            own = tt >= 4
            xnt = P.emit("vector", lambda E, b=buf: E.tensor_scalar(
                out=b[:], in0=b[:], scalar1=t_bo[:, 0:1],
                scalar2=t_rstd[:], op0=OP.subtract, op1=OP.mult),
                deps=[rc])
            prev_ln = xnt
            if own:
                axd.append(P.emit("vector", lambda E, b=buf, tt=tt:
                                  E.tensor_reduce(
                    out=t_ax[:, tt - 4:tt - 3], in_=b[:], axis=AX, op=OP.max,
                    apply_absolute_value=True), deps=[xnt]))
                xnsp[tt - 4] = P.dma("sync", d_xn[tt - 4, :, :], buf[:],
                                     deps=[axd[-1]])
            for g in range(4):
                tts = []
                for k in range(4):
                    fc = g * 4 + k
                    tts.append(P.emit("tensor", lambda E, b=buf, fc=fc, k=k:
                                      E.transpose(
                        out=p_tr[:, k * 128:(k + 1) * 128],
                        in_=b[:, fc * 128:(fc + 1) * 128],
                        identity=t_if32[:]),
                        deps=[xnt, dep_if32] +
                             ([prev_trx] if prev_trx else [])))

                def ev_xn(E, g=g, tt=tt):
                    return E.copy(
                        out=bass.AP(t_xnfm, g * 4 * EXT + tt * 128,
                                    [[NFC * EXT, 128], [EXT, 4], [1, 128]]),
                        in_=p_tr[:, 0:512])
                prev_trx = P.emit("scalar", ev_xn, deps=[tts[-1]])
            bstg_tok[bi] = [prev_trx] + ([xnsp[tt - 4]] if own else [])
        xn_fm_done = prev_trx
        p_axt = psum("p_axt", [1, 128], F32)
        a1 = P.emit("vector", lambda E: E.tensor_reduce(
            out=t_ax1[:], in_=t_ax[:], axis=AX, op=OP.max), deps=axd)
        a2 = P.emit("tensor", lambda E: E.transpose(
            out=p_axt[:], in_=t_ax1[:], identity=t_if32[:]),
            deps=[a1, dep_if32])
        a3 = P.emit("vector", lambda E: E.tensor_reduce(
            out=t_axr[:], in_=p_axt[0:1, :], axis=AX, op=OP.max), deps=[a2])
        a4 = P.dma("sync", d_ar1i[:, :], t_axr[0:1, 0:1], deps=[a3])
        ar1 = P.collective("AllReduce", OP.max, [d_ar1i.ap().opt()],
                           [d_ar1o.ap().opt()], deps=[a4])
        free("p_axt", "p_tr", "p_sws", "p_wt2", "p_wt")
        if stop == "L":
            _stop_here([ar1])

        p_v = [psum("p_v0", [128, 512], F32), psum("p_v1", [128, 512], F32)]
        vev = [None, None]
        for tt in range(NET):
            bi = tt % 2
            mmq = None
            for fc in range(NFC):
                mmq = P.emit("tensor", lambda E, tt=tt, fc=fc, bi=bi:
                             E.matmul(
                    out=p_v[bi][:],
                    lhsT=t_xnfm[:, fc * EXT + tt * 128:
                                fc * EXT + (tt + 1) * 128],
                    rhs=t_wvbf[:, fc * 512:(fc + 1) * 512],
                    start=(fc == 0), stop=(fc == NFC - 1)),
                    deps=(([xn_fm_done, wkv_done] +
                           ([vev[bi]] if vev[bi] else []))
                          if fc == 0 else []))
            vev[bi] = P.emit("vector", lambda E, tt=tt, bi=bi: E.tensor_copy(
                out=t_vtm[:, tt * 512:(tt + 1) * 512], in_=p_v[bi][:]),
                deps=[mmq] + (bd("vtm") if tt < 2 else []))
        v_done_v = P.bar("vector")
        free("p_v1", "p_v0")
        if stop == "KV1":
            _stop_here([v_done_v])

        p_r = [psum("p_r0", [128, 512], F32), psum("p_r1", [128, 512], F32)]
        p_k = [psum("p_k0", [128, 512], F32), psum("p_k1", [128, 512], F32)]

        t_kraw2 = t_krot
        t_u1b = stg[2]
        t_u2b = stg[3]
        t_krA = t_qstg[0]
        t_krB = t_qstg[1]
        qstg_guard = qstg_tok[0] + qstg_tok[1]
        rope_st = {"rn": 0, "u1": [None, None], "u2": [None, None],
                   "cbk": {}, "crr": {}}

        def rope_bank_tok(pp):
            t = rope_st["cbk"].get(id(pp))
            return [t] if t else []

        def rope_block(dst_ap, ppk, ppr, cos_c0, kdep, bdeps=(),
                       rot_deps=()):
            rc = rope_st["rn"]
            rope_st["rn"] = rc + 1
            rbi = rc % 2
            kb = t_kraw if rbi == 0 else t_kraw2
            rb = t_krA if rbi == 0 else t_krB
            ub1 = t_u1 if rbi == 0 else t_u1b
            ub2 = t_u2 if rbi == 0 else t_u2b
            pu1 = rope_st["u1"][rbi]
            pu2 = rope_st["u2"][rbi]
            cb = P.emit("scalar", lambda E, kb=kb: E.copy(
                out=kb[:], in_=ppk[:]),
                deps=[kdep] + ([pu1] if pu1 else bd("kraw") + bd("krot")))
            rope_st["cbk"][id(ppk)] = cb
            prev_cr = rope_st["crr"].get(id(ppr))
            rot = P.emit("tensor", lambda E, kb=kb: E.matmul(
                out=ppr[:], lhsT=t_p[:], rhs=kb[:], start=True,
                stop=True), deps=[cb, dep_p] +
                ([prev_cr] if prev_cr else []) + list(rot_deps))
            cr = P.emit("scalar", lambda E, rb=rb: E.copy(
                out=rb[:], in_=ppr[:]),
                deps=[rot] + ([pu2] if pu2 else qstg_guard))
            rope_st["crr"][id(ppr)] = cr
            u1 = P.emit("vector", lambda E, kb=kb, ub=ub1:
                        E.tensor_tensor(
                out=ub[:], in0=kb[:],
                in1=t_cos[:, cos_c0:cos_c0 + 512], op=OP.mult),
                deps=[cb, dep_cos])
            rope_st["u1"][rbi] = u1
            u2 = P.emit("vector", lambda E, rb=rb, ub=ub2:
                        E.tensor_tensor(
                out=ub[:], in0=rb[:],
                in1=t_sin[:, cos_c0:cos_c0 + 512], op=OP.mult),
                deps=[cr, dep_sin])
            rope_st["u2"][rbi] = u2
            return P.emit("vector", lambda E, ub1=ub1, ub2=ub2:
                          E.tensor_tensor(
                out=dst_ap, in0=ub1[:], in1=ub2[:], op=OP.add),
                deps=[u1, u2] + list(bdeps))

        rope_done = None
        for kv in range(NKV):
            for tc in range(3):
                bi = tc % 2
                mmq = None
                for fc in range(NFC):
                    mmq = P.emit("tensor",
                                 lambda E, kv=kv, tc=tc, fc=fc, bi=bi:
                                 E.matmul(
                        out=p_k[bi][:],
                        lhsT=t_wkbf[:, fc * 512 + kv * 128:
                                    fc * 512 + (kv + 1) * 128],
                        rhs=t_xnfm[:, fc * EXT + tc * 512:
                                   fc * EXT + (tc + 1) * 512],
                        start=(fc == 0), stop=(fc == NFC - 1)),
                        deps=(([xn_fm_done, wkv_done] +
                               rope_bank_tok(p_k[bi]))
                              if fc == 0 else []))
                if stop == "KV3" and kv == 0 and tc == 0:
                    _stop_here([mmq])
                rope_done = rope_block(
                    t_kfm[:, kv * EXT + tc * 512:kv * EXT + (tc + 1) * 512],
                    p_k[bi], p_r[bi], tc * 512, mmq,
                    bd("kfm") if kv == 0 and tc == 0 else [],
                    rot_deps=[v_done_v])
                if stop == "KV2" and kv == 0 and tc == 0:
                    _stop_here([rope_done])
        k_done = rope_done
        if stop == "KV":
            _stop_here([k_done])
        free("p_k1", "p_k0")
        free("wvbf", "wkbf", token=k_done)

        t_wqg = [sb("wqgA", [128, 16 * 512], BF16),
                 sb("wqgB", [128, 16 * 512], BF16)]
        p_q = [psum("p_q0", [128, 512], F32), psum("p_q1", [128, 512], F32)]
        rope_st["cbk"][id(p_q[0])] = k_done
        rope_st["cbk"][id(p_q[1])] = k_done
        qg_free = [None, None]
        qrope_done = k_done
        q_head = {}
        for hg in range(4):
            gb = hg % 2
            gdep = ([qg_free[gb]] if qg_free[gb] else
                    bd("wqgA") + bd("wqgB")) + wq_sts_bar
            gld = P.dma("sync" if hg % 2 == 0 else "scalar",
                        bass.AP(t_wqg[gb], 0,
                                [[16 * 512, 128], [512, 16], [1, 512]]),
                        bass.AP(d_wqbf.ap().tensor, hg * 512,
                                [[D, 128], [128 * D, 16], [1, 512]]),
                        deps=gdep)
            last_mm = None
            for hh in range(hg * 4, hg * 4 + 4):
                for th in range(2):
                    pbi = th % 2
                    mmq = None
                    for fc in range(NFC):
                        mmq = P.emit(
                            "tensor",
                            lambda E, fc=fc, th=th, hh=hh, gb=gb, pbi=pbi:
                            E.matmul(
                                out=p_q[pbi][:],
                                lhsT=t_wqg[gb][:, fc * 512 +
                                               (hh % 4) * 128:
                                               fc * 512 +
                                               (hh % 4) * 128 + 128],
                                rhs=t_xnfm[:, fc * EXT + 512 + th * 512:
                                           fc * EXT + 512 + (th + 1) * 512],
                                start=(fc == 0), stop=(fc == NFC - 1)),
                            deps=(([gld, xn_fm_done] +
                                   rope_bank_tok(p_q[pbi]))
                                  if fc == 0 else []))
                    last_mm = mmq
                    qrope_done = rope_block(
                        t_qrp[:, hh * TOK + th * 512:
                              hh * TOK + (th + 1) * 512],
                        p_q[pbi], p_r[pbi], 512 + th * 512, mmq,
                        bd("qrp") if hh == 0 and th == 0 else [])
                q_head[hh] = qrope_done
            qg_free[gb] = last_mm
        q_done = qrope_done
        ag_qw1 = P.collective("AllGather", OP.bypass, [d_qw1sh.ap().opt()],
                              [d_qw1.ap().opt()], deps=sh1 + [q_done])
        ag_qw2 = P.collective("AllGather", OP.bypass, [d_qw2sh.ap().opt()],
                              [d_qw2.ap().opt()], deps=sh2_bar + [q_done])
        if stop == "Q":
            _stop_here([q_done])
        free("p_q1", "p_q0", "p_r1", "p_r0")
        free("wqgB", "wqgA")
        free("cosb", "sinb")
        free("xnfm", token=q_done)

        _dump_stack("pre-A")
        t_worA = sb("worA", [128, NH * TOK], BF16)
        t_e = [sb("e0", [128, 1280], F32), sb("e1", [128, 1280], F32)]
        t_t = [sb(f"t_sb{k}", [128, 640], F32) for k in range(4)]
        t_dn = [sb("dn0", [128, 640], BF16), sb("dn1", [128, 640], BF16)]
        t_dT = [sb("dT0", [128, 640], BF16), sb("dT1", [128, 640], BF16)]
        p_sc = [psum("p_sc0", [128, 1536], F32),
                psum("p_sc1", [128, 1536], F32)]
        p_dt = psum("p_dt", [128, 640], BF16)
        p_av = psum("p_av", [128, 128], F32)

        def region_plan(qt):
            plan = []
            for half in range(2):
                base = 640 * half
                segs = [(0, 512), (512, 128)] if half == 0 else \
                       [(0, 384), (384, 256)]
                for (c0, w) in segs:
                    plan.append((base + c0, w, "k", c0, half))
                for s in mask_slots(qt):
                    plan.append((base + s * 128, 128, "m",
                                 MASK_INDEX[(qt, s)], half))
            return plan

        exp_done = {}
        tok_done = {}
        evt_done = {}
        aev_done = {}
        mx_done = {}
        dr_done = {}
        av_done = {}
        scores_mm = {}

        def em_scores(idx):
            qt, hh = idx // NH, idx % NH
            sbi = idx % 2
            psc = p_sc[sbi]
            banks = {}
            for item in region_plan(qt):
                banks.setdefault((item[0] * 4) // 2048, []).append(item)
            mmq = None
            emitted = 0
            for b0 in sorted(banks):
                items = banks[b0]
                for i2, (pc0, w, kind, arg, half) in enumerate(items):
                    st_ = (i2 == 0)
                    sp_ = (i2 == len(items) - 1)
                    deps = []
                    if emitted == 0:
                        deps = [q_head[hh] if idx >= 2 else q_done,
                                k_done] + consts_bar
                        if idx - 2 in exp_done:
                            deps.append(exp_done[idx - 2])
                    if kind == "k":
                        def mk_s(E, qt=qt, hh=hh, pc0=pc0, w=w,
                                 arg=arg, half=half, st_=st_, sp_=sp_,
                                 psc=psc, kv=hh // 4):
                            pp = 64 * half
                            return E.matmul(
                                out=psc[:, pc0:pc0 + w],
                                lhsT=t_qrp[pp:pp + 64,
                                           hh * TOK + qt * 128:
                                           hh * TOK + (qt + 1) * 128],
                                rhs=t_kfm[pp:pp + 64,
                                          kv * EXT + qt * 128 + arg:
                                          kv * EXT + qt * 128 + arg + w],
                                start=st_, stop=sp_)
                        mmq = P.emit("tensor", mk_s, deps=deps)
                    else:
                        def mk_m(E, pc0=pc0, arg=arg, st_=st_, sp_=sp_,
                                 psc=psc):
                            return E.matmul(
                                out=psc[:, pc0:pc0 + 128],
                                lhsT=t_ibf[:],
                                rhs=t_msk[:, arg * 128:(arg + 1) * 128],
                                start=st_, stop=sp_)
                        mmq = P.emit("tensor", mk_m, deps=deps)
                    emitted += 1
            scores_mm[idx] = mmq

        def em_mx(idx):
            hh = idx % NH
            psc = p_sc[idx % 2]
            mx_done[idx] = P.emit("vector", lambda E, psc=psc, hh=hh:
                                  E.tensor_reduce(
                out=t_mn[:, hh:hh + 1], in_=psc[:, 0:1280], axis=AX,
                op=OP.max, negate=True), deps=[scores_mm[idx]])

        def em_exp(idx):
            hh = idx % NH
            psc = p_sc[idx % 2]
            ebuf = t_e[idx % 2]
            ed = [tok_done[idx - 2]] if idx - 2 in tok_done else \
                bd("e0") + bd("e1")
            e1 = P.emit("scalar", lambda E, psc=psc, hh=hh, eb=ebuf:
                        E.activation(
                out=eb[:, 0:640], in_=psc[:, 0:640], func=AF.Exp,
                bias=t_mn[:, hh:hh + 1], scale=1.0,
                accum_out=t_st[:, 2 * hh:2 * hh + 1]),
                deps=[mx_done[idx]] + ed)
            exp_done[idx] = P.emit("scalar", lambda E, psc=psc, hh=hh,
                                   eb=ebuf: E.activation(
                out=eb[:, 640:1280], in_=psc[:, 640:1280], func=AF.Exp,
                bias=t_mn[:, hh:hh + 1], scale=1.0,
                accum_out=t_st[:, 2 * hh + 1:2 * hh + 2]), deps=[e1])

        def em_smalls(idx):
            hh = idx % NH
            ebuf = t_e[idx % 2]
            e2 = exp_done[idx]
            rr = P.emit("vector", lambda E, hh=hh: E.reciprocal(
                out=t_rr[:, 2 * hh:2 * hh + 2],
                in_=t_st[:, 2 * hh:2 * hh + 2]), deps=[e2])
            bn = P.emit("vector", lambda E, hh=hh: E.scalar_tensor_tensor(
                out=t_bneg[:, hh:hh + 1], in0=t_st[:, 2 * hh:2 * hh + 1],
                scalar=t_lamn[:, hh:hh + 1],
                in1=t_rr[:, 2 * hh + 1:2 * hh + 2],
                op0=OP.mult, op1=OP.mult), deps=[rr, dep_lamn])
            td = [dr_done[idx - 4]] if idx - 4 in dr_done else \
                (bd("t_sb0") + bd("t_sb1") + bd("t_sb2") + bd("t_sb3"))
            tok_done[idx] = P.emit("vector", lambda E, hh=hh, eb=ebuf,
                                   ti=idx % 4: E.scalar_tensor_tensor(
                out=t_t[ti][:], in0=eb[:, 640:1280],
                scalar=t_bneg[:, hh:hh + 1], in1=eb[:, 0:640],
                op0=OP.mult, op1=OP.add), deps=[bn] + td)

        def em_tail1(idx):
            hh = idx % NH
            sbi = idx % 2
            dbuf = t_dn[sbi]
            dd = [aev_done[idx - 2]] if idx - 2 in aev_done else \
                bd("dn0") + bd("dn1")
            dr = P.emit("vector", lambda E, hh=hh, db=dbuf, ti=idx % 4:
                        E.tensor_scalar(
                out=db[:], in0=t_t[ti][:], scalar1=0.0, scalar2=0.0,
                op0=OP.max, op1=OP.add,
                accum_out=t_ds[:, hh:hh + 1]),
                deps=[tok_done[idx]] + dd)
            dr_done[idx] = dr
            uu = P.emit("vector", lambda E, hh=hh: E.scalar_tensor_tensor(
                out=t_uu[:, hh:hh + 1], in0=t_st[:, 2 * hh:2 * hh + 1],
                scalar=1e-6, in1=t_ds[:, hh:hh + 1],
                op0=OP.mult, op1=OP.add), deps=[dr])
            wwt = P.emit("vector", lambda E, hh=hh: E.reciprocal(
                out=t_ww[:, hh:hh + 1], in_=t_uu[:, hh:hh + 1]),
                deps=[uu])
            dn = P.emit("scalar", lambda E, hh=hh, db=dbuf:
                        E.activation(
                out=db[:], in_=db[:], func=AF.Copy,
                scale=t_ww[:, hh:hh + 1]), deps=[wwt])
            prev_evt = evt_done.get(idx - 1)
            tts = None
            for s5 in range(5):
                tts = P.emit("tensor", lambda E, s5=s5, db=dbuf:
                             E.transpose(
                    out=p_dt[:, s5 * 128:(s5 + 1) * 128],
                    in_=db[:, s5 * 128:(s5 + 1) * 128],
                    identity=t_ibf[:]),
                    deps=[dn] + ([prev_evt] if prev_evt and s5 == 0
                                 else []))
            return tts

        def em_tail2(idx, tts):
            qt, hh = idx // NH, idx % NH
            kv = hh // 4
            dti = idx % 2
            evt = P.emit("vector", lambda E, dti=dti: E.tensor_copy(
                out=t_dT[dti][:], in_=p_dt[:]),
                deps=[tts] +
                ([aev_done[idx - 2]] if idx - 2 in aev_done else
                 bd("dT0") + bd("dT1")))
            evt_done[idx] = evt
            prev_aev = aev_done.get(idx - 1)
            mav = None
            for s5 in range(5):
                mav = P.emit("tensor",
                             lambda E, s5=s5, qt=qt, kv=kv, dti=dti:
                             E.matmul(
                    out=p_av[:],
                    lhsT=t_vtm[:, (qt + s5) * 512 + kv * 128:
                               (qt + s5) * 512 + (kv + 1) * 128],
                    rhs=t_dT[dti][:, s5 * 128:(s5 + 1) * 128],
                    start=(s5 == 0), stop=(s5 == 4)),
                    deps=(([evt, v_done_v] +
                           ([prev_aev] if prev_aev else []))
                          if s5 == 0 else []))
            av_done[idx] = mav

        def em_aev(idx):
            qt, hh = idx // NH, idx % NH
            aev_done[idx] = P.emit("vector", lambda E, hh=hh, qt=qt:
                                   E.tensor_copy(
                out=t_worA[:, hh * TOK + qt * 128:
                           hh * TOK + qt * 128 + 128],
                in_=p_av[:]),
                deps=[av_done[idx]] + (bd("worA") if idx == 0 else []))

        NIDX = NTT * NH
        for i in range(-2, NIDX + 5):
            if 0 <= i < NIDX:
                em_exp(i)
            if 0 <= i + 2 < NIDX:
                em_scores(i + 2)
            if 0 <= i - 4 < NIDX:
                em_aev(i - 4)
            if 0 <= i < NIDX:
                em_smalls(i)
            tts = em_tail1(i - 3) if 0 <= i - 3 < NIDX else None
            if 0 <= i + 2 < NIDX:
                em_mx(i + 2)
            if 0 <= i - 3 < NIDX:
                em_tail2(i - 3, tts)
        afm_done = aev_done[NTT * NH - 1]
        wor_sp = []
        for j in range(NH):
            wor_sp.append(P.dma("gpsimd", d_afm[j, :, :],
                                t_worA[:, j * TOK:(j + 1) * TOK],
                                deps=[aev_done[(NTT - 1) * NH + j]]))
        afm_bar = P.dma_bar("gpsimd")
        if stop == "A":
            _stop_here(afm_bar)
        free("p_av", "p_dt", "p_sc1", "p_sc0")
        free("dT1", "dT0", "dn1", "dn0",
             "t_sb3", "t_sb2", "t_sb1", "t_sb0", "e1", "e0",
             "qrp", "kfm", "vtm", token=afm_done)
        free("msk", token=afm_done)
        free("worA")
        tok_pool.extend(afm_bar)

        bq = P.dma("sync", t_amx[:],
                   bass.AP(d_ar1o.ap().tensor, 0, [[0, 128], [1, 1]]),
                   deps=[ar1])
        r1_ = P.emit("vector", lambda E: E.reciprocal(
            out=t_cx[:], in_=t_amx[:]), deps=[bq])
        cx = P.emit("vector", lambda E: E.tensor_scalar(
            out=t_cx[:], in0=t_cx[:], scalar1=QMAX, scalar2=1e8,
            op0=OP.mult, op1=OP.min), deps=[r1_])
        sx1 = P.emit("vector", lambda E: E.tensor_scalar(
            out=t_sx1[:], in0=t_amx[:], scalar1=QINV, scalar2=QEPS,
            op0=OP.mult, op1=OP.max), deps=[bq])

        t_out = sb("outfm", [128, NFC * TOK], F32)
        t_qxr = sb("qxr", [128, NFC * TOK], BF16)
        t_qv = [sb("q_vA", [128, D], F32), sb("q_vB", [128, D], F32)]
        t_qr0s = sb("q_r0A", [128, D], I32)
        t_qr0 = [t_qr0s, t_qr0s]
        t_qm = sb("q_m", [128, D], BF16)
        t_qb = sb("q_qb", [128, D], BF16)
        p_tq = [psum("p_tq", [128, 512], BF16),
                psum("p_tq2", [128, 512], BF16)]
        vv_tok = [None, None]
        r0_tok = [None, None]
        qq_tok = [None, None]
        ev_tok = [None, None]
        last_evq = None

        def qx_stage2(tt):
            qb2 = tt % 2
            r0 = P.emit("scalar", lambda E, qb2=qb2: E.activation(
                out=t_qr0[qb2][:], in_=t_qv[qb2][:], func=AF.Copy),
                deps=[vv_tok[qb2]] + bd("q_r0A") +
                ([qq_tok[1 - qb2]] if qq_tok[1 - qb2] else []))
            r0_tok[qb2] = r0
            mm_ = P.emit("vector", lambda E, qb2=qb2: E.tensor_tensor(
                out=t_qm[:], in0=t_qv[qb2][:], in1=t_qr0[qb2][:],
                op=OP.is_lt), deps=[r0] + bd("q_m"))
            qq = P.emit("vector", lambda E, qb2=qb2:
                        E.scalar_tensor_tensor(
                out=t_qb[:], in0=t_qr0[qb2][:], scalar=0.0, in1=t_qm[:],
                op0=OP.add, op1=OP.subtract),
                deps=[mm_] + bd("q_qb") +
                ([ev_tok[1]] if ev_tok[1] else []))
            qq_tok[qb2] = qq
            for g in range(4):
                pb = g % 2
                tts = None
                for k in range(4):
                    fc = g * 4 + k
                    tts = P.emit("tensor", lambda E, fc=fc, k=k, pb=pb:
                                 E.transpose(
                        out=p_tq[pb][:, k * 128:(k + 1) * 128],
                        in_=t_qb[:, fc * 128:(fc + 1) * 128],
                        identity=t_ibf[:]),
                        deps=[qq] +
                        ([ev_tok[pb]] if ev_tok[pb] and k == 0 else []) +
                        ([afm_done] if tt == 0 and g == 0 and k == 0
                         else []))
                ev_tok[pb] = P.emit("scalar", lambda E, g=g, tt=tt, pb=pb:
                                    E.copy(
                    out=bass.AP(t_qxr, g * 4 * TOK + tt * 128,
                                [[NFC * TOK, 128], [TOK, 4], [1, 128]]),
                    in_=p_tq[pb][:, 0:512]),
                    deps=[tts] + (bd("qxr") if tt == 0 and g == 0 else []))
            return ev_tok[1]

        for tt in range(NTT):
            bi = tt % 2
            qb2 = tt % 2
            buf = bstg[bi]
            dep = list(bstg_tok[bi])
            ld = P.dma("sync", buf[:], d_xn[tt, :, :],
                       deps=dep + [xnsp[tt]])
            vv = P.emit("vector", lambda E, b=buf, qb2=qb2:
                        E.tensor_scalar(
                out=t_qv[qb2][:], in0=b[:], scalar1=t_cx[:], scalar2=0.5,
                op0=OP.mult, op1=OP.add),
                deps=[cx, ld] + bd("q_vA") + bd("q_vB") +
                ([qq_tok[qb2]] if qq_tok[qb2] else []))
            vv_tok[qb2] = vv
            bstg_tok[bi] = [vv]
            if tt >= 1:
                last_evq = qx_stage2(tt - 1)
        last_evq = qx_stage2(NTT - 1)
        qx_done = last_evq
        if stop == "QX":
            _stop_here([qx_done])
        free("p_tq2", "p_tq")
        free("q_qb", "q_m", "q_r0A", "q_vB", "q_vA",
             token=qx_done)
        qx_done_all = [qx_done]


        def load_colT(dst, src_ap, dep_extra, n, p_dst):
            ld = P.dma("sync", t_swld[0:n, :], src_ap, deps=dep_extra)
            tp = P.emit("tensor", lambda E, n=n, p_dst=p_dst: E.transpose(
                out=p_dst[:, 0:n], in_=t_swld[0:n, :],
                identity=t_if32[0:n, 0:n]), deps=[ld, dep_if32])
            return P.emit("vector", lambda E, d=dst, n=n, p_dst=p_dst:
                          E.tensor_copy(
                out=d[:, 0:n], in_=p_dst[0:128, 0:n]), deps=[tp])

        qxr_bar = qx_done_all

        _dump_stack("pre-qw1")
        t_qw1 = [sb("qw1A", [128, NFC * 512], BF16),
                 sb("qw1B", [128, NFC * 512], BF16)]
        p_sw = psum("p_sw", [128, 128], F32)
        p_h = [psum("p_h0", [128, 512], F32), psum("p_h1", [128, 512], F32)]

        cp1 = load_colT(t_s1c,
                        bass.AP(d_sw1.ap().tensor, 0, [[128, 64], [1, 128]]),
                        [ag_sw1, afm_done] + qx_done_all, 64, p_sw)
        s1fin = P.emit("vector", lambda E: E.tensor_scalar(
            out=t_s1c[:], in0=t_s1c[:], scalar1=t_sx1[:], scalar2=0.0,
            op0=OP.mult, op1=OP.add), deps=[cp1, sx1])
        cpb1 = load_colT(t_b1c,
                         bass.AP(h_b1.tensor, 0, [[128, 64], [1, 128]]),
                         [cp1], 64, p_sw)

        qw1_free = [None, None]
        hst_free = [None] * 4
        ph_free = [None, None]
        hsp_last = None
        for jb2 in range(16):
            jb, hb = jb2 // 2, jb2 % 2
            bi = jb2 % 2
            dep = ([qw1_free[bi]] if qw1_free[bi] else
                   (bd("qw1A") + bd("qw1B"))) + [ag_qw1] + qx_done_all
            for fc in range(NFC):
                P.dma("sync",
                      t_qw1[bi][:, fc * 512:(fc + 1) * 512],
                      d_qw1[jb, fc, :, hb * 512:(hb + 1) * 512],
                      deps=dep if fc == 0 else [])
            qw1_bar = P.dma_bar("sync")
            last_mm = None
            for jc in range(4):
                jg = jb2 * 4 + jc
                for th in range(2):
                    pbi = (jc * 2 + th) % 2
                    mmq = None
                    for fc in range(NFC):
                        mmq = P.emit(
                            "tensor",
                            lambda E, fc=fc, jc=jc, th=th, bi=bi, pbi=pbi:
                            E.matmul(
                                out=p_h[pbi][:],
                                lhsT=t_qw1[bi][:, fc * 512 + jc * 128:
                                               fc * 512 + (jc + 1) * 128],
                                rhs=t_qxr[:, fc * TOK + th * 512:
                                          fc * TOK + (th + 1) * 512],
                                start=(fc == 0), stop=(fc == NFC - 1)),
                            deps=((qw1_bar + qxr_bar + [afm_done] +
                                   ([ph_free[pbi]] if ph_free[pbi] else []))
                                  if fc == 0 else []))
                    si = (jg * 2 + th) % 4
                    ge = P.emit("scalar", lambda E, jg=jg, si=si, pbi=pbi:
                                E.activation(
                        out=stg[si][:], in_=p_h[pbi][:], func=AF.Gelu,
                        scale=t_s1c[:, jg:jg + 1], bias=t_b1c[:, jg:jg + 1]),
                        deps=[mmq, s1fin, cpb1] +
                             ([hst_free[si]] if hst_free[si] else
                              ([stg_tok[si]] if stg_tok[si] else [])))
                    ph_free[pbi] = ge
                    ax = P.emit("vector", lambda E, jg=jg, th=th, si=si:
                                E.tensor_reduce(
                        out=t_hax[:, jg * 2 + th:jg * 2 + th + 1],
                        in_=stg[si][:], axis=AX, op=OP.max,
                        apply_absolute_value=True), deps=[ge])
                    stt = P.dma("scalar", d_h[jg, :, th * 512:(th + 1) * 512],
                                stg[si][:], deps=[ax])
                    hst_free[si] = stt
                    hsp_last = stt
                    last_mm = mmq
            qw1_free[bi] = last_mm
        stg_tok = [hst_free[0], hst_free[1], hst_free[2], hst_free[3],
                   None, None]
        hsp_bar = P.dma_bar("scalar")
        p_hxt = psum("p_hxt", [1, 128], F32)
        if stop == "M1":
            _stop_here(hsp_bar)
        hx1 = P.emit("vector", lambda E: E.tensor_reduce(
            out=t_hx1[:], in_=t_hax[:], axis=AX, op=OP.max),
            deps=[P.bar("vector")])
        hx2 = P.emit("tensor", lambda E: E.transpose(
            out=p_hxt[:], in_=t_hx1[:], identity=t_if32[:]), deps=[hx1])
        hx3 = P.emit("vector", lambda E: E.tensor_reduce(
            out=t_hxr[:], in_=p_hxt[0:1, :], axis=AX, op=OP.max), deps=[hx2])
        hx4 = P.dma("sync", d_ar2i[:, :], t_hxr[0:1, 0:1], deps=[hx3])
        ar2 = P.collective("AllReduce", OP.max, [d_ar2i.ap().opt()],
                           [d_ar2o.ap().opt()], deps=[hx4])
        free("p_hxt", "p_h1", "p_h0", "p_sw")
        free("qw1B", "qw1A", "qxr", token=ar2)

        t_wor = sb("wor", [128, NFC * 512], BF16)
        t_wos = sb("wos", [128, NFC * 1024], BF16)
        p_o = psum("p_o", [128, 8 * 512], F32)
        po_ev = {}
        wo_og_mm = {}
        for og in range(2):
            wcasts = []
            for dc in range(NFC):
                bi = dc % 2
                dep = list(bstg_tok[bi])
                if og == 1:
                    dep = dep + [wo_og_mm[(0, dc)]]
                ldw = P.dma("sync", bstg[bi][:, 0:1024],
                            h_wo[dc * 128:(dc + 1) * 128,
                                 og * 1024:(og + 1) * 1024],
                            deps=dep + (bd("wos") if og == 0 and dc < 2
                                        else []))
                wcasts.append(P.emit("scalar", lambda E, bi=bi, dc=dc:
                                     E.copy(
                    out=t_wos[:, dc * 1024:(dc + 1) * 1024],
                    in_=bstg[bi][:, 0:1024]), deps=[ldw]))
                bstg_tok[bi] = [wcasts[-1]]
            for tq in range(2):
                wor_ld = None
                wdep = afm_bar + bd("wor") + \
                    ([po_ev[(og, tq - 1, 7)]] if (og, tq - 1, 7) in po_ev
                     else []) + \
                    ([po_ev[(og - 1, 1, 7)]] if (og - 1, 1, 7) in po_ev and
                     tq == 0 else [])
                for dc in range(NFC):
                    wor_ld = P.dma("gpsimd",
                                   t_wor[:, dc * 512:(dc + 1) * 512],
                                   d_afm[dc, :, tq * 512:(tq + 1) * 512],
                                   deps=wdep if dc == 0 else [])
                wor_bar = P.dma_bar("gpsimd")
                mm_last = None
                for dc in range(NFC):
                    for oc8 in range(8):
                        extra = []
                        if oc8 == 0 and tq == 0:
                            extra = [wcasts[dc]]
                        if dc == 0 and oc8 == 0:
                            extra = extra + list(wor_bar)
                            pk = (og, tq - 1, 7) if tq == 1 else \
                                (og - 1, 1, 7)
                            if pk in po_ev:
                                extra.append(po_ev[pk])
                        mm_last = P.emit(
                            "tensor",
                            lambda E, dc=dc, oc8=oc8: E.matmul(
                                out=p_o[:, oc8 * 512:(oc8 + 1) * 512],
                                lhsT=t_wos[:, dc * 1024 + oc8 * 128:
                                           dc * 1024 + (oc8 + 1) * 128],
                                rhs=t_wor[:, dc * 512:(dc + 1) * 512],
                                start=(dc == 0), stop=(dc == NFC - 1)),
                            deps=extra)
                    wo_og_mm[(og, dc)] = mm_last
                for oc8 in range(8):
                    oc = og * 8 + oc8
                    po_ev[(og, tq, oc8)] = P.emit(
                        "scalar", lambda E, oc=oc, tq=tq, oc8=oc8:
                        E.copy(
                            out=t_out[:, oc * TOK + tq * 512:
                                      oc * TOK + (tq + 1) * 512],
                            in_=p_o[:, oc8 * 512:(oc8 + 1) * 512]),
                        deps=[mm_last] + bd("outfm"))
        wo_done_v = P.bar("scalar")
        if stop == "WO":
            _stop_here([wo_done_v])
        free("p_o")
        free("wos", "wor", token=wo_done_v)

        bq2 = P.dma("sync", t_amh[:],
                    bass.AP(d_ar2o.ap().tensor, 0, [[0, 128], [1, 1]]),
                    deps=[ar2])
        r2_ = P.emit("vector", lambda E: E.reciprocal(
            out=t_ch[:], in_=t_amh[:]), deps=[bq2])
        ch = P.emit("vector", lambda E: E.tensor_scalar(
            out=t_ch[:], in0=t_ch[:], scalar1=QMAX, scalar2=1e8,
            op0=OP.mult, op1=OP.min), deps=[r2_])
        sx2 = P.emit("vector", lambda E: E.tensor_scalar(
            out=t_sx2[:], in0=t_amh[:], scalar1=QINV, scalar2=QEPS,
            op0=OP.mult, op1=OP.max), deps=[bq2])

        p_sw2 = psum("p_sw2", [128, 128], F32)
        cp2 = load_colT(t_s2c,
                        bass.AP(d_sw2.ap().tensor, 0, [[128, 16], [1, 128]]),
                        [ag_sw2, cpb1, wo_done_v], 16, p_sw2)
        s2fin = P.emit("vector", lambda E: E.tensor_scalar(
            out=t_s2c[:, 0:16], in0=t_s2c[:, 0:16], scalar1=t_sx2[:],
            scalar2=0.0, op0=OP.mult, op1=OP.add), deps=[cp2, sx2])
        cpb2 = load_colT(t_b2c,
                         bass.AP(h_b2.tensor, 0, [[128, 16], [1, 128]]),
                         [cp2], 16, p_sw2)
        free("p_sw2")

        t_qh = sb("qh", [128, NIC * 512], BF16)
        t_qw2 = [sb("qw2A", [128, 2048], BF16), sb("qw2B", [128, 2048], BF16)]
        p_m2 = psum("p_m2", [128, 4096], F32)

        hre_free = [None] * 4
        qw2_free = [None, None]
        fst_free = [None, None]
        prev_hq = None
        qh_tok = {}
        ev_m2 = {}
        mm_last_og = {}
        mm_pair = {}
        sc_m2 = {}
        pend_epi = [None]

        def emit_epi(tb2, og, mm_last):
            for oc8 in range(8):
                oc = og * 8 + oc8
                fbi = oc8 % 2
                sc_ = P.emit("scalar", lambda E, oc8=oc8, oc=oc, fbi=fbi:
                             E.activation(
                    out=t_fstg[fbi][:],
                    in_=p_m2[:, oc8 * 512:(oc8 + 1) * 512],
                    func=AF.Copy, scale=t_s2c[:, oc:oc + 1]),
                    deps=[mm_last, s2fin, cpb2] +
                         ([fst_free[fbi]] if fst_free[fbi] else []))
                ev_m2[(tb2, og, oc8)] = P.emit(
                    "vector", lambda E, oc=oc, tb2=tb2, fbi=fbi:
                    E.scalar_tensor_tensor(
                        out=t_out[:, oc * TOK + tb2 * 512:
                                  oc * TOK + (tb2 + 1) * 512],
                        in0=t_fstg[fbi][:],
                        scalar=t_b2c[:, oc:oc + 1],
                        in1=t_out[:, oc * TOK + tb2 * 512:
                                  oc * TOK + (tb2 + 1) * 512],
                        op0=OP.add, op1=OP.add),
                    deps=[sc_, wo_done_v])
                fst_free[fbi] = ev_m2[(tb2, og, oc8)]
                sc_m2[(tb2, og, oc8)] = sc_

        for tb2 in range(2):
            for ic in range(NIC):
                si = ic % 4
                dep = [hre_free[si]] if hre_free[si] else \
                    ([stg_tok[si]] if stg_tok[si] else [])
                ld = P.dma("gpsimd", stg[si][:],
                           d_h[ic, :, tb2 * 512:(tb2 + 1) * 512],
                           deps=dep + (hsp_bar if ic == 0 and tb2 == 0
                                       else []))
                vv = P.emit("vector", lambda E, si=si: E.tensor_scalar(
                    out=stg[4][:], in0=stg[si][:], scalar1=t_ch[:],
                    scalar2=0.5, op0=OP.mult, op1=OP.add),
                    deps=[ld, ch] + ([prev_hq] if prev_hq else []) +
                    ([stg_tok[4]] if stg_tok[4] and ic == 0 and tb2 == 0
                     else []))
                hre_free[si] = vv
                r0 = P.emit("scalar", lambda E: E.activation(
                    out=t_i32s[:], in_=stg[4][:], func=AF.Copy), deps=[vv])
                mm_ = P.emit("vector", lambda E: E.tensor_tensor(
                    out=stg[5][:], in0=stg[4][:], in1=t_i32s[:],
                    op=OP.is_lt),
                    deps=[r0] + ([stg_tok[5]] if stg_tok[5] and ic == 0 and
                                 tb2 == 0 else []))
                qh_tok[(tb2, ic)] = P.emit("vector", lambda E, ic=ic:
                                           E.scalar_tensor_tensor(
                    out=t_qh[:, ic * 512:(ic + 1) * 512], in0=t_i32s[:],
                    scalar=0.0, in1=stg[5][:], op0=OP.add, op1=OP.subtract),
                    deps=[mm_] + (bd("qh") if ic == 0 and tb2 == 0 else [])
                    + ([mm_pair[(0, 1, ic // 2)]] if tb2 == 1 else []))
                prev_hq = qh_tok[(tb2, ic)]
            if pend_epi[0]:
                emit_epi(*pend_epi[0])
                pend_epi[0] = None
            for og in range(2):
                mm_last = None
                for ic in range(0, NIC, 2):
                    bi = (ic // 2) % 2
                    dep = ([qw2_free[bi]] if qw2_free[bi] else
                           (bd("qw2A") + bd("qw2B"))) + [ag_qw2]
                    lds = []
                    qq_ = "sync" if og == 0 else "scalar"
                    for cpart in range(4):
                        c = og * 4 + cpart
                        lds.append(P.dma(qq_,
                                         bass.AP(t_qw2[bi],
                                                 cpart * 512,
                                                 [[2048, 128], [1, 512]]),
                                         d_qw2[c, ic // 2, :, :],
                                         deps=dep if cpart == 0 else []))
                    for i2 in range(2):
                        ici = ic + i2
                        for oc8 in range(8):
                            oc = og * 8 + oc8
                            extra = []
                            if ici == 0 and i2 == 0 and oc8 == 0:
                                if og == 1:
                                    extra = [sc_m2[(tb2, 0, 7)]]
                                elif tb2 == 1:
                                    extra = [sc_m2[(0, 1, 7)]]
                            mm_last = P.emit(
                                "tensor",
                                lambda E, i2=i2, ici=ici, oc8=oc8, bi=bi:
                                E.matmul(
                                    out=p_m2[:, oc8 * 512:(oc8 + 1) * 512],
                                    lhsT=t_qw2[bi][
                                        :, (oc8 // 2) * 512 + i2 * 256 +
                                        (oc8 % 2) * 128:
                                        (oc8 // 2) * 512 + i2 * 256 +
                                        (oc8 % 2) * 128 + 128],
                                    rhs=t_qh[:, ici * 512:(ici + 1) * 512],
                                    start=(ici == 0), stop=(ici == NIC - 1)),
                                deps=(lds + [qh_tok[(tb2, ici)]] + extra)
                                if oc8 == 0 else [])
                    qw2_free[bi] = mm_last
                    mm_pair[(tb2, og, ic // 2)] = mm_last
                mm_last_og[(tb2, og)] = mm_last
                if og == 0:
                    emit_epi(tb2, 0, mm_last)
                else:
                    pend_epi[0] = (tb2, 1, mm_last)
        if pend_epi[0]:
            emit_epi(*pend_epi[0])
            pend_epi[0] = None
        m2_done = ev_m2[(1, 1, 7)]
        if stop == "M2":
            _stop_here([m2_done])
        stg_tok = [hre_free[0], hre_free[1], hre_free[2], hre_free[3],
                   prev_hq, prev_hq]
        free("p_m2")
        free("qw2B", "qw2A", "qh", token=m2_done)

        p_f = [psum("p_f0", [128, 512], F32), psum("p_f1", [128, 512], F32)]
        xre_free = [None, None]
        ost_free = [None, None]
        pf_free = [None, None]
        st_last = None
        for tt in range(NTT):
            for fg in range(4):
                bi = (tt * 4 + fg) % 2
                dep = [xre_free[bi]] if xre_free[bi] else \
                    ([stg_tok[bi]] if stg_tok[bi] else [])
                ldx = P.dma("gpsimd", stg[bi][:],
                            h_x[(4 + tt) * 128:(5 + tt) * 128,
                                fg * 512:(fg + 1) * 512], deps=dep)
                tts = []
                for k in range(4):
                    fc = fg * 4 + k
                    tts.append(P.emit("tensor",
                                      lambda E, fc=fc, k=k, tt=tt, bi=bi:
                                      E.transpose(
                        out=p_f[bi][:, k * 128:(k + 1) * 128],
                        in_=t_out[:, fc * TOK + tt * 128:
                                  fc * TOK + (tt + 1) * 128],
                        identity=t_if32[:]),
                        deps=[m2_done] +
                             ([pf_free[bi]] if pf_free[bi] and k == 0
                              else [])))
                ost = stg[2 + bi]
                odep = [ost_free[bi]] if ost_free[bi] else \
                    ([stg_tok[2 + bi]] if stg_tok[2 + bi] else [])
                ad = P.emit("vector", lambda E, bi=bi, ost=ost:
                            E.tensor_tensor(
                    out=ost[:], in0=p_f[bi][:], in1=stg[bi][:],
                    op=OP.add), deps=[tts[-1], ldx] + odep)
                pf_free[bi] = ad
                xre_free[bi] = ad
                st = P.dma("sync" if bi == 0 else "scalar",
                           h_out[tt * 128:(tt + 1) * 128,
                                 fg * 512:(fg + 1) * 512],
                           ost[:], deps=[ad])
                ost_free[bi] = st
                st_last = st
        P.wait_only("gpsimd", P.dma_bar("gpsimd"))
        P.wait_only("sync", P.dma_bar("sync"))
        P.wait_only("scalar", P.dma_bar("scalar"))
        free("p_f1", "p_f0")

      except _StopBuild:
        pass
      block = ctx.enter_context(nc.Block())
      P.run(block)
    return nc


def _host_consts(core):
    half = core % 2
    qstart = half * TOK
    pos = np.arange(qstart - WIN, qstart + TOK, dtype=np.float64)
    pos = np.clip(pos, 0, None)
    i = np.arange(64, dtype=np.float64)
    inv = 1.0 / (10000.0 ** (2 * i / 128.0))
    ang = pos[None, :] * inv[:, None]
    cos = np.cos(ang)
    sin = np.sin(ang)
    cos_s = np.concatenate([cos, cos], 0) * C8
    sin_s = np.concatenate([-sin, sin], 0) * C8
    masks = np.zeros((N_MASKS, 128, 128), np.float32)
    r = np.arange(128)
    c = np.arange(128)
    for (qt, s), mi in MASK_INDEX.items():
        rg = qstart + qt * 128 + r[:, None]
        cg = qstart - WIN + (qt + s) * 128 + c[None, :]
        allowed = (cg >= 0) & (cg <= rg) & (cg >= rg - (WIN - 1))
        masks[mi] = np.where(allowed, 0.0, -30000.0)
    return (cos_s.astype(np.float32), sin_s.astype(np.float32),
            masks.astype(ml_dtypes.bfloat16))


def kernel(L0_attn_x, L0_attn_Wq, L0_attn_Wk, L0_attn_Wv, L0_attn_Wo,
           L0_attn_lam, L0_ffn_w1, L0_ffn_b1, L0_ffn_w2, L0_ffn_b2):
    x = np.ascontiguousarray(np.asarray(L0_attn_x, dtype=np.float32))
    Wq = np.ascontiguousarray(np.asarray(L0_attn_Wq, dtype=np.float32))
    Wk = np.ascontiguousarray(np.asarray(L0_attn_Wk, dtype=np.float32))
    Wv = np.ascontiguousarray(np.asarray(L0_attn_Wv, dtype=np.float32))
    Wo = np.ascontiguousarray(np.asarray(L0_attn_Wo, dtype=np.float32))
    lam = np.ascontiguousarray(np.asarray(L0_attn_lam, dtype=np.float32)
                               .reshape(1, NH))
    w1 = np.ascontiguousarray(np.asarray(L0_ffn_w1, dtype=np.float32))
    b1 = np.ascontiguousarray(np.asarray(L0_ffn_b1, dtype=np.float32)
                              .reshape(1, I_DIM))
    w2 = np.ascontiguousarray(np.asarray(L0_ffn_w2, dtype=np.float32))
    b2 = np.ascontiguousarray(np.asarray(L0_ffn_b2, dtype=np.float32)
                              .reshape(1, D))
    ident_bf = np.eye(128, dtype=ml_dtypes.bfloat16)
    ident_f32 = np.eye(128, dtype=np.float32)
    rotP = np.zeros((128, 128), np.float32)
    for d in range(128):
        rotP[(d + 64) % 128, d] = 1.0
    rotP = rotP.astype(ml_dtypes.bfloat16)

    nc = build_nc()
    in_maps = []
    for core in range(NCORE):
        b, half = core // 2, core % 2
        qstart = half * TOK
        x_ext = np.zeros((EXT, D), np.float32)
        lo = qstart - WIN
        src_lo = max(lo, 0)
        x_ext[src_lo - lo:, :] = x[b, src_lo:qstart + TOK, :]
        cos_s, sin_s, masks = _host_consts(core)
        in_maps.append({
            "x_ext": x_ext, "Wq": Wq, "Wk": Wk, "Wv": Wv, "Wo": Wo,
            "lam": lam,
            "w1s": w1[core * 1024:(core + 1) * 1024, :],
            "b1": b1,
            "w2s": w2[core * 256:(core + 1) * 256, :],
            "b2": b2,
            "cos_s": cos_s, "sin_s": sin_s, "masks": masks,
            "ident_bf": ident_bf, "ident_f32": ident_f32, "rotP": rotP,
        })
    res = run_bass_kernel_spmd(nc, in_maps, core_ids=list(range(NCORE)))
    out = np.empty((BATCH, SEQ, D), np.float32)
    for core in range(NCORE):
        b, half = core // 2, core % 2
        out[b, half * TOK:(half + 1) * TOK, :] = res.results[core]["out"]
    return out


# version 17
# speedup vs baseline: 1.0295x; 1.0295x over previous
import numpy as np
import ml_dtypes
import concourse.bass as bass
import concourse.mybir as mybir
from concourse.bass_utils import run_bass_kernel_spmd
from contextlib import ExitStack

F32 = mybir.dt.float32
BF16 = mybir.dt.bfloat16
I32 = mybir.dt.int32
AX = mybir.AxisListType.X
OP = mybir.AluOpType
AF = mybir.ActivationFunctionType

NCORE = 8
D = 2048
SEQ = 2048
BATCH = 4
NH = 16
NKV = 4
WIN = 512
TOK = 1024
EXT = 1536
NTT = 8
NET = 12
NFC = 16
I_DIM = 8192
NIC = 64
EPS_LN = 1e-5
QMAX = 127.0
QINV = float(np.float32(1.0) / np.float32(127.0))
QEPS = 1e-8
C8 = float(1.0 / np.sqrt(8.0))
ENGS = ("tensor", "vector", "scalar", "gpsimd", "sync")


def mask_slots(qt):
    return [0] + [m for m in (1, 2, 3) if m < 4 - qt] + [4]


MASK_INDEX = {}
_mi = 0
for _qt in range(8):
    for _s in mask_slots(_qt):
        MASK_INDEX[(_qt, _s)] = _mi
        _mi += 1
N_MASKS = _mi


class Planner:
    NCH = 16

    def __init__(self, nc, ctx):
        self.nc = nc
        self.q = {e: [] for e in ENGS}
        self.sem = {}
        self.cnt = {}
        for e in ENGS:
            self.sem[e] = ctx.enter_context(nc.semaphore("sem_" + e))
            self.cnt[e] = 0
            for ch in range(self.NCH):
                key = f"dma_{e}{ch}"
                self.sem[key] = ctx.enter_context(
                    nc.semaphore(f"dsem_{e}{ch}"))
                self.cnt[key] = 0
        self.dpos = {e: 0 for e in ENGS}
        self.water = {e: {} for e in ENGS}

    def _wait(self, eng, dep):
        semname, val = dep
        if val <= 0:
            return
        w = self.water[eng]
        if w.get(semname, 0) >= val:
            return
        w[semname] = val
        sem = self.sem[semname]
        self.q[eng].append(lambda E, s=sem, v=val: E.wait_ge(s, v))

    def emit(self, eng, fn, deps=(), inc=1):
        for d in deps:
            self._wait(eng, d)
        sem = self.sem[eng]
        self.q[eng].append(lambda E, f=fn, s=sem, i=inc: f(E).then_inc(s, i))
        self.cnt[eng] += inc
        return (eng, self.cnt[eng])

    def dma(self, eng, out, in_, deps=()):
        n = self.dpos[eng]
        self.dpos[eng] += 1
        key = f"dma_{eng}{n % self.NCH}"
        for d in deps:
            self._wait(eng, d)
        self._wait(eng, (key, self.cnt[key]))
        sem = self.sem[key]
        self.q[eng].append(lambda E, o=out, i=in_, s=sem:
                           E.dma_start(out=o, in_=i).then_inc(s, 16))
        self.cnt[key] += 16
        return (key, self.cnt[key])

    def collective(self, kind, op, ins, outs, deps=()):
        eng = "gpsimd"
        for d in deps:
            self._wait(eng, d)
        sem = self.sem[eng]
        self.q[eng].append(
            lambda E, k=kind, o=op, I=ins, O=outs, s=sem:
            E.collective_compute(k, o, replica_groups=[list(range(NCORE))],
                                 ins=I, outs=O).then_inc(s, 1))
        self.cnt[eng] += 1
        return (eng, self.cnt[eng])

    def wait_only(self, eng, deps):
        for d in deps:
            self._wait(eng, d)

    def bar(self, eng):
        return (eng, self.cnt[eng])

    def dma_bar(self, eng):
        return [(f"dma_{eng}{ch}", self.cnt[f"dma_{eng}{ch}"])
                for ch in range(self.NCH)]

    def run(self, block):
        def mk(name):
            def body(E):
                for fn in self.q[name]:
                    fn(E)
            return body
        for name in ENGS:
            getattr(block, name)(mk(name))


class _StopBuild(Exception):
    pass


def build_nc(stop=None):
    import os
    stop = stop or os.environ.get("KSTOP") or None
    nc = bass.Bass("TRN2", target_bir_lowering=False)

    h_x = nc.dram_tensor("x_ext", [EXT, D], F32, kind="ExternalInput").ap()
    h_wq = nc.dram_tensor("Wq", [D, D], F32, kind="ExternalInput").ap()
    h_wk = nc.dram_tensor("Wk", [D, 512], F32, kind="ExternalInput").ap()
    h_wv = nc.dram_tensor("Wv", [D, 512], F32, kind="ExternalInput").ap()
    h_wo = nc.dram_tensor("Wo", [D, D], F32, kind="ExternalInput").ap()
    h_lam = nc.dram_tensor("lam", [1, NH], F32, kind="ExternalInput").ap()
    h_w1s = nc.dram_tensor("w1s", [1024, D], F32, kind="ExternalInput").ap()
    h_b1 = nc.dram_tensor("b1", [1, I_DIM], F32, kind="ExternalInput").ap()
    h_w2s = nc.dram_tensor("w2s", [256, I_DIM], F32, kind="ExternalInput").ap()
    h_b2 = nc.dram_tensor("b2", [1, D], F32, kind="ExternalInput").ap()
    h_cos = nc.dram_tensor("cos_s", [128, EXT], F32, kind="ExternalInput").ap()
    h_sin = nc.dram_tensor("sin_s", [128, EXT], F32, kind="ExternalInput").ap()
    h_msk = nc.dram_tensor("masks", [N_MASKS, 128, 128], BF16,
                           kind="ExternalInput").ap()
    h_ibf = nc.dram_tensor("ident_bf", [128, 128], BF16,
                           kind="ExternalInput").ap()
    h_if32 = nc.dram_tensor("ident_f32", [128, 128], F32,
                            kind="ExternalInput").ap()
    h_p = nc.dram_tensor("rotP", [128, 128], BF16, kind="ExternalInput").ap()
    h_out = nc.dram_tensor("out", [TOK, D], F32, kind="ExternalOutput").ap()

    d_qw1sh = nc.dram_tensor("qw1sh", [NFC, 128, 1024], BF16)
    d_qw1 = nc.dram_tensor("qw1full", [NCORE, NFC, 128, 1024], BF16)
    d_qw2sh = nc.dram_tensor("qw2sh", [NIC // 2, 128, 512], BF16)
    d_qw2 = nc.dram_tensor("qw2full", [NCORE, NIC // 2, 128, 512], BF16)
    d_sw1sh = nc.dram_tensor("sw1sh", [1, 1024], F32)
    d_sw1 = nc.dram_tensor("sw1full", [NCORE, 1024], F32)
    d_sw2sh = nc.dram_tensor("sw2sh", [1, 256], F32)
    d_sw2 = nc.dram_tensor("sw2full", [NCORE, 256], F32)
    d_ar1i = nc.dram_tensor("ar1i", [1, 1], F32)
    d_ar1o = nc.dram_tensor("ar1o", [1, 1], F32)
    d_ar2i = nc.dram_tensor("ar2i", [1, 1], F32)
    d_ar2o = nc.dram_tensor("ar2o", [1, 1], F32)

    d_h = nc.dram_tensor("h_spill", [NIC, 128, TOK], F32)
    d_xn = nc.dram_tensor("xn_spill", [NTT, 128, D], F32)
    d_qx = nc.dram_tensor("qx_spill", [NFC, 128, TOK], BF16)
    d_wqbf = nc.dram_tensor("wqbf", [D, D], BF16)
    d_afm = nc.dram_tensor("afm_spill", [NH, 128, TOK], BF16)

    with ExitStack() as ctx:
      try:
        P = Planner(nc, ctx)
        import os as _os
        stack = []
        birth = {}
        tok_pool = []
        live_psum = {}

        def sb(name, shape, dt=F32):
            cm = nc.sbuf_tensor(name, shape, dt)
            h = cm.__enter__()
            stack.append([name, cm, False])
            birth[name] = list(tok_pool)
            if _os.environ.get("KBUILD_DEBUG"):
                print(f"[sb] {name} remaining={nc.sbuf_bytes_remaining}")
            return h

        def bd(name):
            return birth.get(name, [])

        import os as _os
        def _dump_stack(tag):
            if _os.environ.get("KBUILD_DEBUG"):
                alive = [(e[0], e[2]) for e in stack]
                print(f"[{tag}] stack:", alive)

        def free(*names, token=None):
            for n in names:
                if n in live_psum:
                    live_psum.pop(n).__exit__(None, None, None)
                    continue
                for ent in stack:
                    if ent[0] == n:
                        ent[2] = True
                        break
                else:
                    raise KeyError(n)
            if token is not None:
                tok_pool.append(token)
            while stack and stack[-1][2]:
                nm, cm, _ = stack.pop()
                cm.__exit__(None, None, None)
                if _os.environ.get("KBUILD_DEBUG"):
                    print(f"[pop] {nm} remaining={nc.sbuf_bytes_remaining}")

        def psum(name, shape, dt=F32):
            cm = nc.psum_tensor(name, shape, dt)
            h = cm.__enter__()
            live_psum[name] = cm
            return h

        t_ibf = sb("ibf", [128, 128], BF16)
        t_if32 = sb("if32", [128, 128], F32)
        t_p = sb("rotp", [128, 128], BF16)
        t_lam1 = sb("lam1", [128, NH], F32)
        t_lamn = sb("lamn", [128, NH], F32)
        t_wax = sb("w_ax", [128, 8], F32)
        t_wc = sb("w_c", [128, 8], F32)
        t_sw1 = sb("sw1loc", [128, 8], F32)
        t_w2ax = sb("w2ax", [128, 8], F32)
        t_w2axf = sb("w2axf", [128, 2], F32)
        t_w2c = sb("w2c", [128, 2], F32)
        t_sw2 = sb("sw2loc", [128, 2], F32)
        t_eps = sb("epsb", [128, 1], F32)
        t_ve = sb("lnve", [128, 1], F32)
        t_nw = sb("lnnw", [128, 1], F32)
        t_bs = sb("bnstat", [128, 24], F32)
        t_bo = sb("bnout", [128, 2], F32)
        t_rstd = sb("rstd", [128, 1], F32)
        t_ax = sb("axn", [128, NTT], F32)
        t_ax1 = sb("ax1", [128, 1], F32)
        t_axr = sb("axr", [1, 1], F32)
        t_amx = sb("amx", [128, 1], F32)
        t_cx = sb("cx", [128, 1], F32)
        t_sx1 = sb("sx1", [128, 1], F32)
        t_ch = sb("chq", [128, 1], F32)
        t_amh = sb("amh", [128, 1], F32)
        t_sx2 = sb("sx2", [128, 1], F32)
        t_st = sb("stats", [128, 2 * NH], F32)
        t_rr = sb("rstats", [128, 2 * NH], F32)
        t_mn = sb("mneg", [128, NH], F32)
        t_bneg = sb("bneg", [128, NH], F32)
        t_uu = sb("unorm", [128, NH], F32)
        t_ww = sb("wnorm", [128, NH], F32)
        t_ds = sb("dsum", [128, NH], F32)
        t_hax = sb("hax", [128, 128], F32)
        t_hx1 = sb("hx1", [128, 1], F32)
        t_hxr = sb("hxr", [1, 1], F32)
        t_swld = sb("swld", [64, 128], F32)
        t_s1c = sb("s1c", [128, 64], F32)
        t_b1c = sb("b1c", [128, 64], F32)
        t_s2c = sb("s2c", [128, 16], F32)
        t_b2c = sb("b2c", [128, 16], F32)
        stg = [sb(f"stg{i}", [128, 512], F32) for i in range(6)]
        bstg = [sb(f"bstg{i}", [128, D], F32) for i in range(2)]
        t_i32s = sb("i32stg", [128, 512], I32)
        t_fstg = [sb("fstgA", [128, 512], F32), sb("fstgB", [128, 512], F32)]
        t_kraw = sb("kraw", [128, 512], BF16)
        t_krot = sb("krot", [128, 512], BF16)
        t_u1 = sb("ropeu", [128, 512], F32)
        t_u2 = sb("ropev", [128, 512], F32)
        t_qstg = [sb("qstgA", [128, 512], BF16), sb("qstgB", [128, 512], BF16)]
        t_msk = sb("msk", [128, N_MASKS * 128], BF16)
        t_cos = sb("cosb", [128, EXT], F32)
        t_sin = sb("sinb", [128, EXT], F32)
        t_vtm = sb("vtm", [128, NET * 512], BF16)
        t_kfm = sb("kfm", [128, NKV * EXT], BF16)
        t_qrp = sb("qrp", [128, NH * TOK], BF16)
        t_xnfm = sb("xnfm", [128, NFC * EXT], BF16)

        dep_ibf = P.dma("sync", t_ibf[:], h_ibf[:, :])
        dep_if32 = P.dma("sync", t_if32[:], h_if32[:, :])
        dep_p = P.dma("sync", t_p[:], h_p[:, :])
        dep_msk = None
        for i in range(N_MASKS):
            dep_msk = P.dma("sync", t_msk[:, i * 128:(i + 1) * 128],
                            h_msk[i, :, :])
        dep_cos = P.dma("sync", t_cos[:], h_cos[:, :])
        dep_sin = P.dma("sync", t_sin[:], h_sin[:, :])
        dep_lam = P.dma("sync", t_lam1[:],
                        bass.AP(h_lam.tensor, 0, [[0, 128], [1, NH]]))
        consts_bar = P.dma_bar("sync")
        dep_msk = consts_bar[-1] if False else dep_msk
        dep_lamn = P.emit("vector", lambda E: E.tensor_single_scalar(
            out=t_lamn[:], in_=t_lam1[:], scalar=-1.0, op=OP.mult),
            deps=[dep_lam])
        dep_eps = P.emit("vector", lambda E: E.memset(t_eps[:], EPS_LN))

        def _stop_here(tokens):
            st = P.dma("gpsimd", h_out[0:128, 0:512], stg[0][:],
                       deps=list(tokens))
            P.wait_only("gpsimd", [st])
            raise _StopBuild()

        try:
            pass
        except Exception:
            pass
        t_wkbf = sb("wkbf", [128, NFC * 512], BF16)
        t_wvbf = sb("wvbf", [128, NFC * 512], BF16)
        wfree = [None, None]
        nxt = 0
        for fc in range(NFC):
            for (hsrc, dst) in ((h_wk, t_wkbf), (h_wv, t_wvbf)):
                bi = nxt % 2
                nxt += 1
                buf = stg[bi]
                dep = [wfree[bi]] if wfree[bi] else []
                ld = P.dma("scalar", buf[:], hsrc[fc * 128:(fc + 1) * 128, :],
                           deps=dep)
                ct = P.emit("scalar", lambda E, b=buf, d=dst, fc=fc: E.copy(
                    out=d[:, fc * 512:(fc + 1) * 512], in_=b[:]), deps=[ld])
                wfree[bi] = ct
        wkv_done = P.bar("scalar")
        stg_tok = [wfree[0], wfree[1], None, None, None, None]

        t_wqc = [sb("wqcA", [128, 512], BF16), sb("wqcB", [128, 512], BF16),
                 sb("wqcC", [128, 512], BF16), sb("wqcD", [128, 512], BF16)]

        p_wt = [psum("p_wt", [128, 512], BF16),
                psum("p_wt2", [128, 512], BF16)]
        qstg_tok = [[], []]
        bstg_tok = [[], []]
        t_i32b = sb("i32stgB", [128, 512], I32)
        qs = {"q": 0, "r0": [None, None], "q8": [None, None],
              "tp": [None, None], "ev": [None, None], "pend": []}

        def _q_stage2(ent):
            (qq, buf, c_col, dst_fn, tile0) = ent
            qb = qq % 2
            vbuf = stg[2] if qb == 0 else t_fstg[0]
            mbuf = stg[3] if qb == 0 else t_fstg[1]
            ibuf = t_i32s if qb == 0 else t_i32b
            kbuf = t_kraw if qb == 0 else t_krot
            pwt = p_wt[qb]
            r0 = P.emit("scalar", lambda E, ib=ibuf, vb=vbuf: E.activation(
                out=ib[:], in_=vb[:], func=AF.Copy),
                deps=[qs["vv"][qb]] +
                     ([qs["q8"][qb]] if qs["q8"][qb] else []))
            qs["r0"][qb] = r0
            mm_ = P.emit("vector", lambda E, mb=mbuf, vb=vbuf, ib=ibuf:
                         E.tensor_tensor(
                out=mb[:], in0=vb[:], in1=ib[:], op=OP.is_lt),
                deps=[r0] + ([stg_tok[3]] if stg_tok[3] and qq < 2 else []))
            q8 = P.emit("vector", lambda E, kb=kbuf, ib=ibuf, mb=mbuf:
                        E.scalar_tensor_tensor(
                out=kb[:], in0=ib[:], scalar=0.0, in1=mb[:],
                op0=OP.add, op1=OP.subtract),
                deps=[mm_] + ([qs["tp"][qb]] if qs["tp"][qb] else []))
            qs["q8"][qb] = q8
            gb = qq % 2
            tts = None
            for k in range(4):
                tts = P.emit("tensor", lambda E, k=k, kb=kbuf, pw=pwt:
                             E.transpose(
                    out=pw[:, k * 128:(k + 1) * 128],
                    in_=kb[:, k * 128:(k + 1) * 128],
                    identity=t_ibf[:]),
                    deps=[q8, dep_ibf] +
                         ([qs["ev"][qb]] if qs["ev"][qb] and k == 0 else []))
            qs["tp"][qb] = tts
            ev = P.emit("scalar", lambda E, gb=gb, pw=pwt: E.copy(
                out=t_qstg[gb][:], in_=pw[:, 0:512]),
                deps=[tts] + qstg_tok[gb])
            qs["ev"][qb] = ev
            qstg_tok[gb] = [P.dma("scalar", dfn,
                                  t_qstg[gb][:, c0:c1], deps=[ev])
                            for (dfn, c0, c1) in dst_fn(tile0)]

        qs["vv"] = [None, None]

        def quant_slice(buf, c_col, dst_fn, qoff):
            last_vv = None
            for qtr in range(4):
                qq = qs["q"]
                qs["q"] = qq + 1
                qb = qq % 2
                vbuf = stg[2] if qb == 0 else t_fstg[0]
                vv = P.emit("vector", lambda E, b=buf, qtr=qtr, c=c_col,
                            vb=vbuf: E.tensor_scalar(
                    out=vb[:], in0=b[:, qtr * 512:(qtr + 1) * 512],
                    scalar1=c, scalar2=0.5, op0=OP.mult, op1=OP.add),
                    deps=([qs["r0"][qb]] if qs["r0"][qb] else []) +
                         ([stg_tok[2]] if stg_tok[2] and qq < 2 else []))
                qs["vv"][qb] = vv
                last_vv = vv
                qs["pend"].append((qq, buf, c_col, dst_fn, qoff + qtr * 4))
                if len(qs["pend"]) >= 2:
                    _q_stage2(qs["pend"].pop(0))
            return last_vv

        def quant_flush():
            while qs["pend"]:
                _q_stage2(qs["pend"].pop(0))

        wq_fpool = [t_u1, t_u2, stg[4], stg[5]]
        _wqst = {"ch": 0, "fq": [None] * 4, "cq": [None] * 4}

        def emit_wq_cast(n):
            for _ in range(n):
                ch = _wqst["ch"]
                if ch >= 64:
                    return
                _wqst["ch"] = ch + 1
                r, c4 = divmod(ch, 4)
                fi = ch % 4
                bo = ch % 4
                fbuf = wq_fpool[fi]
                ld = P.dma("gpsimd", fbuf[:],
                           h_wq[r * 128:(r + 1) * 128,
                                c4 * 512:(c4 + 1) * 512],
                           deps=[_wqst["fq"][fi]] if _wqst["fq"][fi] else [])
                ct = P.emit("vector", lambda E, fb=fbuf, bo=bo:
                            E.tensor_copy(
                    out=t_wqc[bo][:], in_=fb[:]),
                    deps=[ld] + ([_wqst["cq"][bo]] if _wqst["cq"][bo] else
                                 bd("wqcA") + bd("wqcB") + bd("wqcC") +
                                 bd("wqcD")))
                _wqst["fq"][fi] = ct
                st = P.dma("gpsimd",
                           d_wqbf[r * 128:(r + 1) * 128,
                                  c4 * 512:(c4 + 1) * 512],
                           t_wqc[bo][:], deps=[ct])
                _wqst["cq"][bo] = st

        for pt in range(8):
            bi = pt % 2
            buf = bstg[bi]
            dep = list(bstg_tok[bi])
            ld = P.dma("sync", buf[:], h_w1s[pt * 128:(pt + 1) * 128, :],
                       deps=dep)
            amax = P.emit("vector", lambda E, b=buf, pt=pt: E.tensor_reduce(
                out=t_wax[:, pt:pt + 1], in_=b[:], axis=AX, op=OP.max,
                apply_absolute_value=True), deps=[ld])
            rec = P.emit("vector", lambda E, pt=pt: E.reciprocal(
                out=t_wc[:, pt:pt + 1], in_=t_wax[:, pt:pt + 1]), deps=[amax])
            cc = P.emit("vector", lambda E, pt=pt: E.tensor_scalar(
                out=t_wc[:, pt:pt + 1], in0=t_wc[:, pt:pt + 1], scalar1=QMAX,
                scalar2=1e8, op0=OP.mult, op1=OP.min), deps=[rec])
            P.emit("vector", lambda E, pt=pt: E.tensor_scalar(
                out=t_sw1[:, pt:pt + 1], in0=t_wax[:, pt:pt + 1], scalar1=QINV,
                scalar2=QEPS, op0=OP.mult, op1=OP.max), deps=[amax])
            P.wait_only("vector", [cc])
            bstg_tok[bi] = [quant_slice(
                buf, t_wc[:, pt:pt + 1],
                lambda fc0, pt=pt: [(bass.AP(
                    d_qw1sh.ap().tensor, fc0 * 128 * 1024 + pt * 128,
                    [[1024, 128], [128 * 1024, 4], [1, 128]]), 0, 512)],
                0)]
            emit_wq_cast(5)
        quant_flush()
        sh1 = P.dma_bar("scalar")
        p_sws = psum("p_sws", [128, 128], F32)
        tr1 = P.emit("tensor", lambda E: E.transpose(
            out=p_sws[0:8, :], in_=t_sw1[:, 0:8], identity=t_if32[:]),
            deps=[P.bar("vector"), dep_if32])
        ev1 = P.emit("vector", lambda E: E.tensor_copy(
            out=t_swld[0:8, :], in_=p_sws[0:8, :]), deps=[tr1])
        sw1st = P.dma("sync",
                      bass.AP(d_sw1sh.ap().tensor, 0, [[128, 8], [1, 128]]),
                      t_swld[0:8, :], deps=[ev1])

        for pt in range(2):
            for cc2 in range(4):
                bi = cc2 % 2
                buf = bstg[bi]
                dep = list(bstg_tok[bi])
                ld = P.dma("sync", buf[:],
                           h_w2s[pt * 128:(pt + 1) * 128,
                                 cc2 * 2048:(cc2 + 1) * 2048], deps=dep)
                am = P.emit("vector", lambda E, b=buf, pt=pt, cc2=cc2:
                            E.tensor_reduce(
                    out=t_w2ax[:, pt * 4 + cc2:pt * 4 + cc2 + 1],
                    in_=b[:], axis=AX, op=OP.max,
                    apply_absolute_value=True), deps=[ld])
                bstg_tok[bi] = [am]
                emit_wq_cast(2)
            amf = P.emit("vector", lambda E, pt=pt: E.tensor_reduce(
                out=t_w2axf[:, pt:pt + 1], in_=t_w2ax[:, pt * 4:pt * 4 + 4],
                axis=AX, op=OP.max), deps=bstg_tok[0] + bstg_tok[1])
            rec = P.emit("vector", lambda E, pt=pt: E.reciprocal(
                out=t_w2c[:, pt:pt + 1], in_=t_w2axf[:, pt:pt + 1]),
                deps=[amf])
            P.emit("vector", lambda E, pt=pt: E.tensor_scalar(
                out=t_w2c[:, pt:pt + 1], in0=t_w2c[:, pt:pt + 1], scalar1=QMAX,
                scalar2=1e8, op0=OP.mult, op1=OP.min), deps=[rec])
            P.emit("vector", lambda E, pt=pt: E.tensor_scalar(
                out=t_sw2[:, pt:pt + 1], in0=t_w2axf[:, pt:pt + 1],
                scalar1=QINV, scalar2=QEPS, op0=OP.mult, op1=OP.max),
                deps=[amf])
        c2done = P.bar("vector")
        tr2 = P.emit("tensor", lambda E: E.transpose(
            out=p_sws[0:2, :], in_=t_sw2[:, 0:2], identity=t_if32[:]),
            deps=[c2done, dep_if32, ev1])
        ev2 = P.emit("vector", lambda E: E.tensor_copy(
            out=t_swld[32:34, :], in_=p_sws[0:2, :]), deps=[tr2, ev1])
        sw2st = P.dma("sync",
                      bass.AP(d_sw2sh.ap().tensor, 0, [[128, 2], [1, 128]]),
                      t_swld[32:34, :], deps=[ev2])
        for cc2 in range(4):
            for pt in range(2):
                bi = pt % 2
                buf = bstg[bi]
                dep = list(bstg_tok[bi])
                ld = P.dma("sync", buf[:],
                           h_w2s[pt * 128:(pt + 1) * 128,
                                 cc2 * 2048:(cc2 + 1) * 2048],
                           deps=dep + [c2done])
                P.wait_only("vector", [ld])
                bstg_tok[bi] = [quant_slice(
                    buf, t_w2c[:, pt:pt + 1],
                    lambda icl0, pt=pt: [
                        (bass.AP(d_qw2sh.ap().tensor,
                                 (icl0 // 2 + hi) * 128 * 512 + pt * 128,
                                 [[512, 128], [256, 2], [1, 128]]),
                         hi * 256, hi * 256 + 256) for hi in range(2)],
                    cc2 * 16)]
                emit_wq_cast(1)
        sh2_done = None
        quant_flush()
        free("i32stgB", token=qs["q8"][1])
        sh2_bar = P.dma_bar("scalar")
        if stop == "W":
            _stop_here(sh2_bar)
        ag_sw1 = P.collective("AllGather", OP.bypass, [d_sw1sh.ap().opt()],
                              [d_sw1.ap().opt()], deps=[sw1st])
        ag_sw2 = P.collective("AllGather", OP.bypass, [d_sw2sh.ap().opt()],
                              [d_sw2.ap().opt()], deps=[sw2st])

        emit_wq_cast(64)
        wq_sts_bar = P.dma_bar("gpsimd")
        free("wqcD", token=_wqst["cq"][3])
        free("wqcC", token=_wqst["cq"][2])
        free("wqcB", token=_wqst["cq"][1])
        free("wqcA", token=_wqst["cq"][0])

        p_tr = psum("p_tr", [128, 512], F32)

        prev_ln = None
        prev_sq = None
        prev_trx = None
        axd = []
        xnsp = {}
        for tt in range(NET):
            bi = tt % 2
            buf = bstg[bi]
            dep = list(bstg_tok[bi])
            ld = P.dma("sync", buf[:], h_x[tt * 128:(tt + 1) * 128, :],
                       deps=dep)
            bns = None
            for chk in range(4):
                bns = P.emit("vector", lambda E, b=buf, chk=chk: E.bn_stats(
                    out=t_bs[:, chk * 6:(chk + 1) * 6],
                    in_=b[:, chk * 512:(chk + 1) * 512]),
                    deps=[ld] + ([prev_ln] if prev_ln and chk == 0 else []))
            ag = P.emit("vector", lambda E: E.bn_aggr(
                out=t_bo[:], in_=t_bs[:]),
                deps=[bns] + ([prev_sq] if prev_sq else []))
            sq = P.emit("scalar", lambda E: E.activation(
                out=t_rstd[:], in_=t_bo[:, 1:2], func=AF.Sqrt,
                bias=t_eps[:]), deps=[ag, dep_eps])
            prev_sq = sq
            rc0 = P.emit("vector", lambda E: E.reciprocal(
                out=t_rstd[:], in_=t_rstd[:]), deps=[sq])
            ve = P.emit("vector", lambda E: E.tensor_scalar(
                out=t_ve[:], in0=t_bo[:, 1:2], scalar1=t_eps[:], scalar2=0.0,
                op0=OP.add, op1=OP.add), deps=[ag, dep_eps])
            n1 = P.emit("vector", lambda E: E.tensor_tensor(
                out=t_nw[:], in0=t_rstd[:], in1=t_rstd[:], op=OP.mult),
                deps=[rc0])
            n2 = P.emit("vector", lambda E: E.tensor_tensor(
                out=t_nw[:], in0=t_nw[:], in1=t_ve[:], op=OP.mult),
                deps=[n1, ve])
            n3 = P.emit("vector", lambda E: E.tensor_scalar(
                out=t_nw[:], in0=t_nw[:], scalar1=-0.5, scalar2=1.5,
                op0=OP.mult, op1=OP.add), deps=[n2])
            rc = P.emit("vector", lambda E: E.tensor_tensor(
                out=t_rstd[:], in0=t_rstd[:], in1=t_nw[:], op=OP.mult),
                deps=[n3])
            own = tt >= 4
            xnt = P.emit("vector", lambda E, b=buf: E.tensor_scalar(
                out=b[:], in0=b[:], scalar1=t_bo[:, 0:1],
                scalar2=t_rstd[:], op0=OP.subtract, op1=OP.mult),
                deps=[rc])
            prev_ln = xnt
            if own:
                axd.append(P.emit("vector", lambda E, b=buf, tt=tt:
                                  E.tensor_reduce(
                    out=t_ax[:, tt - 4:tt - 3], in_=b[:], axis=AX, op=OP.max,
                    apply_absolute_value=True), deps=[xnt]))
                xnsp[tt - 4] = P.dma("sync", d_xn[tt - 4, :, :], buf[:],
                                     deps=[axd[-1]])
            for g in range(4):
                tts = []
                for k in range(4):
                    fc = g * 4 + k
                    tts.append(P.emit("tensor", lambda E, b=buf, fc=fc, k=k:
                                      E.transpose(
                        out=p_tr[:, k * 128:(k + 1) * 128],
                        in_=b[:, fc * 128:(fc + 1) * 128],
                        identity=t_if32[:]),
                        deps=[xnt, dep_if32] +
                             ([prev_trx] if prev_trx else [])))

                def ev_xn(E, g=g, tt=tt):
                    return E.copy(
                        out=bass.AP(t_xnfm, g * 4 * EXT + tt * 128,
                                    [[NFC * EXT, 128], [EXT, 4], [1, 128]]),
                        in_=p_tr[:, 0:512])
                prev_trx = P.emit("scalar", ev_xn, deps=[tts[-1]])
            bstg_tok[bi] = [prev_trx] + ([xnsp[tt - 4]] if own else [])
        xn_fm_done = prev_trx
        p_axt = psum("p_axt", [1, 128], F32)
        a1 = P.emit("vector", lambda E: E.tensor_reduce(
            out=t_ax1[:], in_=t_ax[:], axis=AX, op=OP.max), deps=axd)
        a2 = P.emit("tensor", lambda E: E.transpose(
            out=p_axt[:], in_=t_ax1[:], identity=t_if32[:]),
            deps=[a1, dep_if32])
        a3 = P.emit("vector", lambda E: E.tensor_reduce(
            out=t_axr[:], in_=p_axt[0:1, :], axis=AX, op=OP.max), deps=[a2])
        a4 = P.dma("sync", d_ar1i[:, :], t_axr[0:1, 0:1], deps=[a3])
        ar1 = P.collective("AllReduce", OP.max, [d_ar1i.ap().opt()],
                           [d_ar1o.ap().opt()], deps=[a4])
        free("p_axt", "p_tr", "p_sws", "p_wt2", "p_wt")
        if stop == "L":
            _stop_here([ar1])

        p_v = [psum("p_v0", [128, 512], F32), psum("p_v1", [128, 512], F32)]
        vev = [None, None]
        for tt in range(NET):
            bi = tt % 2
            mmq = None
            for fc in range(NFC):
                mmq = P.emit("tensor", lambda E, tt=tt, fc=fc, bi=bi:
                             E.matmul(
                    out=p_v[bi][:],
                    lhsT=t_xnfm[:, fc * EXT + tt * 128:
                                fc * EXT + (tt + 1) * 128],
                    rhs=t_wvbf[:, fc * 512:(fc + 1) * 512],
                    start=(fc == 0), stop=(fc == NFC - 1)),
                    deps=(([xn_fm_done, wkv_done] +
                           ([vev[bi]] if vev[bi] else []))
                          if fc == 0 else []))
            vev[bi] = P.emit("vector", lambda E, tt=tt, bi=bi: E.tensor_copy(
                out=t_vtm[:, tt * 512:(tt + 1) * 512], in_=p_v[bi][:]),
                deps=[mmq] + (bd("vtm") if tt < 2 else []))
        v_done_v = P.bar("vector")
        free("p_v1", "p_v0")
        if stop == "KV1":
            _stop_here([v_done_v])

        p_r = [psum("p_r0", [128, 512], F32), psum("p_r1", [128, 512], F32)]
        p_k = [psum("p_k0", [128, 512], F32), psum("p_k1", [128, 512], F32)]

        t_kraw2 = t_krot
        t_u1b = stg[2]
        t_u2b = stg[3]
        t_krA = t_qstg[0]
        t_krB = t_qstg[1]
        qstg_guard = qstg_tok[0] + qstg_tok[1]
        rope_st = {"rn": 0, "u1": [None, None], "u2": [None, None],
                   "cbk": {}, "crr": {}}

        def rope_bank_tok(pp):
            t = rope_st["cbk"].get(id(pp))
            return [t] if t else []

        def rope_block(dst_ap, ppk, ppr, cos_c0, kdep, bdeps=(),
                       rot_deps=()):
            rc = rope_st["rn"]
            rope_st["rn"] = rc + 1
            rbi = rc % 2
            kb = t_kraw if rbi == 0 else t_kraw2
            rb = t_krA if rbi == 0 else t_krB
            ub1 = t_u1 if rbi == 0 else t_u1b
            ub2 = t_u2 if rbi == 0 else t_u2b
            pu1 = rope_st["u1"][rbi]
            pu2 = rope_st["u2"][rbi]
            cb = P.emit("scalar", lambda E, kb=kb: E.copy(
                out=kb[:], in_=ppk[:]),
                deps=[kdep] + ([pu1] if pu1 else bd("kraw") + bd("krot")))
            rope_st["cbk"][id(ppk)] = cb
            prev_cr = rope_st["crr"].get(id(ppr))
            rot = P.emit("tensor", lambda E, kb=kb: E.matmul(
                out=ppr[:], lhsT=t_p[:], rhs=kb[:], start=True,
                stop=True), deps=[cb, dep_p] +
                ([prev_cr] if prev_cr else []) + list(rot_deps))
            cr = P.emit("scalar", lambda E, rb=rb: E.copy(
                out=rb[:], in_=ppr[:]),
                deps=[rot] + ([pu2] if pu2 else qstg_guard))
            rope_st["crr"][id(ppr)] = cr
            u1 = P.emit("vector", lambda E, kb=kb, ub=ub1:
                        E.tensor_tensor(
                out=ub[:], in0=kb[:],
                in1=t_cos[:, cos_c0:cos_c0 + 512], op=OP.mult),
                deps=[cb, dep_cos])
            rope_st["u1"][rbi] = u1
            u2 = P.emit("vector", lambda E, rb=rb, ub=ub2:
                        E.tensor_tensor(
                out=ub[:], in0=rb[:],
                in1=t_sin[:, cos_c0:cos_c0 + 512], op=OP.mult),
                deps=[cr, dep_sin])
            rope_st["u2"][rbi] = u2
            return P.emit("vector", lambda E, ub1=ub1, ub2=ub2:
                          E.tensor_tensor(
                out=dst_ap, in0=ub1[:], in1=ub2[:], op=OP.add),
                deps=[u1, u2] + list(bdeps))

        rope_done = None
        for kv in range(NKV):
            for tc in range(3):
                bi = tc % 2
                mmq = None
                for fc in range(NFC):
                    mmq = P.emit("tensor",
                                 lambda E, kv=kv, tc=tc, fc=fc, bi=bi:
                                 E.matmul(
                        out=p_k[bi][:],
                        lhsT=t_wkbf[:, fc * 512 + kv * 128:
                                    fc * 512 + (kv + 1) * 128],
                        rhs=t_xnfm[:, fc * EXT + tc * 512:
                                   fc * EXT + (tc + 1) * 512],
                        start=(fc == 0), stop=(fc == NFC - 1)),
                        deps=(([xn_fm_done, wkv_done] +
                               rope_bank_tok(p_k[bi]))
                              if fc == 0 else []))
                if stop == "KV3" and kv == 0 and tc == 0:
                    _stop_here([mmq])
                rope_done = rope_block(
                    t_kfm[:, kv * EXT + tc * 512:kv * EXT + (tc + 1) * 512],
                    p_k[bi], p_r[bi], tc * 512, mmq,
                    bd("kfm") if kv == 0 and tc == 0 else [],
                    rot_deps=[v_done_v])
                if stop == "KV2" and kv == 0 and tc == 0:
                    _stop_here([rope_done])
        k_done = rope_done
        if stop == "KV":
            _stop_here([k_done])
        free("p_k1", "p_k0")
        free("wvbf", "wkbf", token=k_done)

        t_wqg = [sb("wqgA", [128, 16 * 512], BF16),
                 sb("wqgB", [128, 16 * 512], BF16)]
        p_q = [psum("p_q0", [128, 512], F32), psum("p_q1", [128, 512], F32)]
        rope_st["cbk"][id(p_q[0])] = k_done
        rope_st["cbk"][id(p_q[1])] = k_done
        qg_free = [None, None]
        qrope_done = k_done
        q_head = {}
        for hg in range(4):
            gb = hg % 2
            gdep = ([qg_free[gb]] if qg_free[gb] else
                    bd("wqgA") + bd("wqgB")) + wq_sts_bar
            gld = P.dma("sync" if hg % 2 == 0 else "scalar",
                        bass.AP(t_wqg[gb], 0,
                                [[16 * 512, 128], [512, 16], [1, 512]]),
                        bass.AP(d_wqbf.ap().tensor, hg * 512,
                                [[D, 128], [128 * D, 16], [1, 512]]),
                        deps=gdep)
            last_mm = None
            for hh in range(hg * 4, hg * 4 + 4):
                for th in range(2):
                    pbi = th % 2
                    mmq = None
                    for fc in range(NFC):
                        mmq = P.emit(
                            "tensor",
                            lambda E, fc=fc, th=th, hh=hh, gb=gb, pbi=pbi:
                            E.matmul(
                                out=p_q[pbi][:],
                                lhsT=t_wqg[gb][:, fc * 512 +
                                               (hh % 4) * 128:
                                               fc * 512 +
                                               (hh % 4) * 128 + 128],
                                rhs=t_xnfm[:, fc * EXT + 512 + th * 512:
                                           fc * EXT + 512 + (th + 1) * 512],
                                start=(fc == 0), stop=(fc == NFC - 1)),
                            deps=(([gld, xn_fm_done] +
                                   rope_bank_tok(p_q[pbi]))
                                  if fc == 0 else []))
                    last_mm = mmq
                    qrope_done = rope_block(
                        t_qrp[:, hh * TOK + th * 512:
                              hh * TOK + (th + 1) * 512],
                        p_q[pbi], p_r[pbi], 512 + th * 512, mmq,
                        bd("qrp") if hh == 0 and th == 0 else [])
                q_head[hh] = qrope_done
            qg_free[gb] = last_mm
        q_done = qrope_done
        ag_qw1 = P.collective("AllGather", OP.bypass, [d_qw1sh.ap().opt()],
                              [d_qw1.ap().opt()], deps=sh1 + [q_done])
        ag_qw2 = P.collective("AllGather", OP.bypass, [d_qw2sh.ap().opt()],
                              [d_qw2.ap().opt()], deps=sh2_bar + [q_done])
        if stop == "Q":
            _stop_here([q_done])
        free("p_q1", "p_q0", "p_r1", "p_r0")
        free("wqgB", "wqgA")
        free("cosb", "sinb")
        free("xnfm", token=q_done)

        _dump_stack("pre-A")
        t_worA = sb("worA", [128, NH * TOK], BF16)
        t_e = [sb("e0", [128, 1280], F32), sb("e1", [128, 1280], F32)]
        t_t = [sb(f"t_sb{k}", [128, 640], F32) for k in range(4)]
        t_dn = [sb("dn0", [128, 640], BF16), sb("dn1", [128, 640], BF16)]
        t_dT = [sb("dT0", [128, 640], BF16), sb("dT1", [128, 640], BF16)]
        p_sc = [psum("p_sc0", [128, 1536], F32),
                psum("p_sc1", [128, 1536], F32)]
        p_dt = psum("p_dt", [128, 640], BF16)
        p_av = psum("p_av", [128, 128], F32)

        def region_plan(qt):
            plan = []
            for half in range(2):
                base = 640 * half
                segs = [(0, 512), (512, 128)] if half == 0 else \
                       [(0, 384), (384, 256)]
                for (c0, w) in segs:
                    plan.append((base + c0, w, "k", c0, half))
                for s in mask_slots(qt):
                    plan.append((base + s * 128, 128, "m",
                                 MASK_INDEX[(qt, s)], half))
            return plan

        exp_done = {}
        tok_done = {}
        evt_done = {}
        aev_done = {}
        mx_done = {}
        dr_done = {}
        av_done = {}
        scores_mm = {}

        def em_scores(idx):
            qt, hh = idx // NH, idx % NH
            sbi = idx % 2
            psc = p_sc[sbi]
            banks = {}
            for item in region_plan(qt):
                banks.setdefault((item[0] * 4) // 2048, []).append(item)
            mmq = None
            emitted = 0
            for b0 in sorted(banks):
                items = banks[b0]
                for i2, (pc0, w, kind, arg, half) in enumerate(items):
                    st_ = (i2 == 0)
                    sp_ = (i2 == len(items) - 1)
                    deps = []
                    if emitted == 0:
                        deps = [q_head[hh] if idx >= 2 else q_done,
                                k_done] + consts_bar
                        if idx - 2 in exp_done:
                            deps.append(exp_done[idx - 2])
                    if kind == "k":
                        def mk_s(E, qt=qt, hh=hh, pc0=pc0, w=w,
                                 arg=arg, half=half, st_=st_, sp_=sp_,
                                 psc=psc, kv=hh // 4):
                            pp = 64 * half
                            return E.matmul(
                                out=psc[:, pc0:pc0 + w],
                                lhsT=t_qrp[pp:pp + 64,
                                           hh * TOK + qt * 128:
                                           hh * TOK + (qt + 1) * 128],
                                rhs=t_kfm[pp:pp + 64,
                                          kv * EXT + qt * 128 + arg:
                                          kv * EXT + qt * 128 + arg + w],
                                start=st_, stop=sp_)
                        mmq = P.emit("tensor", mk_s, deps=deps)
                    else:
                        def mk_m(E, pc0=pc0, arg=arg, st_=st_, sp_=sp_,
                                 psc=psc):
                            return E.matmul(
                                out=psc[:, pc0:pc0 + 128],
                                lhsT=t_ibf[:],
                                rhs=t_msk[:, arg * 128:(arg + 1) * 128],
                                start=st_, stop=sp_)
                        mmq = P.emit("tensor", mk_m, deps=deps)
                    emitted += 1
            scores_mm[idx] = mmq

        def em_mx(idx):
            hh = idx % NH
            psc = p_sc[idx % 2]
            mx_done[idx] = P.emit("vector", lambda E, psc=psc, hh=hh:
                                  E.tensor_reduce(
                out=t_mn[:, hh:hh + 1], in_=psc[:, 0:1280], axis=AX,
                op=OP.max, negate=True), deps=[scores_mm[idx]])

        def em_exp(idx):
            hh = idx % NH
            psc = p_sc[idx % 2]
            ebuf = t_e[idx % 2]
            ed = [tok_done[idx - 2]] if idx - 2 in tok_done else \
                bd("e0") + bd("e1")
            e1 = P.emit("scalar", lambda E, psc=psc, hh=hh, eb=ebuf:
                        E.activation(
                out=eb[:, 0:640], in_=psc[:, 0:640], func=AF.Exp,
                bias=t_mn[:, hh:hh + 1], scale=1.0,
                accum_out=t_st[:, 2 * hh:2 * hh + 1]),
                deps=[mx_done[idx]] + ed)
            exp_done[idx] = P.emit("scalar", lambda E, psc=psc, hh=hh,
                                   eb=ebuf: E.activation(
                out=eb[:, 640:1280], in_=psc[:, 640:1280], func=AF.Exp,
                bias=t_mn[:, hh:hh + 1], scale=1.0,
                accum_out=t_st[:, 2 * hh + 1:2 * hh + 2]), deps=[e1])

        def em_smalls(idx):
            hh = idx % NH
            ebuf = t_e[idx % 2]
            e2 = exp_done[idx]
            rr = P.emit("vector", lambda E, hh=hh: E.reciprocal(
                out=t_rr[:, 2 * hh:2 * hh + 2],
                in_=t_st[:, 2 * hh:2 * hh + 2]), deps=[e2])
            bn = P.emit("vector", lambda E, hh=hh: E.scalar_tensor_tensor(
                out=t_bneg[:, hh:hh + 1], in0=t_st[:, 2 * hh:2 * hh + 1],
                scalar=t_lamn[:, hh:hh + 1],
                in1=t_rr[:, 2 * hh + 1:2 * hh + 2],
                op0=OP.mult, op1=OP.mult), deps=[rr, dep_lamn])
            td = [dr_done[idx - 4]] if idx - 4 in dr_done else \
                (bd("t_sb0") + bd("t_sb1") + bd("t_sb2") + bd("t_sb3"))
            tok_done[idx] = P.emit("vector", lambda E, hh=hh, eb=ebuf,
                                   ti=idx % 4: E.scalar_tensor_tensor(
                out=t_t[ti][:], in0=eb[:, 640:1280],
                scalar=t_bneg[:, hh:hh + 1], in1=eb[:, 0:640],
                op0=OP.mult, op1=OP.add), deps=[bn] + td)

        def em_tail1(idx):
            hh = idx % NH
            sbi = idx % 2
            dbuf = t_dn[sbi]
            dd = [aev_done[idx - 2]] if idx - 2 in aev_done else \
                bd("dn0") + bd("dn1")
            dr = P.emit("vector", lambda E, hh=hh, db=dbuf, ti=idx % 4:
                        E.tensor_scalar(
                out=db[:], in0=t_t[ti][:], scalar1=0.0, scalar2=0.0,
                op0=OP.max, op1=OP.add,
                accum_out=t_ds[:, hh:hh + 1]),
                deps=[tok_done[idx]] + dd)
            dr_done[idx] = dr
            uu = P.emit("vector", lambda E, hh=hh: E.scalar_tensor_tensor(
                out=t_uu[:, hh:hh + 1], in0=t_st[:, 2 * hh:2 * hh + 1],
                scalar=1e-6, in1=t_ds[:, hh:hh + 1],
                op0=OP.mult, op1=OP.add), deps=[dr])
            wwt = P.emit("vector", lambda E, hh=hh: E.reciprocal(
                out=t_ww[:, hh:hh + 1], in_=t_uu[:, hh:hh + 1]),
                deps=[uu])
            dn = P.emit("scalar", lambda E, hh=hh, db=dbuf:
                        E.activation(
                out=db[:], in_=db[:], func=AF.Copy,
                scale=t_ww[:, hh:hh + 1]), deps=[wwt])
            prev_evt = evt_done.get(idx - 1)
            tts = None
            for s5 in range(5):
                tts = P.emit("tensor", lambda E, s5=s5, db=dbuf:
                             E.transpose(
                    out=p_dt[:, s5 * 128:(s5 + 1) * 128],
                    in_=db[:, s5 * 128:(s5 + 1) * 128],
                    identity=t_ibf[:]),
                    deps=[dn] + ([prev_evt] if prev_evt and s5 == 0
                                 else []))
            return tts

        def em_tail2(idx, tts):
            qt, hh = idx // NH, idx % NH
            kv = hh // 4
            dti = idx % 2
            evt = P.emit("vector", lambda E, dti=dti: E.tensor_copy(
                out=t_dT[dti][:], in_=p_dt[:]),
                deps=[tts] +
                ([aev_done[idx - 2]] if idx - 2 in aev_done else
                 bd("dT0") + bd("dT1")))
            evt_done[idx] = evt
            prev_aev = aev_done.get(idx - 1)
            mav = None
            for s5 in range(5):
                mav = P.emit("tensor",
                             lambda E, s5=s5, qt=qt, kv=kv, dti=dti:
                             E.matmul(
                    out=p_av[:],
                    lhsT=t_vtm[:, (qt + s5) * 512 + kv * 128:
                               (qt + s5) * 512 + (kv + 1) * 128],
                    rhs=t_dT[dti][:, s5 * 128:(s5 + 1) * 128],
                    start=(s5 == 0), stop=(s5 == 4)),
                    deps=(([evt, v_done_v] +
                           ([prev_aev] if prev_aev else []))
                          if s5 == 0 else []))
            av_done[idx] = mav

        def em_aev(idx):
            qt, hh = idx // NH, idx % NH
            aev_done[idx] = P.emit("vector", lambda E, hh=hh, qt=qt:
                                   E.tensor_copy(
                out=t_worA[:, hh * TOK + qt * 128:
                           hh * TOK + qt * 128 + 128],
                in_=p_av[:]),
                deps=[av_done[idx]] + (bd("worA") if idx == 0 else []))

        NIDX = NTT * NH
        for i in range(-2, NIDX + 5):
            if 0 <= i < NIDX:
                em_exp(i)
            if 0 <= i + 2 < NIDX:
                em_scores(i + 2)
            if 0 <= i - 4 < NIDX:
                em_aev(i - 4)
            if 0 <= i < NIDX:
                em_smalls(i)
            tts = em_tail1(i - 3) if 0 <= i - 3 < NIDX else None
            if 0 <= i + 2 < NIDX:
                em_mx(i + 2)
            if 0 <= i - 3 < NIDX:
                em_tail2(i - 3, tts)
        afm_done = aev_done[NTT * NH - 1]
        wor_sp = []
        for j in range(NH):
            wor_sp.append(P.dma("gpsimd", d_afm[j, :, :],
                                t_worA[:, j * TOK:(j + 1) * TOK],
                                deps=[aev_done[(NTT - 1) * NH + j]]))
        afm_bar = P.dma_bar("gpsimd")
        if stop == "A":
            _stop_here(afm_bar)
        free("p_av", "p_dt", "p_sc1", "p_sc0")
        free("dT1", "dT0", "dn1", "dn0",
             "t_sb3", "t_sb2", "t_sb1", "t_sb0", "e1", "e0",
             "qrp", "kfm", "vtm", token=afm_done)
        free("msk", token=afm_done)
        free("worA")
        tok_pool.extend(afm_bar)

        bq = P.dma("sync", t_amx[:],
                   bass.AP(d_ar1o.ap().tensor, 0, [[0, 128], [1, 1]]),
                   deps=[ar1])
        r1_ = P.emit("vector", lambda E: E.reciprocal(
            out=t_cx[:], in_=t_amx[:]), deps=[bq])
        cx = P.emit("vector", lambda E: E.tensor_scalar(
            out=t_cx[:], in0=t_cx[:], scalar1=QMAX, scalar2=1e8,
            op0=OP.mult, op1=OP.min), deps=[r1_])
        sx1 = P.emit("vector", lambda E: E.tensor_scalar(
            out=t_sx1[:], in0=t_amx[:], scalar1=QINV, scalar2=QEPS,
            op0=OP.mult, op1=OP.max), deps=[bq])

        t_out = sb("outfm", [128, NFC * TOK], F32)
        t_qxr = sb("qxr", [128, NFC * TOK], BF16)
        t_qv = [sb("q_vA", [128, D], F32), sb("q_vB", [128, D], F32)]
        t_qr0s = sb("q_r0A", [128, D], I32)
        t_qr0 = [t_qr0s, t_qr0s]
        t_qm = sb("q_m", [128, D], BF16)
        t_qb = sb("q_qb", [128, D], BF16)
        p_tq = [psum("p_tq", [128, 512], BF16),
                psum("p_tq2", [128, 512], BF16)]
        vv_tok = [None, None]
        r0_tok = [None, None]
        qq_tok = [None, None]
        ev_tok = [None, None]
        last_evq = None

        def qx_stage2(tt):
            qb2 = tt % 2
            r0 = P.emit("scalar", lambda E, qb2=qb2: E.activation(
                out=t_qr0[qb2][:], in_=t_qv[qb2][:], func=AF.Copy),
                deps=[vv_tok[qb2]] + bd("q_r0A") +
                ([qq_tok[1 - qb2]] if qq_tok[1 - qb2] else []))
            r0_tok[qb2] = r0
            mm_ = P.emit("vector", lambda E, qb2=qb2: E.tensor_tensor(
                out=t_qm[:], in0=t_qv[qb2][:], in1=t_qr0[qb2][:],
                op=OP.is_lt), deps=[r0] + bd("q_m"))
            qq = P.emit("vector", lambda E, qb2=qb2:
                        E.scalar_tensor_tensor(
                out=t_qb[:], in0=t_qr0[qb2][:], scalar=0.0, in1=t_qm[:],
                op0=OP.add, op1=OP.subtract),
                deps=[mm_] + bd("q_qb") +
                ([ev_tok[1]] if ev_tok[1] else []))
            qq_tok[qb2] = qq
            for g in range(4):
                pb = g % 2
                tts = None
                for k in range(4):
                    fc = g * 4 + k
                    tts = P.emit("tensor", lambda E, fc=fc, k=k, pb=pb:
                                 E.transpose(
                        out=p_tq[pb][:, k * 128:(k + 1) * 128],
                        in_=t_qb[:, fc * 128:(fc + 1) * 128],
                        identity=t_ibf[:]),
                        deps=[qq] +
                        ([ev_tok[pb]] if ev_tok[pb] and k == 0 else []) +
                        ([afm_done] if tt == 0 and g == 0 and k == 0
                         else []))
                ev_tok[pb] = P.emit("scalar", lambda E, g=g, tt=tt, pb=pb:
                                    E.copy(
                    out=bass.AP(t_qxr, g * 4 * TOK + tt * 128,
                                [[NFC * TOK, 128], [TOK, 4], [1, 128]]),
                    in_=p_tq[pb][:, 0:512]),
                    deps=[tts] + (bd("qxr") if tt == 0 and g == 0 else []))
            return ev_tok[1]

        for tt in range(NTT):
            bi = tt % 2
            qb2 = tt % 2
            buf = bstg[bi]
            dep = list(bstg_tok[bi])
            ld = P.dma("sync", buf[:], d_xn[tt, :, :],
                       deps=dep + [xnsp[tt]])
            vv = P.emit("vector", lambda E, b=buf, qb2=qb2:
                        E.tensor_scalar(
                out=t_qv[qb2][:], in0=b[:], scalar1=t_cx[:], scalar2=0.5,
                op0=OP.mult, op1=OP.add),
                deps=[cx, ld] + bd("q_vA") + bd("q_vB") +
                ([qq_tok[qb2]] if qq_tok[qb2] else []))
            vv_tok[qb2] = vv
            bstg_tok[bi] = [vv]
            if tt >= 1:
                last_evq = qx_stage2(tt - 1)
        last_evq = qx_stage2(NTT - 1)
        qx_done = last_evq
        if stop == "QX":
            _stop_here([qx_done])
        free("p_tq2", "p_tq")
        free("q_qb", "q_m", "q_r0A", "q_vB", "q_vA",
             token=qx_done)
        qx_done_all = [qx_done]


        def load_colT(dst, src_ap, dep_extra, n, p_dst):
            ld = P.dma("sync", t_swld[0:n, :], src_ap, deps=dep_extra)
            tp = P.emit("tensor", lambda E, n=n, p_dst=p_dst: E.transpose(
                out=p_dst[:, 0:n], in_=t_swld[0:n, :],
                identity=t_if32[0:n, 0:n]), deps=[ld, dep_if32])
            return P.emit("vector", lambda E, d=dst, n=n, p_dst=p_dst:
                          E.tensor_copy(
                out=d[:, 0:n], in_=p_dst[0:128, 0:n]), deps=[tp])

        qxr_bar = qx_done_all

        _dump_stack("pre-qw1")
        t_qw1 = [sb("qw1A", [128, NFC * 512], BF16),
                 sb("qw1B", [128, NFC * 512], BF16)]
        p_sw = psum("p_sw", [128, 128], F32)
        p_h = [psum("p_h0", [128, 512], F32), psum("p_h1", [128, 512], F32)]

        cp1 = load_colT(t_s1c,
                        bass.AP(d_sw1.ap().tensor, 0, [[128, 64], [1, 128]]),
                        [ag_sw1, afm_done] + qx_done_all, 64, p_sw)
        s1fin = P.emit("vector", lambda E: E.tensor_scalar(
            out=t_s1c[:], in0=t_s1c[:], scalar1=t_sx1[:], scalar2=0.0,
            op0=OP.mult, op1=OP.add), deps=[cp1, sx1])
        cpb1 = load_colT(t_b1c,
                         bass.AP(h_b1.tensor, 0, [[128, 64], [1, 128]]),
                         [cp1], 64, p_sw)

        qw1_free = [None, None]
        hst_free = [None] * 4
        ph_free = [None, None]
        hsp_last = None
        for jb2 in range(16):
            jb, hb = jb2 // 2, jb2 % 2
            bi = jb2 % 2
            dep = ([qw1_free[bi]] if qw1_free[bi] else
                   (bd("qw1A") + bd("qw1B"))) + [ag_qw1] + qx_done_all
            for fc in range(NFC):
                P.dma("sync",
                      t_qw1[bi][:, fc * 512:(fc + 1) * 512],
                      d_qw1[jb, fc, :, hb * 512:(hb + 1) * 512],
                      deps=dep if fc == 0 else [])
            qw1_bar = P.dma_bar("sync")
            last_mm = None
            for jc in range(4):
                jg = jb2 * 4 + jc
                for th in range(2):
                    pbi = (jc * 2 + th) % 2
                    mmq = None
                    for fc in range(NFC):
                        mmq = P.emit(
                            "tensor",
                            lambda E, fc=fc, jc=jc, th=th, bi=bi, pbi=pbi:
                            E.matmul(
                                out=p_h[pbi][:],
                                lhsT=t_qw1[bi][:, fc * 512 + jc * 128:
                                               fc * 512 + (jc + 1) * 128],
                                rhs=t_qxr[:, fc * TOK + th * 512:
                                          fc * TOK + (th + 1) * 512],
                                start=(fc == 0), stop=(fc == NFC - 1)),
                            deps=((qw1_bar + qxr_bar + [afm_done] +
                                   ([ph_free[pbi]] if ph_free[pbi] else []))
                                  if fc == 0 else []))
                    si = (jg * 2 + th) % 4
                    ge = P.emit("scalar", lambda E, jg=jg, si=si, pbi=pbi:
                                E.activation(
                        out=stg[si][:], in_=p_h[pbi][:], func=AF.Gelu,
                        scale=t_s1c[:, jg:jg + 1], bias=t_b1c[:, jg:jg + 1]),
                        deps=[mmq, s1fin, cpb1] +
                             ([hst_free[si]] if hst_free[si] else
                              ([stg_tok[si]] if stg_tok[si] else [])))
                    ph_free[pbi] = ge
                    ax = P.emit("vector", lambda E, jg=jg, th=th, si=si:
                                E.tensor_reduce(
                        out=t_hax[:, jg * 2 + th:jg * 2 + th + 1],
                        in_=stg[si][:], axis=AX, op=OP.max,
                        apply_absolute_value=True), deps=[ge])
                    stt = P.dma("scalar", d_h[jg, :, th * 512:(th + 1) * 512],
                                stg[si][:], deps=[ax])
                    hst_free[si] = stt
                    hsp_last = stt
                    last_mm = mmq
            qw1_free[bi] = last_mm
        stg_tok = [hst_free[0], hst_free[1], hst_free[2], hst_free[3],
                   None, None]
        hsp_bar = P.dma_bar("scalar")
        p_hxt = psum("p_hxt", [1, 128], F32)
        if stop == "M1":
            _stop_here(hsp_bar)
        hx1 = P.emit("vector", lambda E: E.tensor_reduce(
            out=t_hx1[:], in_=t_hax[:], axis=AX, op=OP.max),
            deps=[P.bar("vector")])
        hx2 = P.emit("tensor", lambda E: E.transpose(
            out=p_hxt[:], in_=t_hx1[:], identity=t_if32[:]), deps=[hx1])
        hx3 = P.emit("vector", lambda E: E.tensor_reduce(
            out=t_hxr[:], in_=p_hxt[0:1, :], axis=AX, op=OP.max), deps=[hx2])
        hx4 = P.dma("sync", d_ar2i[:, :], t_hxr[0:1, 0:1], deps=[hx3])
        ar2 = P.collective("AllReduce", OP.max, [d_ar2i.ap().opt()],
                           [d_ar2o.ap().opt()], deps=[hx4])
        free("p_hxt", "p_h1", "p_h0", "p_sw")
        free("qw1B", "qw1A", "qxr", token=ar2)

        t_wor = sb("wor", [128, NFC * 512], BF16)
        t_wos = sb("wos", [128, NFC * 1024], BF16)
        p_o = psum("p_o", [128, 8 * 512], F32)
        po_ev = {}
        wo_og_mm = {}
        for og in range(2):
            wcasts = []
            for dc in range(NFC):
                bi = dc % 2
                dep = list(bstg_tok[bi])
                if og == 1:
                    dep = dep + [wo_og_mm[(0, dc)]]
                ldw = P.dma("sync", bstg[bi][:, 0:1024],
                            h_wo[dc * 128:(dc + 1) * 128,
                                 og * 1024:(og + 1) * 1024],
                            deps=dep + (bd("wos") if og == 0 and dc < 2
                                        else []))
                wcasts.append(P.emit("scalar", lambda E, bi=bi, dc=dc:
                                     E.copy(
                    out=t_wos[:, dc * 1024:(dc + 1) * 1024],
                    in_=bstg[bi][:, 0:1024]), deps=[ldw]))
                bstg_tok[bi] = [wcasts[-1]]
            for tq in range(2):
                wor_ld = None
                wdep = afm_bar + bd("wor") + \
                    ([po_ev[(og, tq - 1, 7)]] if (og, tq - 1, 7) in po_ev
                     else []) + \
                    ([po_ev[(og - 1, 1, 7)]] if (og - 1, 1, 7) in po_ev and
                     tq == 0 else [])
                for dc in range(NFC):
                    wor_ld = P.dma("gpsimd",
                                   t_wor[:, dc * 512:(dc + 1) * 512],
                                   d_afm[dc, :, tq * 512:(tq + 1) * 512],
                                   deps=wdep if dc == 0 else [])
                wor_bar = P.dma_bar("gpsimd")
                mm_last = None
                for dc in range(NFC):
                    for oc8 in range(8):
                        extra = []
                        if oc8 == 0 and tq == 0:
                            extra = [wcasts[dc]]
                        if dc == 0 and oc8 == 0:
                            extra = extra + list(wor_bar)
                            pk = (og, tq - 1, 7) if tq == 1 else \
                                (og - 1, 1, 7)
                            if pk in po_ev:
                                extra.append(po_ev[pk])
                        mm_last = P.emit(
                            "tensor",
                            lambda E, dc=dc, oc8=oc8: E.matmul(
                                out=p_o[:, oc8 * 512:(oc8 + 1) * 512],
                                lhsT=t_wos[:, dc * 1024 + oc8 * 128:
                                           dc * 1024 + (oc8 + 1) * 128],
                                rhs=t_wor[:, dc * 512:(dc + 1) * 512],
                                start=(dc == 0), stop=(dc == NFC - 1)),
                            deps=extra)
                    wo_og_mm[(og, dc)] = mm_last
                for oc8 in range(8):
                    oc = og * 8 + oc8
                    po_ev[(og, tq, oc8)] = P.emit(
                        "scalar", lambda E, oc=oc, tq=tq, oc8=oc8:
                        E.copy(
                            out=t_out[:, oc * TOK + tq * 512:
                                      oc * TOK + (tq + 1) * 512],
                            in_=p_o[:, oc8 * 512:(oc8 + 1) * 512]),
                        deps=[mm_last] + bd("outfm"))
        wo_done_v = P.bar("scalar")
        if stop == "WO":
            _stop_here([wo_done_v])
        free("p_o")
        free("wos", "wor", token=wo_done_v)

        bq2 = P.dma("sync", t_amh[:],
                    bass.AP(d_ar2o.ap().tensor, 0, [[0, 128], [1, 1]]),
                    deps=[ar2])
        r2_ = P.emit("vector", lambda E: E.reciprocal(
            out=t_ch[:], in_=t_amh[:]), deps=[bq2])
        ch = P.emit("vector", lambda E: E.tensor_scalar(
            out=t_ch[:], in0=t_ch[:], scalar1=QMAX, scalar2=1e8,
            op0=OP.mult, op1=OP.min), deps=[r2_])
        sx2 = P.emit("vector", lambda E: E.tensor_scalar(
            out=t_sx2[:], in0=t_amh[:], scalar1=QINV, scalar2=QEPS,
            op0=OP.mult, op1=OP.max), deps=[bq2])

        p_sw2 = psum("p_sw2", [128, 128], F32)
        cp2 = load_colT(t_s2c,
                        bass.AP(d_sw2.ap().tensor, 0, [[128, 16], [1, 128]]),
                        [ag_sw2, cpb1, wo_done_v], 16, p_sw2)
        s2fin = P.emit("vector", lambda E: E.tensor_scalar(
            out=t_s2c[:, 0:16], in0=t_s2c[:, 0:16], scalar1=t_sx2[:],
            scalar2=0.0, op0=OP.mult, op1=OP.add), deps=[cp2, sx2])
        cpb2 = load_colT(t_b2c,
                         bass.AP(h_b2.tensor, 0, [[128, 16], [1, 128]]),
                         [cp2], 16, p_sw2)
        free("p_sw2")

        t_qh = sb("qh", [128, NIC * 512], BF16)
        t_qw2 = [sb("qw2A", [128, 4096], BF16), sb("qw2B", [128, 4096], BF16)]
        p_m2 = psum("p_m2", [128, 4096], F32)

        hre_free = [None] * 4
        qw2_free = [None, None]
        fst_free = [None, None]
        prev_hq = None
        qh_tok = {}
        ev_m2 = {}
        mm_last_og = {}
        mm_pair = {}
        sc_m2 = {}
        pend_epi = [None]

        def emit_epi(tb2, og, mm_last):
            for oc8 in range(8):
                oc = og * 8 + oc8
                fbi = oc8 % 2
                sc_ = P.emit("scalar", lambda E, oc8=oc8, oc=oc, fbi=fbi:
                             E.activation(
                    out=t_fstg[fbi][:],
                    in_=p_m2[:, oc8 * 512:(oc8 + 1) * 512],
                    func=AF.Copy, scale=t_s2c[:, oc:oc + 1]),
                    deps=[mm_last, s2fin, cpb2] +
                         ([fst_free[fbi]] if fst_free[fbi] else []))
                ev_m2[(tb2, og, oc8)] = P.emit(
                    "vector", lambda E, oc=oc, tb2=tb2, fbi=fbi:
                    E.scalar_tensor_tensor(
                        out=t_out[:, oc * TOK + tb2 * 512:
                                  oc * TOK + (tb2 + 1) * 512],
                        in0=t_fstg[fbi][:],
                        scalar=t_b2c[:, oc:oc + 1],
                        in1=t_out[:, oc * TOK + tb2 * 512:
                                  oc * TOK + (tb2 + 1) * 512],
                        op0=OP.add, op1=OP.add),
                    deps=[sc_, wo_done_v])
                fst_free[fbi] = ev_m2[(tb2, og, oc8)]
                sc_m2[(tb2, og, oc8)] = sc_

        for tb2 in range(2):
            for ic in range(NIC):
                si = ic % 4
                dep = [hre_free[si]] if hre_free[si] else \
                    ([stg_tok[si]] if stg_tok[si] else [])
                ld = P.dma("gpsimd", stg[si][:],
                           d_h[ic, :, tb2 * 512:(tb2 + 1) * 512],
                           deps=dep + (hsp_bar if ic == 0 and tb2 == 0
                                       else []))
                vv = P.emit("vector", lambda E, si=si: E.tensor_scalar(
                    out=stg[4][:], in0=stg[si][:], scalar1=t_ch[:],
                    scalar2=0.5, op0=OP.mult, op1=OP.add),
                    deps=[ld, ch] + ([prev_hq] if prev_hq else []) +
                    ([stg_tok[4]] if stg_tok[4] and ic == 0 and tb2 == 0
                     else []))
                hre_free[si] = vv
                r0 = P.emit("scalar", lambda E: E.activation(
                    out=t_i32s[:], in_=stg[4][:], func=AF.Copy), deps=[vv])
                mm_ = P.emit("vector", lambda E: E.tensor_tensor(
                    out=stg[5][:], in0=stg[4][:], in1=t_i32s[:],
                    op=OP.is_lt),
                    deps=[r0] + ([stg_tok[5]] if stg_tok[5] and ic == 0 and
                                 tb2 == 0 else []))
                qh_tok[(tb2, ic)] = P.emit("vector", lambda E, ic=ic:
                                           E.scalar_tensor_tensor(
                    out=t_qh[:, ic * 512:(ic + 1) * 512], in0=t_i32s[:],
                    scalar=0.0, in1=stg[5][:], op0=OP.add, op1=OP.subtract),
                    deps=[mm_] + (bd("qh") if ic == 0 and tb2 == 0 else [])
                    + ([mm_pair[(0, 1, ic // 2)]] if tb2 == 1 else []))
                prev_hq = qh_tok[(tb2, ic)]
            if pend_epi[0]:
                emit_epi(*pend_epi[0])
                pend_epi[0] = None
            for og in range(2):
                mm_last = None
                for ic in range(0, NIC, 4):
                    bi = (ic // 4) % 2
                    dep = ([qw2_free[bi]] if qw2_free[bi] else
                           (bd("qw2A") + bd("qw2B"))) + [ag_qw2]
                    lds = []
                    qq_ = "sync" if og == 0 else "scalar"
                    for cpart in range(4):
                        c = og * 4 + cpart
                        lds.append(P.dma(qq_,
                                         bass.AP(t_qw2[bi],
                                                 cpart * 1024,
                                                 [[4096, 128], [512, 2],
                                                  [1, 512]]),
                                         bass.AP(d_qw2.ap().tensor,
                                                 (c * (NIC // 2) +
                                                  ic // 2) * 65536,
                                                 [[512, 128], [65536, 2],
                                                  [1, 512]]),
                                         deps=dep if cpart == 0 else []))
                    for i4 in range(4):
                        ici = ic + i4
                        for oc8 in range(8):
                            oc = og * 8 + oc8
                            extra = []
                            if ici == 0 and i4 == 0 and oc8 == 0:
                                if og == 1:
                                    extra = [sc_m2[(tb2, 0, 7)]]
                                elif tb2 == 1:
                                    extra = [sc_m2[(0, 1, 7)]]
                            mm_last = P.emit(
                                "tensor",
                                lambda E, i4=i4, ici=ici, oc8=oc8, bi=bi:
                                E.matmul(
                                    out=p_m2[:, oc8 * 512:(oc8 + 1) * 512],
                                    lhsT=t_qw2[bi][
                                        :, (oc8 // 2) * 1024 +
                                        (i4 // 2) * 512 + (i4 % 2) * 256 +
                                        (oc8 % 2) * 128:
                                        (oc8 // 2) * 1024 +
                                        (i4 // 2) * 512 + (i4 % 2) * 256 +
                                        (oc8 % 2) * 128 + 128],
                                    rhs=t_qh[:, ici * 512:(ici + 1) * 512],
                                    start=(ici == 0), stop=(ici == NIC - 1)),
                                deps=(lds + [qh_tok[(tb2, ici)]] + extra)
                                if oc8 == 0 else [])
                    qw2_free[bi] = mm_last
                    mm_pair[(tb2, og, ic // 2)] = mm_last
                    mm_pair[(tb2, og, ic // 2 + 1)] = mm_last
                mm_last_og[(tb2, og)] = mm_last
                if og == 0:
                    emit_epi(tb2, 0, mm_last)
                else:
                    pend_epi[0] = (tb2, 1, mm_last)
        if pend_epi[0]:
            emit_epi(*pend_epi[0])
            pend_epi[0] = None
        m2_done = ev_m2[(1, 1, 7)]
        if stop == "M2":
            _stop_here([m2_done])
        stg_tok = [hre_free[0], hre_free[1], hre_free[2], hre_free[3],
                   prev_hq, prev_hq]
        free("p_m2")
        free("qw2B", "qw2A", "qh", token=m2_done)

        p_f = [psum("p_f0", [128, 512], F32), psum("p_f1", [128, 512], F32)]
        xre_free = [None, None]
        ost_free = [None, None]
        pf_free = [None, None]
        st_last = None
        for tt in range(NTT):
            for fg in range(4):
                bi = (tt * 4 + fg) % 2
                dep = [xre_free[bi]] if xre_free[bi] else \
                    ([stg_tok[bi]] if stg_tok[bi] else [])
                ldx = P.dma("gpsimd", stg[bi][:],
                            h_x[(4 + tt) * 128:(5 + tt) * 128,
                                fg * 512:(fg + 1) * 512], deps=dep)
                tts = []
                for k in range(4):
                    fc = fg * 4 + k
                    tts.append(P.emit("tensor",
                                      lambda E, fc=fc, k=k, tt=tt, bi=bi:
                                      E.transpose(
                        out=p_f[bi][:, k * 128:(k + 1) * 128],
                        in_=t_out[:, fc * TOK + tt * 128:
                                  fc * TOK + (tt + 1) * 128],
                        identity=t_if32[:]),
                        deps=[m2_done] +
                             ([pf_free[bi]] if pf_free[bi] and k == 0
                              else [])))
                ost = stg[2 + bi]
                odep = [ost_free[bi]] if ost_free[bi] else \
                    ([stg_tok[2 + bi]] if stg_tok[2 + bi] else [])
                ad = P.emit("vector", lambda E, bi=bi, ost=ost:
                            E.tensor_tensor(
                    out=ost[:], in0=p_f[bi][:], in1=stg[bi][:],
                    op=OP.add), deps=[tts[-1], ldx] + odep)
                pf_free[bi] = ad
                xre_free[bi] = ad
                st = P.dma("sync" if bi == 0 else "scalar",
                           h_out[tt * 128:(tt + 1) * 128,
                                 fg * 512:(fg + 1) * 512],
                           ost[:], deps=[ad])
                ost_free[bi] = st
                st_last = st
        P.wait_only("gpsimd", P.dma_bar("gpsimd"))
        P.wait_only("sync", P.dma_bar("sync"))
        P.wait_only("scalar", P.dma_bar("scalar"))
        free("p_f1", "p_f0")

      except _StopBuild:
        pass
      block = ctx.enter_context(nc.Block())
      P.run(block)
    return nc


def _host_consts(core):
    half = core % 2
    qstart = half * TOK
    pos = np.arange(qstart - WIN, qstart + TOK, dtype=np.float64)
    pos = np.clip(pos, 0, None)
    i = np.arange(64, dtype=np.float64)
    inv = 1.0 / (10000.0 ** (2 * i / 128.0))
    ang = pos[None, :] * inv[:, None]
    cos = np.cos(ang)
    sin = np.sin(ang)
    cos_s = np.concatenate([cos, cos], 0) * C8
    sin_s = np.concatenate([-sin, sin], 0) * C8
    masks = np.zeros((N_MASKS, 128, 128), np.float32)
    r = np.arange(128)
    c = np.arange(128)
    for (qt, s), mi in MASK_INDEX.items():
        rg = qstart + qt * 128 + r[:, None]
        cg = qstart - WIN + (qt + s) * 128 + c[None, :]
        allowed = (cg >= 0) & (cg <= rg) & (cg >= rg - (WIN - 1))
        masks[mi] = np.where(allowed, 0.0, -30000.0)
    return (cos_s.astype(np.float32), sin_s.astype(np.float32),
            masks.astype(ml_dtypes.bfloat16))


def kernel(L0_attn_x, L0_attn_Wq, L0_attn_Wk, L0_attn_Wv, L0_attn_Wo,
           L0_attn_lam, L0_ffn_w1, L0_ffn_b1, L0_ffn_w2, L0_ffn_b2):
    x = np.ascontiguousarray(np.asarray(L0_attn_x, dtype=np.float32))
    Wq = np.ascontiguousarray(np.asarray(L0_attn_Wq, dtype=np.float32))
    Wk = np.ascontiguousarray(np.asarray(L0_attn_Wk, dtype=np.float32))
    Wv = np.ascontiguousarray(np.asarray(L0_attn_Wv, dtype=np.float32))
    Wo = np.ascontiguousarray(np.asarray(L0_attn_Wo, dtype=np.float32))
    lam = np.ascontiguousarray(np.asarray(L0_attn_lam, dtype=np.float32)
                               .reshape(1, NH))
    w1 = np.ascontiguousarray(np.asarray(L0_ffn_w1, dtype=np.float32))
    b1 = np.ascontiguousarray(np.asarray(L0_ffn_b1, dtype=np.float32)
                              .reshape(1, I_DIM))
    w2 = np.ascontiguousarray(np.asarray(L0_ffn_w2, dtype=np.float32))
    b2 = np.ascontiguousarray(np.asarray(L0_ffn_b2, dtype=np.float32)
                              .reshape(1, D))
    ident_bf = np.eye(128, dtype=ml_dtypes.bfloat16)
    ident_f32 = np.eye(128, dtype=np.float32)
    rotP = np.zeros((128, 128), np.float32)
    for d in range(128):
        rotP[(d + 64) % 128, d] = 1.0
    rotP = rotP.astype(ml_dtypes.bfloat16)

    nc = build_nc()
    in_maps = []
    for core in range(NCORE):
        b, half = core // 2, core % 2
        qstart = half * TOK
        x_ext = np.zeros((EXT, D), np.float32)
        lo = qstart - WIN
        src_lo = max(lo, 0)
        x_ext[src_lo - lo:, :] = x[b, src_lo:qstart + TOK, :]
        cos_s, sin_s, masks = _host_consts(core)
        in_maps.append({
            "x_ext": x_ext, "Wq": Wq, "Wk": Wk, "Wv": Wv, "Wo": Wo,
            "lam": lam,
            "w1s": w1[core * 1024:(core + 1) * 1024, :],
            "b1": b1,
            "w2s": w2[core * 256:(core + 1) * 256, :],
            "b2": b2,
            "cos_s": cos_s, "sin_s": sin_s, "masks": masks,
            "ident_bf": ident_bf, "ident_f32": ident_f32, "rotP": rotP,
        })
    res = run_bass_kernel_spmd(nc, in_maps, core_ids=list(range(NCORE)))
    out = np.empty((BATCH, SEQ, D), np.float32)
    for core in range(NCORE):
        b, half = core // 2, core % 2
        out[b, half * TOK:(half + 1) * TOK, :] = res.results[core]["out"]
    return out


# version 18
# speedup vs baseline: 1.0568x; 1.0266x over previous
import numpy as np
import ml_dtypes
import concourse.bass as bass
import concourse.mybir as mybir
from concourse.bass_utils import run_bass_kernel_spmd
from contextlib import ExitStack

F32 = mybir.dt.float32
BF16 = mybir.dt.bfloat16
I32 = mybir.dt.int32
AX = mybir.AxisListType.X
OP = mybir.AluOpType
AF = mybir.ActivationFunctionType

NCORE = 8
D = 2048
SEQ = 2048
BATCH = 4
NH = 16
NKV = 4
WIN = 512
TOK = 1024
EXT = 1536
NTT = 8
NET = 12
NFC = 16
I_DIM = 8192
NIC = 64
EPS_LN = 1e-5
QMAX = 127.0
QINV = float(np.float32(1.0) / np.float32(127.0))
QEPS = 1e-8
C8 = float(1.0 / np.sqrt(8.0))
ENGS = ("tensor", "vector", "scalar", "gpsimd", "sync")


def mask_slots(qt):
    return [0] + [m for m in (1, 2, 3) if m < 4 - qt] + [4]


MASK_INDEX = {}
_mi = 0
for _qt in range(8):
    for _s in mask_slots(_qt):
        MASK_INDEX[(_qt, _s)] = _mi
        _mi += 1
N_MASKS = _mi


class Planner:
    NCH = 16

    def __init__(self, nc, ctx):
        self.nc = nc
        self.q = {e: [] for e in ENGS}
        self.sem = {}
        self.cnt = {}
        for e in ENGS:
            self.sem[e] = ctx.enter_context(nc.semaphore("sem_" + e))
            self.cnt[e] = 0
            for ch in range(self.NCH):
                key = f"dma_{e}{ch}"
                self.sem[key] = ctx.enter_context(
                    nc.semaphore(f"dsem_{e}{ch}"))
                self.cnt[key] = 0
        self.dpos = {e: 0 for e in ENGS}
        self.water = {e: {} for e in ENGS}

    def _wait(self, eng, dep):
        semname, val = dep
        if val <= 0:
            return
        w = self.water[eng]
        if w.get(semname, 0) >= val:
            return
        w[semname] = val
        sem = self.sem[semname]
        self.q[eng].append(lambda E, s=sem, v=val: E.wait_ge(s, v))

    def emit(self, eng, fn, deps=(), inc=1):
        for d in deps:
            self._wait(eng, d)
        sem = self.sem[eng]
        self.q[eng].append(lambda E, f=fn, s=sem, i=inc: f(E).then_inc(s, i))
        self.cnt[eng] += inc
        return (eng, self.cnt[eng])

    def dma(self, eng, out, in_, deps=()):
        n = self.dpos[eng]
        self.dpos[eng] += 1
        key = f"dma_{eng}{n % self.NCH}"
        for d in deps:
            self._wait(eng, d)
        self._wait(eng, (key, self.cnt[key]))
        sem = self.sem[key]
        self.q[eng].append(lambda E, o=out, i=in_, s=sem:
                           E.dma_start(out=o, in_=i).then_inc(s, 16))
        self.cnt[key] += 16
        return (key, self.cnt[key])

    def collective(self, kind, op, ins, outs, deps=()):
        eng = "gpsimd"
        for d in deps:
            self._wait(eng, d)
        sem = self.sem[eng]
        self.q[eng].append(
            lambda E, k=kind, o=op, I=ins, O=outs, s=sem:
            E.collective_compute(k, o, replica_groups=[list(range(NCORE))],
                                 ins=I, outs=O).then_inc(s, 1))
        self.cnt[eng] += 1
        return (eng, self.cnt[eng])

    def wait_only(self, eng, deps):
        for d in deps:
            self._wait(eng, d)

    def bar(self, eng):
        return (eng, self.cnt[eng])

    def dma_bar(self, eng):
        return [(f"dma_{eng}{ch}", self.cnt[f"dma_{eng}{ch}"])
                for ch in range(self.NCH)]

    def run(self, block):
        def mk(name):
            def body(E):
                for fn in self.q[name]:
                    fn(E)
            return body
        for name in ENGS:
            getattr(block, name)(mk(name))


class _StopBuild(Exception):
    pass


def build_nc(stop=None):
    import os
    stop = stop or os.environ.get("KSTOP") or None
    nc = bass.Bass("TRN2", target_bir_lowering=False)

    h_x = nc.dram_tensor("x_ext", [EXT, D], F32, kind="ExternalInput").ap()
    h_wq = nc.dram_tensor("Wq", [D, D], F32, kind="ExternalInput").ap()
    h_wk = nc.dram_tensor("Wk", [D, 512], F32, kind="ExternalInput").ap()
    h_wv = nc.dram_tensor("Wv", [D, 512], F32, kind="ExternalInput").ap()
    h_wo = nc.dram_tensor("Wo", [D, D], F32, kind="ExternalInput").ap()
    h_lam = nc.dram_tensor("lam", [1, NH], F32, kind="ExternalInput").ap()
    h_w1s = nc.dram_tensor("w1s", [1024, D], F32, kind="ExternalInput").ap()
    h_b1 = nc.dram_tensor("b1", [1, I_DIM], F32, kind="ExternalInput").ap()
    h_w2s = nc.dram_tensor("w2s", [256, I_DIM], F32, kind="ExternalInput").ap()
    h_b2 = nc.dram_tensor("b2", [1, D], F32, kind="ExternalInput").ap()
    h_cos = nc.dram_tensor("cos_s", [128, EXT], F32, kind="ExternalInput").ap()
    h_sin = nc.dram_tensor("sin_s", [128, EXT], F32, kind="ExternalInput").ap()
    h_msk = nc.dram_tensor("masks", [N_MASKS, 128, 128], BF16,
                           kind="ExternalInput").ap()
    h_ibf = nc.dram_tensor("ident_bf", [128, 128], BF16,
                           kind="ExternalInput").ap()
    h_if32 = nc.dram_tensor("ident_f32", [128, 128], F32,
                            kind="ExternalInput").ap()
    h_p = nc.dram_tensor("rotP", [128, 128], BF16, kind="ExternalInput").ap()
    h_out = nc.dram_tensor("out", [TOK, D], F32, kind="ExternalOutput").ap()

    d_qw1sh = nc.dram_tensor("qw1sh", [NFC, 128, 1024], BF16)
    d_qw1 = nc.dram_tensor("qw1full", [NCORE, NFC, 128, 1024], BF16)
    d_qw2sh = nc.dram_tensor("qw2sh", [NIC // 2, 128, 512], BF16)
    d_qw2 = nc.dram_tensor("qw2full", [NCORE, NIC // 2, 128, 512], BF16)
    d_sw1sh = nc.dram_tensor("sw1sh", [1, 1024], F32)
    d_sw1 = nc.dram_tensor("sw1full", [NCORE, 1024], F32)
    d_sw2sh = nc.dram_tensor("sw2sh", [1, 256], F32)
    d_sw2 = nc.dram_tensor("sw2full", [NCORE, 256], F32)
    d_ar1i = nc.dram_tensor("ar1i", [1, 1], F32)
    d_ar1o = nc.dram_tensor("ar1o", [1, 1], F32)
    d_ar2i = nc.dram_tensor("ar2i", [1, 1], F32)
    d_ar2o = nc.dram_tensor("ar2o", [1, 1], F32)

    d_h = nc.dram_tensor("h_spill", [NIC, 128, TOK], F32)
    d_xn = nc.dram_tensor("xn_spill", [NTT, 128, D], F32)
    d_qx = nc.dram_tensor("qx_spill", [NFC, 128, TOK], BF16)
    d_wqbf = nc.dram_tensor("wqbf", [D, D], BF16)
    d_afm = nc.dram_tensor("afm_spill", [NH, 128, TOK], BF16)

    with ExitStack() as ctx:
      try:
        P = Planner(nc, ctx)
        import os as _os
        stack = []
        birth = {}
        tok_pool = []
        live_psum = {}

        def sb(name, shape, dt=F32):
            cm = nc.sbuf_tensor(name, shape, dt)
            h = cm.__enter__()
            stack.append([name, cm, False])
            birth[name] = list(tok_pool)
            if _os.environ.get("KBUILD_DEBUG"):
                print(f"[sb] {name} remaining={nc.sbuf_bytes_remaining}")
            return h

        def bd(name):
            return birth.get(name, [])

        import os as _os
        def _dump_stack(tag):
            if _os.environ.get("KBUILD_DEBUG"):
                alive = [(e[0], e[2]) for e in stack]
                print(f"[{tag}] stack:", alive)

        def free(*names, token=None):
            for n in names:
                if n in live_psum:
                    live_psum.pop(n).__exit__(None, None, None)
                    continue
                for ent in stack:
                    if ent[0] == n:
                        ent[2] = True
                        break
                else:
                    raise KeyError(n)
            if token is not None:
                tok_pool.append(token)
            while stack and stack[-1][2]:
                nm, cm, _ = stack.pop()
                cm.__exit__(None, None, None)
                if _os.environ.get("KBUILD_DEBUG"):
                    print(f"[pop] {nm} remaining={nc.sbuf_bytes_remaining}")

        def psum(name, shape, dt=F32):
            cm = nc.psum_tensor(name, shape, dt)
            h = cm.__enter__()
            live_psum[name] = cm
            return h

        t_ibf = sb("ibf", [128, 128], BF16)
        t_if32 = sb("if32", [128, 128], F32)
        t_p = sb("rotp", [128, 128], BF16)
        t_lam1 = sb("lam1", [128, NH], F32)
        t_lamn = sb("lamn", [128, NH], F32)
        t_wax = sb("w_ax", [128, 8], F32)
        t_wc = sb("w_c", [128, 8], F32)
        t_sw1 = sb("sw1loc", [128, 8], F32)
        t_w2ax = sb("w2ax", [128, 8], F32)
        t_w2axf = sb("w2axf", [128, 2], F32)
        t_w2c = sb("w2c", [128, 2], F32)
        t_sw2 = sb("sw2loc", [128, 2], F32)
        t_eps = sb("epsb", [128, 1], F32)
        t_ve = sb("lnve", [128, 1], F32)
        t_nw = sb("lnnw", [128, 1], F32)
        t_bs = sb("bnstat", [128, 24], F32)
        t_bo = sb("bnout", [128, 2], F32)
        t_rstd = sb("rstd", [128, 1], F32)
        t_ax = sb("axn", [128, NTT], F32)
        t_ax1 = sb("ax1", [128, 1], F32)
        t_axr = sb("axr", [1, 1], F32)
        t_amx = sb("amx", [128, 1], F32)
        t_cx = sb("cx", [128, 1], F32)
        t_sx1 = sb("sx1", [128, 1], F32)
        t_ch = sb("chq", [128, 1], F32)
        t_amh = sb("amh", [128, 1], F32)
        t_sx2 = sb("sx2", [128, 1], F32)
        t_st = sb("stats", [128, 2 * NH], F32)
        t_rr = sb("rstats", [128, 2 * NH], F32)
        t_mn = sb("mneg", [128, NH], F32)
        t_bneg = sb("bneg", [128, NH], F32)
        t_uu = sb("unorm", [128, NH], F32)
        t_ww = sb("wnorm", [128, NH], F32)
        t_ds = sb("dsum", [128, NH], F32)
        t_hax = sb("hax", [128, 128], F32)
        t_hx1 = sb("hx1", [128, 1], F32)
        t_hxr = sb("hxr", [1, 1], F32)
        t_swld = sb("swld", [64, 128], F32)
        t_s1c = sb("s1c", [128, 64], F32)
        t_b1c = sb("b1c", [128, 64], F32)
        t_s2c = sb("s2c", [128, 16], F32)
        t_b2c = sb("b2c", [128, 16], F32)
        stg = [sb(f"stg{i}", [128, 512], F32) for i in range(6)]
        bstg = [sb(f"bstg{i}", [128, D], F32) for i in range(2)]
        t_i32s = sb("i32stg", [128, 512], I32)
        t_fstg = [sb("fstgA", [128, 512], F32), sb("fstgB", [128, 512], F32)]
        t_kraw = sb("kraw", [128, 512], BF16)
        t_krot = sb("krot", [128, 512], BF16)
        t_u1 = sb("ropeu", [128, 512], F32)
        t_u2 = sb("ropev", [128, 512], F32)
        t_qstg = [sb("qstgA", [128, 512], BF16), sb("qstgB", [128, 512], BF16)]
        t_msk = sb("msk", [128, N_MASKS * 128], BF16)
        t_cos = sb("cosb", [128, EXT], F32)
        t_sin = sb("sinb", [128, EXT], F32)
        t_vtm = sb("vtm", [128, NET * 512], BF16)
        t_kfm = sb("kfm", [128, NKV * EXT], BF16)
        t_qrp = sb("qrp", [128, NH * TOK], BF16)
        t_xnfm = sb("xnfm", [128, NFC * EXT], BF16)

        dep_ibf = P.dma("sync", t_ibf[:], h_ibf[:, :])
        dep_if32 = P.dma("sync", t_if32[:], h_if32[:, :])
        dep_p = P.dma("sync", t_p[:], h_p[:, :])
        dep_msk = None
        for i in range(N_MASKS):
            dep_msk = P.dma("sync", t_msk[:, i * 128:(i + 1) * 128],
                            h_msk[i, :, :])
        dep_cos = P.dma("sync", t_cos[:], h_cos[:, :])
        dep_sin = P.dma("sync", t_sin[:], h_sin[:, :])
        dep_lam = P.dma("sync", t_lam1[:],
                        bass.AP(h_lam.tensor, 0, [[0, 128], [1, NH]]))
        consts_bar = P.dma_bar("sync")
        dep_msk = consts_bar[-1] if False else dep_msk
        dep_lamn = P.emit("vector", lambda E: E.tensor_single_scalar(
            out=t_lamn[:], in_=t_lam1[:], scalar=-1.0, op=OP.mult),
            deps=[dep_lam])
        dep_eps = P.emit("vector", lambda E: E.memset(t_eps[:], EPS_LN))

        def _stop_here(tokens):
            st = P.dma("gpsimd", h_out[0:128, 0:512], stg[0][:],
                       deps=list(tokens))
            P.wait_only("gpsimd", [st])
            raise _StopBuild()

        try:
            pass
        except Exception:
            pass
        t_wkbf = sb("wkbf", [128, NFC * 512], BF16)
        t_wvbf = sb("wvbf", [128, NFC * 512], BF16)
        wfree = [None, None]
        nxt = 0
        for fc in range(NFC):
            for (hsrc, dst) in ((h_wk, t_wkbf), (h_wv, t_wvbf)):
                bi = nxt % 2
                nxt += 1
                buf = stg[bi]
                dep = [wfree[bi]] if wfree[bi] else []
                ld = P.dma("scalar", buf[:], hsrc[fc * 128:(fc + 1) * 128, :],
                           deps=dep)
                ct = P.emit("scalar", lambda E, b=buf, d=dst, fc=fc: E.copy(
                    out=d[:, fc * 512:(fc + 1) * 512], in_=b[:]), deps=[ld])
                wfree[bi] = ct
        wkv_done = P.bar("scalar")
        stg_tok = [wfree[0], wfree[1], None, None, None, None]

        t_wqc = [sb("wqcA", [128, 512], BF16), sb("wqcB", [128, 512], BF16),
                 sb("wqcC", [128, 512], BF16), sb("wqcD", [128, 512], BF16)]

        p_wt = [psum("p_wt", [128, 512], BF16),
                psum("p_wt2", [128, 512], BF16)]
        qstg_tok = [[], []]
        bstg_tok = [[], []]
        t_i32b = sb("i32stgB", [128, 512], I32)
        qs = {"q": 0, "r0": [None, None], "q8": [None, None],
              "tp": [None, None], "ev": [None, None], "pend": []}

        def _q_stage2(ent):
            (qq, buf, c_col, dst_fn, tile0) = ent
            qb = qq % 2
            vbuf = stg[2] if qb == 0 else t_fstg[0]
            mbuf = stg[3] if qb == 0 else t_fstg[1]
            ibuf = t_i32s if qb == 0 else t_i32b
            kbuf = t_kraw if qb == 0 else t_krot
            pwt = p_wt[qb]
            r0 = P.emit("scalar", lambda E, ib=ibuf, vb=vbuf: E.activation(
                out=ib[:], in_=vb[:], func=AF.Copy),
                deps=[qs["vv"][qb]] +
                     ([qs["q8"][qb]] if qs["q8"][qb] else []))
            qs["r0"][qb] = r0
            mm_ = P.emit("vector", lambda E, mb=mbuf, vb=vbuf, ib=ibuf:
                         E.tensor_tensor(
                out=mb[:], in0=vb[:], in1=ib[:], op=OP.is_lt),
                deps=[r0] + ([stg_tok[3]] if stg_tok[3] and qq < 2 else []))
            q8 = P.emit("vector", lambda E, kb=kbuf, ib=ibuf, mb=mbuf:
                        E.scalar_tensor_tensor(
                out=kb[:], in0=ib[:], scalar=0.0, in1=mb[:],
                op0=OP.add, op1=OP.subtract),
                deps=[mm_] + ([qs["tp"][qb]] if qs["tp"][qb] else []))
            qs["q8"][qb] = q8
            gb = qq % 2
            tts = None
            for k in range(4):
                tts = P.emit("tensor", lambda E, k=k, kb=kbuf, pw=pwt:
                             E.transpose(
                    out=pw[:, k * 128:(k + 1) * 128],
                    in_=kb[:, k * 128:(k + 1) * 128],
                    identity=t_ibf[:]),
                    deps=[q8, dep_ibf] +
                         ([qs["ev"][qb]] if qs["ev"][qb] and k == 0 else []))
            qs["tp"][qb] = tts
            ev = P.emit("scalar", lambda E, gb=gb, pw=pwt: E.copy(
                out=t_qstg[gb][:], in_=pw[:, 0:512]),
                deps=[tts] + qstg_tok[gb])
            qs["ev"][qb] = ev
            qstg_tok[gb] = [P.dma("scalar", dfn,
                                  t_qstg[gb][:, c0:c1], deps=[ev])
                            for (dfn, c0, c1) in dst_fn(tile0)]

        qs["vv"] = [None, None]

        def quant_slice(buf, c_col, dst_fn, qoff):
            last_vv = None
            for qtr in range(4):
                qq = qs["q"]
                qs["q"] = qq + 1
                qb = qq % 2
                vbuf = stg[2] if qb == 0 else t_fstg[0]
                vv = P.emit("vector", lambda E, b=buf, qtr=qtr, c=c_col,
                            vb=vbuf: E.tensor_scalar(
                    out=vb[:], in0=b[:, qtr * 512:(qtr + 1) * 512],
                    scalar1=c, scalar2=0.5, op0=OP.mult, op1=OP.add),
                    deps=([qs["r0"][qb]] if qs["r0"][qb] else []) +
                         ([stg_tok[2]] if stg_tok[2] and qq < 2 else []))
                qs["vv"][qb] = vv
                last_vv = vv
                qs["pend"].append((qq, buf, c_col, dst_fn, qoff + qtr * 4))
                if len(qs["pend"]) >= 2:
                    _q_stage2(qs["pend"].pop(0))
            return last_vv

        def quant_flush():
            while qs["pend"]:
                _q_stage2(qs["pend"].pop(0))

        wq_fpool = [t_u1, t_u2, stg[4], stg[5]]
        _wqst = {"ch": 0, "fq": [None] * 4, "cq": [None] * 4}

        def emit_wq_cast(n):
            for _ in range(n):
                ch = _wqst["ch"]
                if ch >= 64:
                    return
                _wqst["ch"] = ch + 1
                r, c4 = divmod(ch, 4)
                fi = ch % 4
                bo = ch % 4
                fbuf = wq_fpool[fi]
                ld = P.dma("gpsimd", fbuf[:],
                           h_wq[r * 128:(r + 1) * 128,
                                c4 * 512:(c4 + 1) * 512],
                           deps=[_wqst["fq"][fi]] if _wqst["fq"][fi] else [])
                ct = P.emit("vector", lambda E, fb=fbuf, bo=bo:
                            E.tensor_copy(
                    out=t_wqc[bo][:], in_=fb[:]),
                    deps=[ld] + ([_wqst["cq"][bo]] if _wqst["cq"][bo] else
                                 bd("wqcA") + bd("wqcB") + bd("wqcC") +
                                 bd("wqcD")))
                _wqst["fq"][fi] = ct
                st = P.dma("gpsimd",
                           d_wqbf[r * 128:(r + 1) * 128,
                                  c4 * 512:(c4 + 1) * 512],
                           t_wqc[bo][:], deps=[ct])
                _wqst["cq"][bo] = st

        for pt in range(8):
            bi = pt % 2
            buf = bstg[bi]
            dep = list(bstg_tok[bi])
            ld = P.dma("sync", buf[:], h_w1s[pt * 128:(pt + 1) * 128, :],
                       deps=dep)
            amax = P.emit("vector", lambda E, b=buf, pt=pt: E.tensor_reduce(
                out=t_wax[:, pt:pt + 1], in_=b[:], axis=AX, op=OP.max,
                apply_absolute_value=True), deps=[ld])
            rec = P.emit("vector", lambda E, pt=pt: E.reciprocal(
                out=t_wc[:, pt:pt + 1], in_=t_wax[:, pt:pt + 1]), deps=[amax])
            cc = P.emit("vector", lambda E, pt=pt: E.tensor_scalar(
                out=t_wc[:, pt:pt + 1], in0=t_wc[:, pt:pt + 1], scalar1=QMAX,
                scalar2=1e8, op0=OP.mult, op1=OP.min), deps=[rec])
            P.emit("vector", lambda E, pt=pt: E.tensor_scalar(
                out=t_sw1[:, pt:pt + 1], in0=t_wax[:, pt:pt + 1], scalar1=QINV,
                scalar2=QEPS, op0=OP.mult, op1=OP.max), deps=[amax])
            P.wait_only("vector", [cc])
            bstg_tok[bi] = [quant_slice(
                buf, t_wc[:, pt:pt + 1],
                lambda fc0, pt=pt: [(bass.AP(
                    d_qw1sh.ap().tensor, fc0 * 128 * 1024 + pt * 128,
                    [[1024, 128], [128 * 1024, 4], [1, 128]]), 0, 512)],
                0)]
            emit_wq_cast(5)
        quant_flush()
        sh1 = P.dma_bar("scalar")
        p_sws = psum("p_sws", [128, 128], F32)
        tr1 = P.emit("tensor", lambda E: E.transpose(
            out=p_sws[0:8, :], in_=t_sw1[:, 0:8], identity=t_if32[:]),
            deps=[P.bar("vector"), dep_if32])
        ev1 = P.emit("vector", lambda E: E.tensor_copy(
            out=t_swld[0:8, :], in_=p_sws[0:8, :]), deps=[tr1])
        sw1st = P.dma("sync",
                      bass.AP(d_sw1sh.ap().tensor, 0, [[128, 8], [1, 128]]),
                      t_swld[0:8, :], deps=[ev1])

        for pt in range(2):
            for cc2 in range(4):
                bi = cc2 % 2
                buf = bstg[bi]
                dep = list(bstg_tok[bi])
                ld = P.dma("sync", buf[:],
                           h_w2s[pt * 128:(pt + 1) * 128,
                                 cc2 * 2048:(cc2 + 1) * 2048], deps=dep)
                am = P.emit("vector", lambda E, b=buf, pt=pt, cc2=cc2:
                            E.tensor_reduce(
                    out=t_w2ax[:, pt * 4 + cc2:pt * 4 + cc2 + 1],
                    in_=b[:], axis=AX, op=OP.max,
                    apply_absolute_value=True), deps=[ld])
                bstg_tok[bi] = [am]
                emit_wq_cast(2)
            amf = P.emit("vector", lambda E, pt=pt: E.tensor_reduce(
                out=t_w2axf[:, pt:pt + 1], in_=t_w2ax[:, pt * 4:pt * 4 + 4],
                axis=AX, op=OP.max), deps=bstg_tok[0] + bstg_tok[1])
            rec = P.emit("vector", lambda E, pt=pt: E.reciprocal(
                out=t_w2c[:, pt:pt + 1], in_=t_w2axf[:, pt:pt + 1]),
                deps=[amf])
            P.emit("vector", lambda E, pt=pt: E.tensor_scalar(
                out=t_w2c[:, pt:pt + 1], in0=t_w2c[:, pt:pt + 1], scalar1=QMAX,
                scalar2=1e8, op0=OP.mult, op1=OP.min), deps=[rec])
            P.emit("vector", lambda E, pt=pt: E.tensor_scalar(
                out=t_sw2[:, pt:pt + 1], in0=t_w2axf[:, pt:pt + 1],
                scalar1=QINV, scalar2=QEPS, op0=OP.mult, op1=OP.max),
                deps=[amf])
        c2done = P.bar("vector")
        tr2 = P.emit("tensor", lambda E: E.transpose(
            out=p_sws[0:2, :], in_=t_sw2[:, 0:2], identity=t_if32[:]),
            deps=[c2done, dep_if32, ev1])
        ev2 = P.emit("vector", lambda E: E.tensor_copy(
            out=t_swld[32:34, :], in_=p_sws[0:2, :]), deps=[tr2, ev1])
        sw2st = P.dma("sync",
                      bass.AP(d_sw2sh.ap().tensor, 0, [[128, 2], [1, 128]]),
                      t_swld[32:34, :], deps=[ev2])
        for cc2 in range(4):
            for pt in range(2):
                bi = pt % 2
                buf = bstg[bi]
                dep = list(bstg_tok[bi])
                ld = P.dma("sync", buf[:],
                           h_w2s[pt * 128:(pt + 1) * 128,
                                 cc2 * 2048:(cc2 + 1) * 2048],
                           deps=dep + [c2done])
                P.wait_only("vector", [ld])
                bstg_tok[bi] = [quant_slice(
                    buf, t_w2c[:, pt:pt + 1],
                    lambda icl0, pt=pt: [
                        (bass.AP(d_qw2sh.ap().tensor,
                                 (icl0 // 2 + hi) * 128 * 512 + pt * 128,
                                 [[512, 128], [256, 2], [1, 128]]),
                         hi * 256, hi * 256 + 256) for hi in range(2)],
                    cc2 * 16)]
                emit_wq_cast(1)
        sh2_done = None
        quant_flush()
        free("i32stgB", token=qs["q8"][1])
        sh2_bar = P.dma_bar("scalar")
        if stop == "W":
            _stop_here(sh2_bar)
        ag_sw1 = P.collective("AllGather", OP.bypass, [d_sw1sh.ap().opt()],
                              [d_sw1.ap().opt()], deps=[sw1st])
        ag_sw2 = P.collective("AllGather", OP.bypass, [d_sw2sh.ap().opt()],
                              [d_sw2.ap().opt()], deps=[sw2st])

        emit_wq_cast(64)
        wq_sts_bar = P.dma_bar("gpsimd")
        free("wqcD", token=_wqst["cq"][3])
        free("wqcC", token=_wqst["cq"][2])
        free("wqcB", token=_wqst["cq"][1])
        free("wqcA", token=_wqst["cq"][0])

        p_tr = psum("p_tr", [128, 512], F32)

        prev_ln = None
        prev_sq = None
        prev_trx = None
        axd = []
        xnsp = {}
        for tt in range(NET):
            bi = tt % 2
            buf = bstg[bi]
            dep = list(bstg_tok[bi])
            ld = P.dma("sync", buf[:], h_x[tt * 128:(tt + 1) * 128, :],
                       deps=dep)
            bns = None
            for chk in range(4):
                bns = P.emit("vector", lambda E, b=buf, chk=chk: E.bn_stats(
                    out=t_bs[:, chk * 6:(chk + 1) * 6],
                    in_=b[:, chk * 512:(chk + 1) * 512]),
                    deps=[ld] + ([prev_ln] if prev_ln and chk == 0 else []))
            ag = P.emit("vector", lambda E: E.bn_aggr(
                out=t_bo[:], in_=t_bs[:]),
                deps=[bns] + ([prev_sq] if prev_sq else []))
            sq = P.emit("scalar", lambda E: E.activation(
                out=t_rstd[:], in_=t_bo[:, 1:2], func=AF.Sqrt,
                bias=t_eps[:]), deps=[ag, dep_eps])
            prev_sq = sq
            rc0 = P.emit("vector", lambda E: E.reciprocal(
                out=t_rstd[:], in_=t_rstd[:]), deps=[sq])
            ve = P.emit("vector", lambda E: E.tensor_scalar(
                out=t_ve[:], in0=t_bo[:, 1:2], scalar1=t_eps[:], scalar2=0.0,
                op0=OP.add, op1=OP.add), deps=[ag, dep_eps])
            n1 = P.emit("vector", lambda E: E.tensor_tensor(
                out=t_nw[:], in0=t_rstd[:], in1=t_rstd[:], op=OP.mult),
                deps=[rc0])
            n2 = P.emit("vector", lambda E: E.tensor_tensor(
                out=t_nw[:], in0=t_nw[:], in1=t_ve[:], op=OP.mult),
                deps=[n1, ve])
            n3 = P.emit("vector", lambda E: E.tensor_scalar(
                out=t_nw[:], in0=t_nw[:], scalar1=-0.5, scalar2=1.5,
                op0=OP.mult, op1=OP.add), deps=[n2])
            rc = P.emit("vector", lambda E: E.tensor_tensor(
                out=t_rstd[:], in0=t_rstd[:], in1=t_nw[:], op=OP.mult),
                deps=[n3])
            own = tt >= 4
            xnt = P.emit("vector", lambda E, b=buf: E.tensor_scalar(
                out=b[:], in0=b[:], scalar1=t_bo[:, 0:1],
                scalar2=t_rstd[:], op0=OP.subtract, op1=OP.mult),
                deps=[rc])
            prev_ln = xnt
            if own:
                axd.append(P.emit("vector", lambda E, b=buf, tt=tt:
                                  E.tensor_reduce(
                    out=t_ax[:, tt - 4:tt - 3], in_=b[:], axis=AX, op=OP.max,
                    apply_absolute_value=True), deps=[xnt]))
                xnsp[tt - 4] = P.dma("sync", d_xn[tt - 4, :, :], buf[:],
                                     deps=[axd[-1]])
            for g in range(4):
                tts = []
                for k in range(4):
                    fc = g * 4 + k
                    tts.append(P.emit("tensor", lambda E, b=buf, fc=fc, k=k:
                                      E.transpose(
                        out=p_tr[:, k * 128:(k + 1) * 128],
                        in_=b[:, fc * 128:(fc + 1) * 128],
                        identity=t_if32[:]),
                        deps=[xnt, dep_if32] +
                             ([prev_trx] if prev_trx else [])))

                def ev_xn(E, g=g, tt=tt):
                    return E.copy(
                        out=bass.AP(t_xnfm, g * 4 * EXT + tt * 128,
                                    [[NFC * EXT, 128], [EXT, 4], [1, 128]]),
                        in_=p_tr[:, 0:512])
                prev_trx = P.emit("scalar", ev_xn, deps=[tts[-1]])
            bstg_tok[bi] = [prev_trx] + ([xnsp[tt - 4]] if own else [])
        xn_fm_done = prev_trx
        p_axt = psum("p_axt", [1, 128], F32)
        a1 = P.emit("vector", lambda E: E.tensor_reduce(
            out=t_ax1[:], in_=t_ax[:], axis=AX, op=OP.max), deps=axd)
        a2 = P.emit("tensor", lambda E: E.transpose(
            out=p_axt[:], in_=t_ax1[:], identity=t_if32[:]),
            deps=[a1, dep_if32])
        a3 = P.emit("vector", lambda E: E.tensor_reduce(
            out=t_axr[:], in_=p_axt[0:1, :], axis=AX, op=OP.max), deps=[a2])
        a4 = P.dma("sync", d_ar1i[:, :], t_axr[0:1, 0:1], deps=[a3])
        ar1 = P.collective("AllReduce", OP.max, [d_ar1i.ap().opt()],
                           [d_ar1o.ap().opt()], deps=[a4])
        free("p_axt", "p_tr", "p_sws", "p_wt2", "p_wt")
        if stop == "L":
            _stop_here([ar1])

        p_v = [psum("p_v0", [128, 512], F32), psum("p_v1", [128, 512], F32)]
        vev = [None, None]
        for tt in range(NET):
            bi = tt % 2
            mmq = None
            for fc in range(NFC):
                mmq = P.emit("tensor", lambda E, tt=tt, fc=fc, bi=bi:
                             E.matmul(
                    out=p_v[bi][:],
                    lhsT=t_xnfm[:, fc * EXT + tt * 128:
                                fc * EXT + (tt + 1) * 128],
                    rhs=t_wvbf[:, fc * 512:(fc + 1) * 512],
                    start=(fc == 0), stop=(fc == NFC - 1)),
                    deps=(([xn_fm_done, wkv_done] +
                           ([vev[bi]] if vev[bi] else []))
                          if fc == 0 else []))
            vev[bi] = P.emit("vector", lambda E, tt=tt, bi=bi: E.tensor_copy(
                out=t_vtm[:, tt * 512:(tt + 1) * 512], in_=p_v[bi][:]),
                deps=[mmq] + (bd("vtm") if tt < 2 else []))
        v_done_v = P.bar("vector")
        free("p_v1", "p_v0")
        if stop == "KV1":
            _stop_here([v_done_v])

        p_r = [psum("p_r0", [128, 512], F32), psum("p_r1", [128, 512], F32)]
        p_k = [psum("p_k0", [128, 512], F32), psum("p_k1", [128, 512], F32)]

        t_kraw2 = t_krot
        t_u1b = stg[2]
        t_u2b = stg[3]
        t_krA = t_qstg[0]
        t_krB = t_qstg[1]
        qstg_guard = qstg_tok[0] + qstg_tok[1]
        rope_st = {"rn": 0, "u1": [None, None], "u2": [None, None],
                   "cbk": {}, "crr": {}}

        def rope_bank_tok(pp):
            t = rope_st["cbk"].get(id(pp))
            return [t] if t else []

        def rope_block(dst_ap, ppk, ppr, cos_c0, kdep, bdeps=(),
                       rot_deps=()):
            rc = rope_st["rn"]
            rope_st["rn"] = rc + 1
            rbi = rc % 2
            kb = t_kraw if rbi == 0 else t_kraw2
            rb = t_krA if rbi == 0 else t_krB
            ub1 = t_u1 if rbi == 0 else t_u1b
            ub2 = t_u2 if rbi == 0 else t_u2b
            pu1 = rope_st["u1"][rbi]
            pu2 = rope_st["u2"][rbi]
            cb = P.emit("scalar", lambda E, kb=kb: E.copy(
                out=kb[:], in_=ppk[:]),
                deps=[kdep] + ([pu1] if pu1 else bd("kraw") + bd("krot")))
            rope_st["cbk"][id(ppk)] = cb
            prev_cr = rope_st["crr"].get(id(ppr))
            rot = P.emit("tensor", lambda E, kb=kb: E.matmul(
                out=ppr[:], lhsT=t_p[:], rhs=kb[:], start=True,
                stop=True), deps=[cb, dep_p] +
                ([prev_cr] if prev_cr else []) + list(rot_deps))
            cr = P.emit("scalar", lambda E, rb=rb: E.copy(
                out=rb[:], in_=ppr[:]),
                deps=[rot] + ([pu2] if pu2 else qstg_guard))
            rope_st["crr"][id(ppr)] = cr
            u1 = P.emit("vector", lambda E, kb=kb, ub=ub1:
                        E.tensor_tensor(
                out=ub[:], in0=kb[:],
                in1=t_cos[:, cos_c0:cos_c0 + 512], op=OP.mult),
                deps=[cb, dep_cos])
            rope_st["u1"][rbi] = u1
            u2 = P.emit("vector", lambda E, rb=rb, ub=ub2:
                        E.tensor_tensor(
                out=ub[:], in0=rb[:],
                in1=t_sin[:, cos_c0:cos_c0 + 512], op=OP.mult),
                deps=[cr, dep_sin])
            rope_st["u2"][rbi] = u2
            return P.emit("vector", lambda E, ub1=ub1, ub2=ub2:
                          E.tensor_tensor(
                out=dst_ap, in0=ub1[:], in1=ub2[:], op=OP.add),
                deps=[u1, u2] + list(bdeps))

        rope_done = None
        for kv in range(NKV):
            for tc in range(3):
                bi = tc % 2
                mmq = None
                for fc in range(NFC):
                    mmq = P.emit("tensor",
                                 lambda E, kv=kv, tc=tc, fc=fc, bi=bi:
                                 E.matmul(
                        out=p_k[bi][:],
                        lhsT=t_wkbf[:, fc * 512 + kv * 128:
                                    fc * 512 + (kv + 1) * 128],
                        rhs=t_xnfm[:, fc * EXT + tc * 512:
                                   fc * EXT + (tc + 1) * 512],
                        start=(fc == 0), stop=(fc == NFC - 1)),
                        deps=(([xn_fm_done, wkv_done] +
                               rope_bank_tok(p_k[bi]))
                              if fc == 0 else []))
                if stop == "KV3" and kv == 0 and tc == 0:
                    _stop_here([mmq])
                rope_done = rope_block(
                    t_kfm[:, kv * EXT + tc * 512:kv * EXT + (tc + 1) * 512],
                    p_k[bi], p_r[bi], tc * 512, mmq,
                    bd("kfm") if kv == 0 and tc == 0 else [],
                    rot_deps=[v_done_v])
                if stop == "KV2" and kv == 0 and tc == 0:
                    _stop_here([rope_done])
        k_done = rope_done
        if stop == "KV":
            _stop_here([k_done])
        free("p_k1", "p_k0")
        free("wvbf", "wkbf", token=k_done)

        t_wqg = [sb("wqgA", [128, 16 * 512], BF16),
                 sb("wqgB", [128, 16 * 512], BF16)]
        p_q = [psum("p_q0", [128, 512], F32), psum("p_q1", [128, 512], F32)]
        rope_st["cbk"][id(p_q[0])] = k_done
        rope_st["cbk"][id(p_q[1])] = k_done
        qg_free = [None, None]
        qrope_done = k_done
        q_head = {}
        for hg in range(4):
            gb = hg % 2
            gdep = ([qg_free[gb]] if qg_free[gb] else
                    bd("wqgA") + bd("wqgB")) + wq_sts_bar
            gld = P.dma("sync" if hg % 2 == 0 else "scalar",
                        bass.AP(t_wqg[gb], 0,
                                [[16 * 512, 128], [512, 16], [1, 512]]),
                        bass.AP(d_wqbf.ap().tensor, hg * 512,
                                [[D, 128], [128 * D, 16], [1, 512]]),
                        deps=gdep)
            last_mm = None
            for hh in range(hg * 4, hg * 4 + 4):
                for th in range(2):
                    pbi = th % 2
                    mmq = None
                    for fc in range(NFC):
                        mmq = P.emit(
                            "tensor",
                            lambda E, fc=fc, th=th, hh=hh, gb=gb, pbi=pbi:
                            E.matmul(
                                out=p_q[pbi][:],
                                lhsT=t_wqg[gb][:, fc * 512 +
                                               (hh % 4) * 128:
                                               fc * 512 +
                                               (hh % 4) * 128 + 128],
                                rhs=t_xnfm[:, fc * EXT + 512 + th * 512:
                                           fc * EXT + 512 + (th + 1) * 512],
                                start=(fc == 0), stop=(fc == NFC - 1)),
                            deps=(([gld, xn_fm_done] +
                                   rope_bank_tok(p_q[pbi]))
                                  if fc == 0 else []))
                    last_mm = mmq
                    qrope_done = rope_block(
                        t_qrp[:, hh * TOK + th * 512:
                              hh * TOK + (th + 1) * 512],
                        p_q[pbi], p_r[pbi], 512 + th * 512, mmq,
                        bd("qrp") if hh == 0 and th == 0 else [])
                q_head[hh] = qrope_done
            qg_free[gb] = last_mm
        q_done = qrope_done
        ag_qw1 = P.collective("AllGather", OP.bypass, [d_qw1sh.ap().opt()],
                              [d_qw1.ap().opt()], deps=sh1 + [q_done])
        ag_qw2 = P.collective("AllGather", OP.bypass, [d_qw2sh.ap().opt()],
                              [d_qw2.ap().opt()], deps=sh2_bar + [q_done])
        if stop == "Q":
            _stop_here([q_done])
        free("p_q1", "p_q0", "p_r1", "p_r0")
        free("wqgB", "wqgA")
        free("cosb", "sinb")
        free("xnfm", token=q_done)

        _dump_stack("pre-A")
        t_worA = sb("worA", [128, NH * TOK], BF16)
        t_e = [sb("e0", [128, 1280], F32), sb("e1", [128, 1280], F32)]
        t_t = [sb(f"t_sb{k}", [128, 640], F32) for k in range(4)]
        t_dn = [sb("dn0", [128, 640], BF16), sb("dn1", [128, 640], BF16)]
        t_dT = [sb("dT0", [128, 640], BF16), sb("dT1", [128, 640], BF16)]
        p_sc = [psum("p_sc0", [128, 1536], F32),
                psum("p_sc1", [128, 1536], F32)]
        p_dt = psum("p_dt", [128, 640], BF16)
        p_av = psum("p_av", [128, 128], F32)

        def region_plan(qt):
            plan = []
            for half in range(2):
                base = 640 * half
                segs = [(0, 512), (512, 128)] if half == 0 else \
                       [(0, 384), (384, 256)]
                for (c0, w) in segs:
                    plan.append((base + c0, w, "k", c0, half))
                for s in mask_slots(qt):
                    plan.append((base + s * 128, 128, "m",
                                 MASK_INDEX[(qt, s)], half))
            return plan

        exp_done = {}
        tok_done = {}
        evt_done = {}
        aev_done = {}
        mx_done = {}
        dr_done = {}
        ww_done = {}
        av_done = {}
        scores_mm = {}

        def em_scores(idx):
            qt, hh = idx // NH, idx % NH
            sbi = idx % 2
            psc = p_sc[sbi]
            banks = {}
            for item in region_plan(qt):
                banks.setdefault((item[0] * 4) // 2048, []).append(item)
            mmq = None
            emitted = 0
            for b0 in sorted(banks):
                items = banks[b0]
                for i2, (pc0, w, kind, arg, half) in enumerate(items):
                    st_ = (i2 == 0)
                    sp_ = (i2 == len(items) - 1)
                    deps = []
                    if emitted == 0:
                        deps = [q_head[hh] if idx >= 2 else q_done,
                                k_done] + consts_bar
                        if idx - 2 in exp_done:
                            deps.append(exp_done[idx - 2])
                    if kind == "k":
                        def mk_s(E, qt=qt, hh=hh, pc0=pc0, w=w,
                                 arg=arg, half=half, st_=st_, sp_=sp_,
                                 psc=psc, kv=hh // 4):
                            pp = 64 * half
                            return E.matmul(
                                out=psc[:, pc0:pc0 + w],
                                lhsT=t_qrp[pp:pp + 64,
                                           hh * TOK + qt * 128:
                                           hh * TOK + (qt + 1) * 128],
                                rhs=t_kfm[pp:pp + 64,
                                          kv * EXT + qt * 128 + arg:
                                          kv * EXT + qt * 128 + arg + w],
                                start=st_, stop=sp_)
                        mmq = P.emit("tensor", mk_s, deps=deps)
                    else:
                        def mk_m(E, pc0=pc0, arg=arg, st_=st_, sp_=sp_,
                                 psc=psc):
                            return E.matmul(
                                out=psc[:, pc0:pc0 + 128],
                                lhsT=t_ibf[:],
                                rhs=t_msk[:, arg * 128:(arg + 1) * 128],
                                start=st_, stop=sp_)
                        mmq = P.emit("tensor", mk_m, deps=deps)
                    emitted += 1
            scores_mm[idx] = mmq

        def em_mx(idx):
            hh = idx % NH
            psc = p_sc[idx % 2]
            mx_done[idx] = P.emit("vector", lambda E, psc=psc, hh=hh:
                                  E.tensor_reduce(
                out=t_mn[:, hh:hh + 1], in_=psc[:, 0:1280], axis=AX,
                op=OP.max, negate=True), deps=[scores_mm[idx]])

        def em_exp(idx):
            hh = idx % NH
            psc = p_sc[idx % 2]
            ebuf = t_e[idx % 2]
            ed = [tok_done[idx - 2]] if idx - 2 in tok_done else \
                bd("e0") + bd("e1")
            e1 = P.emit("scalar", lambda E, psc=psc, hh=hh, eb=ebuf:
                        E.activation(
                out=eb[:, 0:640], in_=psc[:, 0:640], func=AF.Exp,
                bias=t_mn[:, hh:hh + 1], scale=1.0,
                accum_out=t_st[:, 2 * hh:2 * hh + 1]),
                deps=[mx_done[idx]] + ed)
            exp_done[idx] = P.emit("scalar", lambda E, psc=psc, hh=hh,
                                   eb=ebuf: E.activation(
                out=eb[:, 640:1280], in_=psc[:, 640:1280], func=AF.Exp,
                bias=t_mn[:, hh:hh + 1], scale=1.0,
                accum_out=t_st[:, 2 * hh + 1:2 * hh + 2]), deps=[e1])

        def em_smalls(idx):
            hh = idx % NH
            ebuf = t_e[idx % 2]
            e2 = exp_done[idx]
            rr = P.emit("vector", lambda E, hh=hh: E.reciprocal(
                out=t_rr[:, 2 * hh:2 * hh + 2],
                in_=t_st[:, 2 * hh:2 * hh + 2]), deps=[e2])
            bn = P.emit("vector", lambda E, hh=hh: E.scalar_tensor_tensor(
                out=t_bneg[:, hh:hh + 1], in0=t_st[:, 2 * hh:2 * hh + 1],
                scalar=t_lamn[:, hh:hh + 1],
                in1=t_rr[:, 2 * hh + 1:2 * hh + 2],
                op0=OP.mult, op1=OP.mult), deps=[rr, dep_lamn])
            td = [dr_done[idx - 4]] if idx - 4 in dr_done else \
                (bd("t_sb0") + bd("t_sb1") + bd("t_sb2") + bd("t_sb3"))
            tok_done[idx] = P.emit("vector", lambda E, hh=hh, eb=ebuf,
                                   ti=idx % 4: E.scalar_tensor_tensor(
                out=t_t[ti][:], in0=eb[:, 640:1280],
                scalar=t_bneg[:, hh:hh + 1], in1=eb[:, 0:640],
                op0=OP.mult, op1=OP.add), deps=[bn] + td)

        def em_tail1a(idx):
            hh = idx % NH
            sbi = idx % 2
            dbuf = t_dn[sbi]
            dd = [aev_done[idx - 2]] if idx - 2 in aev_done else \
                bd("dn0") + bd("dn1")
            dr = P.emit("vector", lambda E, hh=hh, db=dbuf, ti=idx % 4:
                        E.tensor_scalar(
                out=db[:], in0=t_t[ti][:], scalar1=0.0, scalar2=0.0,
                op0=OP.max, op1=OP.add,
                accum_out=t_ds[:, hh:hh + 1]),
                deps=[tok_done[idx]] + dd)
            dr_done[idx] = dr
            uu = P.emit("vector", lambda E, hh=hh: E.scalar_tensor_tensor(
                out=t_uu[:, hh:hh + 1], in0=t_st[:, 2 * hh:2 * hh + 1],
                scalar=1e-6, in1=t_ds[:, hh:hh + 1],
                op0=OP.mult, op1=OP.add), deps=[dr])
            ww_done[idx] = P.emit("vector", lambda E, hh=hh: E.reciprocal(
                out=t_ww[:, hh:hh + 1], in_=t_uu[:, hh:hh + 1]),
                deps=[uu])

        def em_tail1b(idx):
            hh = idx % NH
            sbi = idx % 2
            dbuf = t_dn[sbi]
            dn = P.emit("scalar", lambda E, hh=hh, db=dbuf:
                        E.activation(
                out=db[:], in_=db[:], func=AF.Copy,
                scale=t_ww[:, hh:hh + 1]), deps=[ww_done[idx]])
            prev_evt = evt_done.get(idx - 1)
            tts = None
            for s5 in range(5):
                tts = P.emit("tensor", lambda E, s5=s5, db=dbuf:
                             E.transpose(
                    out=p_dt[:, s5 * 128:(s5 + 1) * 128],
                    in_=db[:, s5 * 128:(s5 + 1) * 128],
                    identity=t_ibf[:]),
                    deps=[dn] + ([prev_evt] if prev_evt and s5 == 0
                                 else []))
            return tts

        def em_tail2(idx, tts):
            qt, hh = idx // NH, idx % NH
            kv = hh // 4
            dti = idx % 2
            evt = P.emit("vector", lambda E, dti=dti: E.tensor_copy(
                out=t_dT[dti][:], in_=p_dt[:]),
                deps=[tts] +
                ([aev_done[idx - 2]] if idx - 2 in aev_done else
                 bd("dT0") + bd("dT1")))
            evt_done[idx] = evt
            prev_aev = aev_done.get(idx - 1)
            mav = None
            for s5 in range(5):
                mav = P.emit("tensor",
                             lambda E, s5=s5, qt=qt, kv=kv, dti=dti:
                             E.matmul(
                    out=p_av[:],
                    lhsT=t_vtm[:, (qt + s5) * 512 + kv * 128:
                               (qt + s5) * 512 + (kv + 1) * 128],
                    rhs=t_dT[dti][:, s5 * 128:(s5 + 1) * 128],
                    start=(s5 == 0), stop=(s5 == 4)),
                    deps=(([evt, v_done_v] +
                           ([prev_aev] if prev_aev else []))
                          if s5 == 0 else []))
            av_done[idx] = mav

        def em_aev(idx):
            qt, hh = idx // NH, idx % NH
            aev_done[idx] = P.emit("vector", lambda E, hh=hh, qt=qt:
                                   E.tensor_copy(
                out=t_worA[:, hh * TOK + qt * 128:
                           hh * TOK + qt * 128 + 128],
                in_=p_av[:]),
                deps=[av_done[idx]] + (bd("worA") if idx == 0 else []))

        NIDX = NTT * NH
        for i in range(-2, NIDX + 6):
            if 0 <= i < NIDX:
                em_exp(i)
            if 0 <= i + 2 < NIDX:
                em_scores(i + 2)
            if 0 <= i - 5 < NIDX:
                em_aev(i - 5)
            if 0 <= i < NIDX:
                em_smalls(i)
            if 0 <= i - 3 < NIDX:
                em_tail1a(i - 3)
            if 0 <= i + 2 < NIDX:
                em_mx(i + 2)
            tts = em_tail1b(i - 4) if 0 <= i - 4 < NIDX else None
            if 0 <= i - 4 < NIDX:
                em_tail2(i - 4, tts)
        afm_done = aev_done[NTT * NH - 1]
        wor_sp = []
        for j in range(NH):
            wor_sp.append(P.dma("gpsimd", d_afm[j, :, :],
                                t_worA[:, j * TOK:(j + 1) * TOK],
                                deps=[aev_done[(NTT - 1) * NH + j]]))
        afm_bar = P.dma_bar("gpsimd")
        if stop == "A":
            _stop_here(afm_bar)
        free("p_av", "p_dt", "p_sc1", "p_sc0")
        free("dT1", "dT0", "dn1", "dn0",
             "t_sb3", "t_sb2", "t_sb1", "t_sb0", "e1", "e0",
             "qrp", "kfm", "vtm", token=afm_done)
        free("msk", token=afm_done)
        free("worA")
        tok_pool.extend(afm_bar)

        bq = P.dma("sync", t_amx[:],
                   bass.AP(d_ar1o.ap().tensor, 0, [[0, 128], [1, 1]]),
                   deps=[ar1])
        r1_ = P.emit("vector", lambda E: E.reciprocal(
            out=t_cx[:], in_=t_amx[:]), deps=[bq])
        cx = P.emit("vector", lambda E: E.tensor_scalar(
            out=t_cx[:], in0=t_cx[:], scalar1=QMAX, scalar2=1e8,
            op0=OP.mult, op1=OP.min), deps=[r1_])
        sx1 = P.emit("vector", lambda E: E.tensor_scalar(
            out=t_sx1[:], in0=t_amx[:], scalar1=QINV, scalar2=QEPS,
            op0=OP.mult, op1=OP.max), deps=[bq])

        t_out = sb("outfm", [128, NFC * TOK], F32)
        t_qxr = sb("qxr", [128, NFC * TOK], BF16)
        t_qv = [sb("q_vA", [128, D], F32), sb("q_vB", [128, D], F32)]
        t_qr0s = sb("q_r0A", [128, D], I32)
        t_qr0 = [t_qr0s, t_qr0s]
        t_qm = sb("q_m", [128, D], BF16)
        t_qb = sb("q_qb", [128, D], BF16)
        p_tq = [psum("p_tq", [128, 512], BF16),
                psum("p_tq2", [128, 512], BF16)]
        vv_tok = [None, None]
        r0_tok = [None, None]
        qq_tok = [None, None]
        ev_tok = [None, None]
        last_evq = None

        def qx_stage2(tt):
            qb2 = tt % 2
            r0 = P.emit("scalar", lambda E, qb2=qb2: E.activation(
                out=t_qr0[qb2][:], in_=t_qv[qb2][:], func=AF.Copy),
                deps=[vv_tok[qb2]] + bd("q_r0A") +
                ([qq_tok[1 - qb2]] if qq_tok[1 - qb2] else []))
            r0_tok[qb2] = r0
            mm_ = P.emit("vector", lambda E, qb2=qb2: E.tensor_tensor(
                out=t_qm[:], in0=t_qv[qb2][:], in1=t_qr0[qb2][:],
                op=OP.is_lt), deps=[r0] + bd("q_m"))
            qq = P.emit("vector", lambda E, qb2=qb2:
                        E.scalar_tensor_tensor(
                out=t_qb[:], in0=t_qr0[qb2][:], scalar=0.0, in1=t_qm[:],
                op0=OP.add, op1=OP.subtract),
                deps=[mm_] + bd("q_qb") +
                ([ev_tok[1]] if ev_tok[1] else []))
            qq_tok[qb2] = qq
            for g in range(4):
                pb = g % 2
                tts = None
                for k in range(4):
                    fc = g * 4 + k
                    tts = P.emit("tensor", lambda E, fc=fc, k=k, pb=pb:
                                 E.transpose(
                        out=p_tq[pb][:, k * 128:(k + 1) * 128],
                        in_=t_qb[:, fc * 128:(fc + 1) * 128],
                        identity=t_ibf[:]),
                        deps=[qq] +
                        ([ev_tok[pb]] if ev_tok[pb] and k == 0 else []) +
                        ([afm_done] if tt == 0 and g == 0 and k == 0
                         else []))
                ev_tok[pb] = P.emit("scalar", lambda E, g=g, tt=tt, pb=pb:
                                    E.copy(
                    out=bass.AP(t_qxr, g * 4 * TOK + tt * 128,
                                [[NFC * TOK, 128], [TOK, 4], [1, 128]]),
                    in_=p_tq[pb][:, 0:512]),
                    deps=[tts] + (bd("qxr") if tt == 0 and g == 0 else []))
            return ev_tok[1]

        for tt in range(NTT):
            bi = tt % 2
            qb2 = tt % 2
            buf = bstg[bi]
            dep = list(bstg_tok[bi])
            ld = P.dma("sync", buf[:], d_xn[tt, :, :],
                       deps=dep + [xnsp[tt]])
            vv = P.emit("vector", lambda E, b=buf, qb2=qb2:
                        E.tensor_scalar(
                out=t_qv[qb2][:], in0=b[:], scalar1=t_cx[:], scalar2=0.5,
                op0=OP.mult, op1=OP.add),
                deps=[cx, ld] + bd("q_vA") + bd("q_vB") +
                ([qq_tok[qb2]] if qq_tok[qb2] else []))
            vv_tok[qb2] = vv
            bstg_tok[bi] = [vv]
            if tt >= 1:
                last_evq = qx_stage2(tt - 1)
        last_evq = qx_stage2(NTT - 1)
        qx_done = last_evq
        if stop == "QX":
            _stop_here([qx_done])
        free("p_tq2", "p_tq")
        free("q_qb", "q_m", "q_r0A", "q_vB", "q_vA",
             token=qx_done)
        qx_done_all = [qx_done]


        def load_colT(dst, src_ap, dep_extra, n, p_dst):
            ld = P.dma("sync", t_swld[0:n, :], src_ap, deps=dep_extra)
            tp = P.emit("tensor", lambda E, n=n, p_dst=p_dst: E.transpose(
                out=p_dst[:, 0:n], in_=t_swld[0:n, :],
                identity=t_if32[0:n, 0:n]), deps=[ld, dep_if32])
            return P.emit("vector", lambda E, d=dst, n=n, p_dst=p_dst:
                          E.tensor_copy(
                out=d[:, 0:n], in_=p_dst[0:128, 0:n]), deps=[tp])

        qxr_bar = qx_done_all

        _dump_stack("pre-qw1")
        t_qw1 = [sb("qw1A", [128, NFC * 512], BF16),
                 sb("qw1B", [128, NFC * 512], BF16)]
        p_sw = psum("p_sw", [128, 128], F32)
        p_h = [psum("p_h0", [128, 512], F32), psum("p_h1", [128, 512], F32)]

        cp1 = load_colT(t_s1c,
                        bass.AP(d_sw1.ap().tensor, 0, [[128, 64], [1, 128]]),
                        [ag_sw1, afm_done] + qx_done_all, 64, p_sw)
        s1fin = P.emit("vector", lambda E: E.tensor_scalar(
            out=t_s1c[:], in0=t_s1c[:], scalar1=t_sx1[:], scalar2=0.0,
            op0=OP.mult, op1=OP.add), deps=[cp1, sx1])
        cpb1 = load_colT(t_b1c,
                         bass.AP(h_b1.tensor, 0, [[128, 64], [1, 128]]),
                         [cp1], 64, p_sw)

        qw1_free = [None, None]
        hst_free = [None] * 4
        ph_free = [None, None]
        hsp_last = None
        for jb2 in range(16):
            jb, hb = jb2 // 2, jb2 % 2
            bi = jb2 % 2
            dep = ([qw1_free[bi]] if qw1_free[bi] else
                   (bd("qw1A") + bd("qw1B"))) + [ag_qw1] + qx_done_all
            for fc in range(NFC):
                P.dma("sync",
                      t_qw1[bi][:, fc * 512:(fc + 1) * 512],
                      d_qw1[jb, fc, :, hb * 512:(hb + 1) * 512],
                      deps=dep if fc == 0 else [])
            qw1_bar = P.dma_bar("sync")
            last_mm = None
            for jc in range(4):
                jg = jb2 * 4 + jc
                for th in range(2):
                    pbi = (jc * 2 + th) % 2
                    mmq = None
                    for fc in range(NFC):
                        mmq = P.emit(
                            "tensor",
                            lambda E, fc=fc, jc=jc, th=th, bi=bi, pbi=pbi:
                            E.matmul(
                                out=p_h[pbi][:],
                                lhsT=t_qw1[bi][:, fc * 512 + jc * 128:
                                               fc * 512 + (jc + 1) * 128],
                                rhs=t_qxr[:, fc * TOK + th * 512:
                                          fc * TOK + (th + 1) * 512],
                                start=(fc == 0), stop=(fc == NFC - 1)),
                            deps=((qw1_bar + qxr_bar + [afm_done] +
                                   ([ph_free[pbi]] if ph_free[pbi] else []))
                                  if fc == 0 else []))
                    si = (jg * 2 + th) % 4
                    ge = P.emit("scalar", lambda E, jg=jg, si=si, pbi=pbi:
                                E.activation(
                        out=stg[si][:], in_=p_h[pbi][:], func=AF.Gelu,
                        scale=t_s1c[:, jg:jg + 1], bias=t_b1c[:, jg:jg + 1]),
                        deps=[mmq, s1fin, cpb1] +
                             ([hst_free[si]] if hst_free[si] else
                              ([stg_tok[si]] if stg_tok[si] else [])))
                    ph_free[pbi] = ge
                    ax = P.emit("vector", lambda E, jg=jg, th=th, si=si:
                                E.tensor_reduce(
                        out=t_hax[:, jg * 2 + th:jg * 2 + th + 1],
                        in_=stg[si][:], axis=AX, op=OP.max,
                        apply_absolute_value=True), deps=[ge])
                    stt = P.dma("scalar", d_h[jg, :, th * 512:(th + 1) * 512],
                                stg[si][:], deps=[ax])
                    hst_free[si] = stt
                    hsp_last = stt
                    last_mm = mmq
            qw1_free[bi] = last_mm
        stg_tok = [hst_free[0], hst_free[1], hst_free[2], hst_free[3],
                   None, None]
        hsp_bar = P.dma_bar("scalar")
        p_hxt = psum("p_hxt", [1, 128], F32)
        if stop == "M1":
            _stop_here(hsp_bar)
        hx1 = P.emit("vector", lambda E: E.tensor_reduce(
            out=t_hx1[:], in_=t_hax[:], axis=AX, op=OP.max),
            deps=[P.bar("vector")])
        hx2 = P.emit("tensor", lambda E: E.transpose(
            out=p_hxt[:], in_=t_hx1[:], identity=t_if32[:]), deps=[hx1])
        hx3 = P.emit("vector", lambda E: E.tensor_reduce(
            out=t_hxr[:], in_=p_hxt[0:1, :], axis=AX, op=OP.max), deps=[hx2])
        hx4 = P.dma("sync", d_ar2i[:, :], t_hxr[0:1, 0:1], deps=[hx3])
        ar2 = P.collective("AllReduce", OP.max, [d_ar2i.ap().opt()],
                           [d_ar2o.ap().opt()], deps=[hx4])
        free("p_hxt", "p_h1", "p_h0", "p_sw")
        free("qw1B", "qw1A", "qxr", token=ar2)

        t_wor = sb("wor", [128, NFC * 512], BF16)
        t_wos = sb("wos", [128, NFC * 1024], BF16)
        p_o = psum("p_o", [128, 8 * 512], F32)
        po_ev = {}
        wo_og_mm = {}
        for og in range(2):
            wcasts = []
            for dc in range(NFC):
                bi = dc % 2
                dep = list(bstg_tok[bi])
                if og == 1:
                    dep = dep + [wo_og_mm[(0, dc)]]
                ldw = P.dma("sync", bstg[bi][:, 0:1024],
                            h_wo[dc * 128:(dc + 1) * 128,
                                 og * 1024:(og + 1) * 1024],
                            deps=dep + (bd("wos") if og == 0 and dc < 2
                                        else []))
                wcasts.append(P.emit("scalar", lambda E, bi=bi, dc=dc:
                                     E.copy(
                    out=t_wos[:, dc * 1024:(dc + 1) * 1024],
                    in_=bstg[bi][:, 0:1024]), deps=[ldw]))
                bstg_tok[bi] = [wcasts[-1]]
            for tq in range(2):
                wor_ld = None
                wdep = afm_bar + bd("wor") + \
                    ([po_ev[(og, tq - 1, 7)]] if (og, tq - 1, 7) in po_ev
                     else []) + \
                    ([po_ev[(og - 1, 1, 7)]] if (og - 1, 1, 7) in po_ev and
                     tq == 0 else [])
                for dc in range(NFC):
                    wor_ld = P.dma("gpsimd",
                                   t_wor[:, dc * 512:(dc + 1) * 512],
                                   d_afm[dc, :, tq * 512:(tq + 1) * 512],
                                   deps=wdep if dc == 0 else [])
                wor_bar = P.dma_bar("gpsimd")
                mm_last = None
                for dc in range(NFC):
                    for oc8 in range(8):
                        extra = []
                        if oc8 == 0 and tq == 0:
                            extra = [wcasts[dc]]
                        if dc == 0 and oc8 == 0:
                            extra = extra + list(wor_bar)
                            pk = (og, tq - 1, 7) if tq == 1 else \
                                (og - 1, 1, 7)
                            if pk in po_ev:
                                extra.append(po_ev[pk])
                        mm_last = P.emit(
                            "tensor",
                            lambda E, dc=dc, oc8=oc8: E.matmul(
                                out=p_o[:, oc8 * 512:(oc8 + 1) * 512],
                                lhsT=t_wos[:, dc * 1024 + oc8 * 128:
                                           dc * 1024 + (oc8 + 1) * 128],
                                rhs=t_wor[:, dc * 512:(dc + 1) * 512],
                                start=(dc == 0), stop=(dc == NFC - 1)),
                            deps=extra)
                    wo_og_mm[(og, dc)] = mm_last
                for oc8 in range(8):
                    oc = og * 8 + oc8
                    po_ev[(og, tq, oc8)] = P.emit(
                        "scalar", lambda E, oc=oc, tq=tq, oc8=oc8:
                        E.copy(
                            out=t_out[:, oc * TOK + tq * 512:
                                      oc * TOK + (tq + 1) * 512],
                            in_=p_o[:, oc8 * 512:(oc8 + 1) * 512]),
                        deps=[mm_last] + bd("outfm"))
        wo_done_v = P.bar("scalar")
        if stop == "WO":
            _stop_here([wo_done_v])
        free("p_o")
        free("wos", "wor", token=wo_done_v)

        bq2 = P.dma("sync", t_amh[:],
                    bass.AP(d_ar2o.ap().tensor, 0, [[0, 128], [1, 1]]),
                    deps=[ar2])
        r2_ = P.emit("vector", lambda E: E.reciprocal(
            out=t_ch[:], in_=t_amh[:]), deps=[bq2])
        ch = P.emit("vector", lambda E: E.tensor_scalar(
            out=t_ch[:], in0=t_ch[:], scalar1=QMAX, scalar2=1e8,
            op0=OP.mult, op1=OP.min), deps=[r2_])
        sx2 = P.emit("vector", lambda E: E.tensor_scalar(
            out=t_sx2[:], in0=t_amh[:], scalar1=QINV, scalar2=QEPS,
            op0=OP.mult, op1=OP.max), deps=[bq2])

        p_sw2 = psum("p_sw2", [128, 128], F32)
        cp2 = load_colT(t_s2c,
                        bass.AP(d_sw2.ap().tensor, 0, [[128, 16], [1, 128]]),
                        [ag_sw2, cpb1, wo_done_v], 16, p_sw2)
        s2fin = P.emit("vector", lambda E: E.tensor_scalar(
            out=t_s2c[:, 0:16], in0=t_s2c[:, 0:16], scalar1=t_sx2[:],
            scalar2=0.0, op0=OP.mult, op1=OP.add), deps=[cp2, sx2])
        cpb2 = load_colT(t_b2c,
                         bass.AP(h_b2.tensor, 0, [[128, 16], [1, 128]]),
                         [cp2], 16, p_sw2)
        free("p_sw2")

        t_qh = sb("qh", [128, NIC * 512], BF16)
        t_qw2 = [sb("qw2A", [128, 4096], BF16), sb("qw2B", [128, 4096], BF16)]
        p_m2 = psum("p_m2", [128, 4096], F32)

        hre_free = [None] * 4
        qw2_free = [None, None]
        fst_free = [None, None]
        prev_hq = None
        qh_tok = {}
        ev_m2 = {}
        mm_last_og = {}
        mm_pair = {}
        sc_m2 = {}
        pend_epi = [None]

        def emit_epi(tb2, og, mm_last):
            for oc8 in range(8):
                oc = og * 8 + oc8
                fbi = oc8 % 2
                sc_ = P.emit("scalar", lambda E, oc8=oc8, oc=oc, fbi=fbi:
                             E.activation(
                    out=t_fstg[fbi][:],
                    in_=p_m2[:, oc8 * 512:(oc8 + 1) * 512],
                    func=AF.Copy, scale=t_s2c[:, oc:oc + 1]),
                    deps=[mm_last, s2fin, cpb2] +
                         ([fst_free[fbi]] if fst_free[fbi] else []))
                ev_m2[(tb2, og, oc8)] = P.emit(
                    "vector", lambda E, oc=oc, tb2=tb2, fbi=fbi:
                    E.scalar_tensor_tensor(
                        out=t_out[:, oc * TOK + tb2 * 512:
                                  oc * TOK + (tb2 + 1) * 512],
                        in0=t_fstg[fbi][:],
                        scalar=t_b2c[:, oc:oc + 1],
                        in1=t_out[:, oc * TOK + tb2 * 512:
                                  oc * TOK + (tb2 + 1) * 512],
                        op0=OP.add, op1=OP.add),
                    deps=[sc_, wo_done_v])
                fst_free[fbi] = ev_m2[(tb2, og, oc8)]
                sc_m2[(tb2, og, oc8)] = sc_

        for tb2 in range(2):
            for ic in range(NIC):
                si = ic % 4
                dep = [hre_free[si]] if hre_free[si] else \
                    ([stg_tok[si]] if stg_tok[si] else [])
                ld = P.dma("gpsimd", stg[si][:],
                           d_h[ic, :, tb2 * 512:(tb2 + 1) * 512],
                           deps=dep + (hsp_bar if ic == 0 and tb2 == 0
                                       else []))
                vv = P.emit("vector", lambda E, si=si: E.tensor_scalar(
                    out=stg[4][:], in0=stg[si][:], scalar1=t_ch[:],
                    scalar2=0.5, op0=OP.mult, op1=OP.add),
                    deps=[ld, ch] + ([prev_hq] if prev_hq else []) +
                    ([stg_tok[4]] if stg_tok[4] and ic == 0 and tb2 == 0
                     else []))
                hre_free[si] = vv
                r0 = P.emit("scalar", lambda E: E.activation(
                    out=t_i32s[:], in_=stg[4][:], func=AF.Copy), deps=[vv])
                mm_ = P.emit("vector", lambda E: E.tensor_tensor(
                    out=stg[5][:], in0=stg[4][:], in1=t_i32s[:],
                    op=OP.is_lt),
                    deps=[r0] + ([stg_tok[5]] if stg_tok[5] and ic == 0 and
                                 tb2 == 0 else []))
                qh_tok[(tb2, ic)] = P.emit("vector", lambda E, ic=ic:
                                           E.scalar_tensor_tensor(
                    out=t_qh[:, ic * 512:(ic + 1) * 512], in0=t_i32s[:],
                    scalar=0.0, in1=stg[5][:], op0=OP.add, op1=OP.subtract),
                    deps=[mm_] + (bd("qh") if ic == 0 and tb2 == 0 else [])
                    + ([mm_pair[(0, 1, ic // 2)]] if tb2 == 1 else []))
                prev_hq = qh_tok[(tb2, ic)]
            if pend_epi[0]:
                emit_epi(*pend_epi[0])
                pend_epi[0] = None
            for og in range(2):
                mm_last = None
                for ic in range(0, NIC, 4):
                    bi = (ic // 4) % 2
                    dep = ([qw2_free[bi]] if qw2_free[bi] else
                           (bd("qw2A") + bd("qw2B"))) + [ag_qw2]
                    lds = []
                    qq_ = "sync" if og == 0 else "scalar"
                    for cpart in range(4):
                        c = og * 4 + cpart
                        lds.append(P.dma(qq_,
                                         bass.AP(t_qw2[bi],
                                                 cpart * 1024,
                                                 [[4096, 128], [512, 2],
                                                  [1, 512]]),
                                         bass.AP(d_qw2.ap().tensor,
                                                 (c * (NIC // 2) +
                                                  ic // 2) * 65536,
                                                 [[512, 128], [65536, 2],
                                                  [1, 512]]),
                                         deps=dep if cpart == 0 else []))
                    for i4 in range(4):
                        ici = ic + i4
                        for oc8 in range(8):
                            oc = og * 8 + oc8
                            extra = []
                            if ici == 0 and i4 == 0 and oc8 == 0:
                                if og == 1:
                                    extra = [sc_m2[(tb2, 0, 7)]]
                                elif tb2 == 1:
                                    extra = [sc_m2[(0, 1, 7)]]
                            mm_last = P.emit(
                                "tensor",
                                lambda E, i4=i4, ici=ici, oc8=oc8, bi=bi:
                                E.matmul(
                                    out=p_m2[:, oc8 * 512:(oc8 + 1) * 512],
                                    lhsT=t_qw2[bi][
                                        :, (oc8 // 2) * 1024 +
                                        (i4 // 2) * 512 + (i4 % 2) * 256 +
                                        (oc8 % 2) * 128:
                                        (oc8 // 2) * 1024 +
                                        (i4 // 2) * 512 + (i4 % 2) * 256 +
                                        (oc8 % 2) * 128 + 128],
                                    rhs=t_qh[:, ici * 512:(ici + 1) * 512],
                                    start=(ici == 0), stop=(ici == NIC - 1)),
                                deps=(lds + [qh_tok[(tb2, ici)]] + extra)
                                if oc8 == 0 else [])
                    qw2_free[bi] = mm_last
                    mm_pair[(tb2, og, ic // 2)] = mm_last
                    mm_pair[(tb2, og, ic // 2 + 1)] = mm_last
                mm_last_og[(tb2, og)] = mm_last
                if og == 0:
                    emit_epi(tb2, 0, mm_last)
                else:
                    pend_epi[0] = (tb2, 1, mm_last)
        if pend_epi[0]:
            emit_epi(*pend_epi[0])
            pend_epi[0] = None
        m2_done = ev_m2[(1, 1, 7)]
        if stop == "M2":
            _stop_here([m2_done])
        stg_tok = [hre_free[0], hre_free[1], hre_free[2], hre_free[3],
                   prev_hq, prev_hq]
        free("p_m2")
        free("qw2B", "qw2A", "qh", token=m2_done)

        p_f = [psum("p_f0", [128, 512], F32), psum("p_f1", [128, 512], F32)]
        xre_free = [None, None]
        ost_free = [None, None]
        pf_free = [None, None]
        st_last = None
        for tt in range(NTT):
            for fg in range(4):
                bi = (tt * 4 + fg) % 2
                dep = [xre_free[bi]] if xre_free[bi] else \
                    ([stg_tok[bi]] if stg_tok[bi] else [])
                ldx = P.dma("gpsimd", stg[bi][:],
                            h_x[(4 + tt) * 128:(5 + tt) * 128,
                                fg * 512:(fg + 1) * 512], deps=dep)
                tts = []
                for k in range(4):
                    fc = fg * 4 + k
                    tts.append(P.emit("tensor",
                                      lambda E, fc=fc, k=k, tt=tt, bi=bi:
                                      E.transpose(
                        out=p_f[bi][:, k * 128:(k + 1) * 128],
                        in_=t_out[:, fc * TOK + tt * 128:
                                  fc * TOK + (tt + 1) * 128],
                        identity=t_if32[:]),
                        deps=[m2_done] +
                             ([pf_free[bi]] if pf_free[bi] and k == 0
                              else [])))
                ost = stg[2 + bi]
                odep = [ost_free[bi]] if ost_free[bi] else \
                    ([stg_tok[2 + bi]] if stg_tok[2 + bi] else [])
                ad = P.emit("vector", lambda E, bi=bi, ost=ost:
                            E.tensor_tensor(
                    out=ost[:], in0=p_f[bi][:], in1=stg[bi][:],
                    op=OP.add), deps=[tts[-1], ldx] + odep)
                pf_free[bi] = ad
                xre_free[bi] = ad
                st = P.dma("sync" if bi == 0 else "scalar",
                           h_out[tt * 128:(tt + 1) * 128,
                                 fg * 512:(fg + 1) * 512],
                           ost[:], deps=[ad])
                ost_free[bi] = st
                st_last = st
        P.wait_only("gpsimd", P.dma_bar("gpsimd"))
        P.wait_only("sync", P.dma_bar("sync"))
        P.wait_only("scalar", P.dma_bar("scalar"))
        free("p_f1", "p_f0")

      except _StopBuild:
        pass
      block = ctx.enter_context(nc.Block())
      P.run(block)
    return nc


def _host_consts(core):
    half = core % 2
    qstart = half * TOK
    pos = np.arange(qstart - WIN, qstart + TOK, dtype=np.float64)
    pos = np.clip(pos, 0, None)
    i = np.arange(64, dtype=np.float64)
    inv = 1.0 / (10000.0 ** (2 * i / 128.0))
    ang = pos[None, :] * inv[:, None]
    cos = np.cos(ang)
    sin = np.sin(ang)
    cos_s = np.concatenate([cos, cos], 0) * C8
    sin_s = np.concatenate([-sin, sin], 0) * C8
    masks = np.zeros((N_MASKS, 128, 128), np.float32)
    r = np.arange(128)
    c = np.arange(128)
    for (qt, s), mi in MASK_INDEX.items():
        rg = qstart + qt * 128 + r[:, None]
        cg = qstart - WIN + (qt + s) * 128 + c[None, :]
        allowed = (cg >= 0) & (cg <= rg) & (cg >= rg - (WIN - 1))
        masks[mi] = np.where(allowed, 0.0, -30000.0)
    return (cos_s.astype(np.float32), sin_s.astype(np.float32),
            masks.astype(ml_dtypes.bfloat16))


def kernel(L0_attn_x, L0_attn_Wq, L0_attn_Wk, L0_attn_Wv, L0_attn_Wo,
           L0_attn_lam, L0_ffn_w1, L0_ffn_b1, L0_ffn_w2, L0_ffn_b2):
    x = np.ascontiguousarray(np.asarray(L0_attn_x, dtype=np.float32))
    Wq = np.ascontiguousarray(np.asarray(L0_attn_Wq, dtype=np.float32))
    Wk = np.ascontiguousarray(np.asarray(L0_attn_Wk, dtype=np.float32))
    Wv = np.ascontiguousarray(np.asarray(L0_attn_Wv, dtype=np.float32))
    Wo = np.ascontiguousarray(np.asarray(L0_attn_Wo, dtype=np.float32))
    lam = np.ascontiguousarray(np.asarray(L0_attn_lam, dtype=np.float32)
                               .reshape(1, NH))
    w1 = np.ascontiguousarray(np.asarray(L0_ffn_w1, dtype=np.float32))
    b1 = np.ascontiguousarray(np.asarray(L0_ffn_b1, dtype=np.float32)
                              .reshape(1, I_DIM))
    w2 = np.ascontiguousarray(np.asarray(L0_ffn_w2, dtype=np.float32))
    b2 = np.ascontiguousarray(np.asarray(L0_ffn_b2, dtype=np.float32)
                              .reshape(1, D))
    ident_bf = np.eye(128, dtype=ml_dtypes.bfloat16)
    ident_f32 = np.eye(128, dtype=np.float32)
    rotP = np.zeros((128, 128), np.float32)
    for d in range(128):
        rotP[(d + 64) % 128, d] = 1.0
    rotP = rotP.astype(ml_dtypes.bfloat16)

    nc = build_nc()
    in_maps = []
    for core in range(NCORE):
        b, half = core // 2, core % 2
        qstart = half * TOK
        x_ext = np.zeros((EXT, D), np.float32)
        lo = qstart - WIN
        src_lo = max(lo, 0)
        x_ext[src_lo - lo:, :] = x[b, src_lo:qstart + TOK, :]
        cos_s, sin_s, masks = _host_consts(core)
        in_maps.append({
            "x_ext": x_ext, "Wq": Wq, "Wk": Wk, "Wv": Wv, "Wo": Wo,
            "lam": lam,
            "w1s": w1[core * 1024:(core + 1) * 1024, :],
            "b1": b1,
            "w2s": w2[core * 256:(core + 1) * 256, :],
            "b2": b2,
            "cos_s": cos_s, "sin_s": sin_s, "masks": masks,
            "ident_bf": ident_bf, "ident_f32": ident_f32, "rotP": rotP,
        })
    res = run_bass_kernel_spmd(nc, in_maps, core_ids=list(range(NCORE)))
    out = np.empty((BATCH, SEQ, D), np.float32)
    for core in range(NCORE):
        b, half = core // 2, core % 2
        out[b, half * TOK:(half + 1) * TOK, :] = res.results[core]["out"]
    return out
